# version 02
# baseline (speedup 1.0000x reference)
import numpy as np
import ml_dtypes
import concourse.bass as bass
import concourse.mybir as mybir
from concourse.bass_utils import run_bass_kernel_spmd

F32 = mybir.dt.float32
BF16 = mybir.dt.bfloat16
I32 = mybir.dt.int32
AF = mybir.ActivationFunctionType
ALU = mybir.AluOpType
AX = mybir.AxisListType

NCORES = 8
D = 4096
R = 1024
KT = 32
NGW = 256
NG = D // NGW
NB = 32
EPS_LN = 1e-5
EPS_G = 1e-6
GAMMA = 0.5
SCALING = 16.0 / 32.0

DEBUG_TAPS = False
SKIP_CC = False
NB_EFF = None
PHASE1_ONLY = False
STOP_N = None

ENGS = ("sync", "gpsimd", "scalar", "tensor", "vector")


def build_nc(debug_taps=False):
    nc = bass.Bass("TRN2", target_bir_lowering=False, debug=False,
                   dynamic_dma_scratch_size=16384)

    d_xT = nc.dram_tensor("xT", [D, R], F32, kind="ExternalInput").ap()
    d_wm = nc.dram_tensor("wmh", [128, KT * D], F32, kind="ExternalInput").ap()
    d_W = nc.dram_tensor("Wh", [128, KT * D], F32, kind="ExternalInput").ap()
    d_msk = nc.dram_tensor("mskh", [128, KT * 512], I32, kind="ExternalInput").ap()
    d_A = nc.dram_tensor("Ah", [128, KT * 32], F32, kind="ExternalInput").ap()
    d_W8 = nc.dram_tensor("W8h", [128, KT * 8], F32, kind="ExternalInput").ap()
    d_b32 = nc.dram_tensor("b32", [32, 128], F32, kind="ExternalInput").ap()
    d_b8 = nc.dram_tensor("b8", [1, 8], F32, kind="ExternalInput").ap()
    d_Bm = nc.dram_tensor("Bm", [32, D], F32, kind="ExternalInput").ap()
    d_LiM = nc.dram_tensor("LiM", [8, 32], F32, kind="ExternalInput").ap()
    d_idn = nc.dram_tensor("idn", [128, 128], BF16, kind="ExternalInput").ap()
    d_idnf = nc.dram_tensor("idnf", [128, 128], F32, kind="ExternalInput").ap()
    d_avgp = nc.dram_tensor("avgp", [128, 128], BF16, kind="ExternalInput").ap()
    d_EXP = nc.dram_tensor("EXPc", [16, 512], BF16, kind="ExternalInput").ap()
    d_Q4 = nc.dram_tensor("Q4c", [128, 4], BF16, kind="ExternalInput").ap()
    d_oddm = nc.dram_tensor("oddm", [128, 1], F32, kind="ExternalInput").ap()
    d_onesc = nc.dram_tensor("onesc", [1, 128], F32, kind="ExternalInput").ap()
    d_pairP = nc.dram_tensor("pairP", [128, 128], BF16, kind="ExternalInput").ap()

    d_out = nc.dram_tensor("outT", [D, R], F32, kind="ExternalOutput").ap()

    cc_b_in = nc.dram_tensor("cc_b_in", [16, 128], BF16)
    cc_b_out = nc.dram_tensor("cc_b_out", [128, 128], BF16)
    cc_m_in = nc.dram_tensor("cc_m_in", [1, 2], F32)
    cc_m_out = nc.dram_tensor("cc_m_out", [1, 2], F32)

    taps = {}
    if debug_taps:
        for nm, shp, dt in [("t_xnT", [128, KT * R], BF16), ("t_EKcol", [128, KT * 128], BF16),
                            ("t_Ball", [128, 128], BF16), ("t_H8tok", [128, 64], F32),
                            ("t_dstok", [128, 64], F32), ("t_wnb", [128, 64], BF16),
                            ("t_mulT", [32, R], BF16), ("t_meanB", [128, R], F32),
                            ("t_rstdB", [128, R], F32), ("t_gmax", [1, 8], F32),
                            ("t_Sn", [128, 512], BF16), ("t_zT", [32, R], BF16),
                            ("t_H8T", [8, R], F32)]:
            taps[nm] = nc.dram_tensor(nm, shp, dt, kind="ExternalOutput").ap()

    from contextlib import ExitStack
    with ExitStack() as st:
        E = st.enter_context

        xnT = E(nc.sbuf_tensor("xnT", [128, KT * R], BF16))
        wstage = E(nc.sbuf_tensor("wstage", [128, 16 * NGW], BF16))
        wmbuf = E(nc.sbuf_tensor("wmbuf", [128, 2 * KT * NGW], BF16))
        Wbuf = E(nc.sbuf_tensor("Wbuf", [128, KT * NGW], BF16))
        mstage = E(nc.sbuf_tensor("mstage", [128, 4 * 512], BF16))
        EKcol = E(nc.sbuf_tensor("EKcol", [128, KT * 128], BF16))
        xf = E(nc.sbuf_tensor("xf", [128, 2 * R], F32))

        meanB = E(nc.sbuf_tensor("meanB", [128, R], F32))
        rstdB = E(nc.sbuf_tensor("rstdB", [128, R], F32))
        stats = E(nc.sbuf_tensor("stats", [1, 3072], F32))
        outstg = E(nc.sbuf_tensor("outstg", [128, 1024], F32))
        xab = E(nc.sbuf_tensor("xab", [128, 2 * R], BF16))
        xsqb = E(nc.sbuf_tensor("xsqb", [128, 2 * R], BF16))
        gat = E(nc.sbuf_tensor("gat", [128, 832], F32))
        Sn = E(nc.sbuf_tensor("Sn", [128, 512], BF16))
        B16 = E(nc.sbuf_tensor("B16", [16, 128], BF16))
        Ball = E(nc.sbuf_tensor("Ball", [128, 128], BF16))
        BallT = E(nc.sbuf_tensor("BallT", [128, 128], BF16))
        zTb = E(nc.sbuf_tensor("zTb", [32, R], BF16))
        H8T = E(nc.sbuf_tensor("H8T", [8, R], F32))
        dsT = E(nc.sbuf_tensor("dsT", [8, R], F32))
        H8tok = E(nc.sbuf_tensor("H8tok", [128, 64], F32))
        dstok = E(nc.sbuf_tensor("dstok", [128, 64], F32))
        wnb = E(nc.sbuf_tensor("wnb", [128, 64], BF16))
        wfb = E(nc.sbuf_tensor("wfb", [128, 64], BF16))
        wfT = E(nc.sbuf_tensor("wfT", [8, R], BF16))
        mulT = E(nc.sbuf_tensor("mulT", [32, R], BF16))
        A_bf = E(nc.sbuf_tensor("A_bf", [128, KT * 32], BF16))
        W8_bf = E(nc.sbuf_tensor("W8_bf", [128, KT * 8], BF16))
        Bm_bf = E(nc.sbuf_tensor("Bm_bf", [32, D], BF16))
        b_sb = E(nc.sbuf_tensor("b_sb", [32, 128], F32))
        b_col = E(nc.sbuf_tensor("b_col", [128, 32], F32))
        b8B = E(nc.sbuf_tensor("b8B", [128, 8], F32))
        b8r = E(nc.sbuf_tensor("b8r", [1, 8], F32))
        LiM = E(nc.sbuf_tensor("LiMs", [8, 32], F32))
        LiM_bf = E(nc.sbuf_tensor("LiM_bf", [8, 32], BF16))
        LiMs_bf = E(nc.sbuf_tensor("LiMsc_bf", [8, 32], BF16))
        LiMT_bf = E(nc.sbuf_tensor("LiMT_bf", [32, 8], BF16))
        idn = E(nc.sbuf_tensor("idn_sb", [128, 128], BF16))
        idnf = E(nc.sbuf_tensor("idnf_sb", [128, 128], F32))
        avgp = E(nc.sbuf_tensor("avgp_sb", [128, 128], BF16))
        EXPc = E(nc.sbuf_tensor("EXP_sb", [16, 512], BF16))
        balltc = E(nc.sbuf_tensor("balltc", [16, 128], BF16))
        Q4 = E(nc.sbuf_tensor("Q4_sb", [128, 4], BF16))
        oddm = E(nc.sbuf_tensor("oddm_sb", [128, 1], F32))
        onesc = E(nc.sbuf_tensor("onesc_sb", [1, 128], F32))
        pairP = E(nc.sbuf_tensor("pairP_sb", [128, 128], BF16))
        gmax = E(nc.sbuf_tensor("gmax", [1, 16], F32))
        scB = E(nc.sbuf_tensor("scB", [128, 2], F32))

        tmpf = mstage[:, :].bitcast(F32)
        ekstrip = gat[:, 576:832].bitcast(BF16)

        pm = [E(nc.psum_tensor(f"pm{i}", [128, 512], F32)) for i in range(4)]
        pa = E(nc.psum_tensor("pa", [128, 512], F32))
        pb = E(nc.psum_tensor("pb", [128, 512], F32))
        pc = E(nc.psum_tensor("pc", [128, 512], F32))
        pbf = E(nc.psum_tensor("pbf", [128, 512], BF16))

        sem_names = ["S_cst", "S_xa0", "S_xa1", "S_xb0", "S_xb1", "S_res0", "S_res1",
                     "S_ccb", "S_ccm", "G_msk", "G_wst", "G_wbf0", "G_wbf1", "G_wbf2", "G_wbf3", "G_cmn", "G_cc", "G_cc2",
                     "SCQ0", "SCQ1", "CC1", "CC2", "T", "V", "SC"]
        SEM = {nm: E(nc.semaphore(nm)) for nm in sem_names}
        DMA_SEMS = {"S_cst", "S_xa0", "S_xa1", "S_xb0", "S_xb1", "S_res0", "S_res1",
                    "S_ccb", "S_ccm", "G_msk", "G_wst", "G_wbf0", "G_wbf1", "G_wbf2", "G_wbf3", "G_cmn", "G_cc", "G_cc2",
                    "SCQ0", "SCQ1"}
        CC_SEMS = {"CC1", "CC2"}

        block = E(nc.Block())

        PLAN = {e: [] for e in ENGS}
        CNT = {nm: 0 for nm in sem_names}
        CHK = {}

        nops = [0]

        def op(eng, emit, waits=(), inc=None, chk=None):
            nops[0] += 1
            if STOP_N is not None and nops[0] > STOP_N:
                return
            PLAN[eng].append((emit, list(waits), inc))
            if inc is not None:
                CNT[inc] += 16 if inc in DMA_SEMS else 1
                if chk is not None:
                    CHK[chk] = (inc, CNT[inc])

        def wc(name):
            if name not in CHK and STOP_N is not None:
                return ("S_cst", 0)
            return CHK[name]

        wm3 = d_wm.rearrange("p (g t n) -> p g t n", g=NG, t=KT)
        W3 = d_W.rearrange("p (g t n) -> p g t n", g=NG, t=KT)


        for dst, src in [(idn[:, :], d_idn), (idnf[:, :], d_idnf), (avgp[:, :], d_avgp),
                         (EXPc[:, :], d_EXP), (Q4[:, :], d_Q4), (oddm[:, :], d_oddm),
                         (onesc[:, :], d_onesc), (pairP[:, :], d_pairP),
                         (b_sb[:, :], d_b32), (LiM[:, :], d_LiM), (b8r[:, :], d_b8)]:
            op("sync", (lambda e, a=dst, b=src: e.dma_start(out=a, in_=b)), inc="S_cst")
        CHK["csts"] = ("S_cst", CNT["S_cst"])

        def em_mask(q):
            def f(e):
                return e.dma_start(out=mstage[:, :],
                                   in_=d_msk[:, q * 2048:(q + 1) * 2048],
                                   max_dma_last_dim=8192)
            return f
        op("gpsimd", em_mask(0), inc="G_msk", chk="mq0")
        op("gpsimd", lambda e: e.dma_start(out=wstage[:, :], in_=wm3[:, 0, 0:16]),
           inc="G_wst", chk="wst0h0")
        for sb_ in range(4):
            op("gpsimd", (lambda e, sb_=sb_: e.dma_start(
                out=Wbuf[:, sb_ * 8 * NGW:(sb_ + 1) * 8 * NGW],
                in_=W3[:, 0, 8 * sb_:8 * (sb_ + 1)])),
                inc=f"G_wbf{sb_}")
        op("gpsimd", lambda e: e.dma_start(out=A_bf[:, :], in_=d_A), inc="G_cmn")
        op("gpsimd", lambda e: e.dma_start(out=W8_bf[:, :], in_=d_W8), inc="G_cmn")
        op("gpsimd", lambda e: e.dma_start(out=Bm_bf[:, :], in_=d_Bm), inc="G_cmn", chk="cmn")

        def emit_mask_batch(q):
            for t in range(4):
                ktg = 4 * q + t

                def em_red(e, t=t, ktg=ktg):
                    with nc.allow_low_precision(reason="mask block-sums <=32 exact in bf16"):
                        return e.tensor_reduce(
                            Sn[:, ktg * 16:(ktg + 1) * 16],
                            mstage[:, t * 512:(t + 1) * 512]
                                .rearrange("p (nb w) -> p nb w", w=32),
                            axis=AX.X, op=ALU.add)
                op("vector", em_red, waits=[wc(f"mq{q}")], inc="V")
            CHK[f"mrq{q}"] = ("V", CNT["V"])
            if q < 7:
                op("gpsimd", em_mask(q + 1), waits=[wc(f"mrq{q}")], inc="G_msk",
                   chk=f"mq{q+1}")

        mask_trigger = {3: 0, 6: 1, 9: 2, 12: 3, 15: 4, 18: 5, 21: 6, 24: 7}
        for kt in range(KT):
            sl = kt % 2
            waits = []
            if kt >= 2:
                waits += [wc(f"sqA{kt-2}"), wc(f"xabV{kt-2}")]
            op("sync", (lambda e, kt=kt, sl=sl: e.dma_start(
                out=xf[:, sl * R:(sl + 1) * R], in_=d_xT[kt * 128:(kt + 1) * 128, :])),
                waits=waits, inc=f"S_xa{sl}", chk=f"xa{kt}")
            waits = [wc(f"xa{kt}")]
            if kt >= 2:
                waits.append(wc(f"stT{kt-2}"))
            op("vector", (lambda e, sl=sl: e.tensor_copy(
                xab[:, sl * R:(sl + 1) * R], xf[:, sl * R:(sl + 1) * R])),
                waits=waits, inc="V", chk=f"xabV{kt}")
            op("scalar", (lambda e, sl=sl: e.activation(
                xsqb[:, sl * R:(sl + 1) * R], xf[:, sl * R:(sl + 1) * R], AF.Square)),
                waits=waits, inc="SC", chk=f"sqA{kt}")
            st_ = (kt == 0)
            sp = (kt == KT - 1)
            wst_ = [wc(f"sqA{kt}"), wc(f"xabV{kt}")]
            if kt == 0:
                wst_.append(wc("csts"))
            op("tensor", (lambda e, sl=sl, st_=st_, sp=sp: e.matmul(
                pm[0][:, :], avgp[:, :], xab[:, sl * R:sl * R + 512], start=st_, stop=sp)),
                waits=wst_)
            op("tensor", (lambda e, sl=sl, st_=st_, sp=sp: e.matmul(
                pm[1][:, :], avgp[:, :], xab[:, sl * R + 512:sl * R + 1024],
                start=st_, stop=sp)))
            op("tensor", (lambda e, sl=sl, st_=st_, sp=sp: e.matmul(
                pm[2][:, :], avgp[:, :], xsqb[:, sl * R:sl * R + 512], start=st_, stop=sp)))
            op("tensor", (lambda e, sl=sl, st_=st_, sp=sp: e.matmul(
                pm[3][:, :], avgp[:, :], xsqb[:, sl * R + 512:sl * R + 1024],
                start=st_, stop=sp)), inc="T", chk=f"stT{kt}")
            if kt in mask_trigger:
                emit_mask_batch(mask_trigger[kt])
            if kt == 26:
                op("tensor", lambda e: e.transpose(pa[:, 0:32], b_sb[:, :],
                                                   idnf[0:32, 0:32]),
                   waits=[wc("csts")], inc="T", chk="bcolT")
                op("tensor", lambda e: e.matmul(pc[:, 0:8], onesc[:, :], b8r[:, :],
                                                start=True, stop=True), inc="T", chk="b8T")
                op("vector", lambda e: e.tensor_copy(b_col[:, :], pa[:, 0:32]),
                   waits=[wc("bcolT")], inc="V", chk="bcolV")
                op("vector", lambda e: e.tensor_copy(b8B[:, :], pc[:, 0:8]),
                   waits=[wc("b8T")], inc="V", chk="b8V")
                for ktb in range(KT):
                    w2 = [wc("mrq7"), wc("b8V")] if ktb == 0 else []
                    op("tensor", (lambda e, ktb=ktb: e.matmul(
                        pc[0:16, 4 * ktb:4 * ktb + 4], Sn[:, ktb * 16:(ktb + 1) * 16],
                        Q4[:, :], start=True, stop=True)), waits=w2,
                        inc=("T" if ktb == KT - 1 else None),
                        chk=(f"b16T{KT-1}" if ktb == KT - 1 else None))
                op("vector", lambda e: e.tensor_scalar(B16[:, :], pc[0:16, 0:128], 0.0,
                                                       None, op0=ALU.is_gt),
                   waits=[wc(f"b16T{KT-1}")], inc="V", chk="bthr")
                op("sync", lambda e: e.dma_start(out=cc_b_in.ap(), in_=B16[:, :]),
                   waits=[wc("bthr")], inc="S_ccb", chk="ccbin")
                if SKIP_CC:
                    def em_ag(e):
                        return e.dma_start(out=cc_b_out.ap()[0:16, :], in_=cc_b_in.ap())
                    op("gpsimd", em_ag, waits=[wc("ccbin")], inc="CC1", chk="ag")
                else:
                    op("gpsimd", lambda e: e.collective_compute(
                        "AllGather", ALU.bypass, replica_groups=[list(range(NCORES))],
                        ins=[cc_b_in.ap().opt()], outs=[cc_b_out.ap().opt()]),
                        waits=[wc("ccbin")], inc="CC1", chk="ag")
                op("gpsimd", lambda e: e.dma_start(out=Ball[:, :], in_=cc_b_out.ap()),
                   waits=[("CC1", 1)], inc="G_cc", chk="ballin")

        op("vector", lambda e: e.tensor_copy(stats[0:1, 0:512], pm[0][0:1, :]),
           waits=[wc(f"stT{KT-1}")], inc="V")
        op("vector", lambda e: e.tensor_copy(stats[0:1, 512:1024], pm[1][0:1, :]), inc="V")
        op("vector", lambda e: e.tensor_copy(stats[0:1, 1024:1536], pm[2][0:1, :]), inc="V")
        op("vector", lambda e: e.tensor_copy(stats[0:1, 1536:2048], pm[3][0:1, :]), inc="V",
           chk="stev")
        op("vector", lambda e: e.tensor_tensor(stats[0:1, 2048:3072], stats[0:1, 0:1024],
                                               stats[0:1, 0:1024], op=ALU.mult), inc="V")
        op("vector", lambda e: e.tensor_tensor(stats[0:1, 1024:2048], stats[0:1, 1024:2048],
                                               stats[0:1, 2048:3072], op=ALU.subtract),
           inc="V")
        op("vector", lambda e: e.tensor_scalar(stats[0:1, 1024:2048], stats[0:1, 1024:2048],
                                               EPS_LN, None, op0=ALU.add), inc="V",
           chk="varV")
        op("scalar", lambda e: e.activation(stats[0:1, 2048:3072], stats[0:1, 1024:2048],
                                            AF.Ln),
           waits=[wc("varV")], inc="SC")
        op("scalar", lambda e: e.activation(stats[0:1, 1024:2048], stats[0:1, 2048:3072],
                                            AF.Exp, scale=-0.5), inc="SC", chk="rstdS")
        op("tensor", lambda e: e.matmul(pa[:, :], onesc[:, :], stats[0:1, 0:512],
                                        start=True, stop=True),
           waits=[wc("stev"), wc("bcolV")], inc="T", chk="bc0")
        op("tensor", lambda e: e.matmul(pb[:, :], onesc[:, :], stats[0:1, 512:1024],
                                        start=True, stop=True), inc="T", chk="bc1")
        op("vector", lambda e: e.tensor_copy(meanB[:, 0:512], pa[:, :]),
           waits=[wc("bc0")], inc="V", chk="bc0V")
        op("vector", lambda e: e.tensor_copy(meanB[:, 512:1024], pb[:, :]),
           waits=[wc("bc1")], inc="V", chk="bc1V")
        op("tensor", lambda e: e.matmul(pa[:, :], onesc[:, :], stats[0:1, 1024:1536],
                                        start=True, stop=True),
           waits=[wc("rstdS"), wc("bc0V")], inc="T", chk="bc2")
        op("tensor", lambda e: e.matmul(pb[:, :], onesc[:, :], stats[0:1, 1536:2048],
                                        start=True, stop=True),
           waits=[wc("bc1V")], inc="T", chk="bc3")
        op("vector", lambda e: e.tensor_copy(rstdB[:, 0:512], pa[:, :]),
           waits=[wc("bc2")], inc="V", chk="bc2V")
        op("vector", lambda e: e.tensor_copy(rstdB[:, 512:1024], pb[:, :]),
           waits=[wc("bc3")], inc="V", chk="bc3V")

        op("tensor", lambda e: e.transpose(pbf[:, 0:128], Ball[:, :], idn[:, :]),
           waits=[wc("ballin")], inc="T", chk="ballT")
        op("vector", lambda e: e.tensor_copy(BallT[:, :], pbf[:, 0:128]),
           waits=[wc("ballT")], inc="V", chk="ballTV")
        for c in range(8):
            bank = pa if c % 2 == 0 else pb
            wdma = [wc("ballTV")]
            if c >= 1:
                wdma.append(wc(f"ekmm{c-1}"))
            op("sync", (lambda e, c=c: e.dma_start(
                out=balltc[:, :], in_=BallT[16 * c:16 * (c + 1), :])),
                waits=wdma, inc="S_ccb", chk=f"btc{c}")
            waits = [wc(f"btc{c}")]
            if c >= 2:
                waits.append(wc(f"ekstrip{c-2}"))
            if c < 2:
                waits.append(wc("bc2V") if c == 0 else wc("bc3V"))
            op("tensor", (lambda e, bank=bank: e.matmul(
                bank[:, :], balltc[:, :], EXPc[:, :],
                start=True, stop=True)), waits=waits, inc="T", chk=f"ekmm{c}")
            wv = [wc(f"ekmm{c}")]
            if c >= 1:
                wv.append(wc(f"ektr{c-1}_3"))
            op("vector", (lambda e, bank=bank: e.tensor_copy(
                ekstrip[:, 0:512], bank[:, :])),
                waits=wv, inc="V", chk=f"ekstrip{c}")
            for t in range(4):
                wt = [wc(f"ekstrip{c}")]
                if c > 0 and t == 0:
                    wt.append(wc(f"ekcol{c-1}"))
                op("tensor", (lambda e, t=t: e.transpose(
                    pbf[:, t * 128:(t + 1) * 128],
                    ekstrip[:, t * 128:(t + 1) * 128], idn[:, :])),
                    waits=wt, inc="T", chk=f"ektr{c}_{t}")
            op("vector", (lambda e, c=c: e.tensor_copy(
                EKcol[:, c * 512:(c + 1) * 512], pbf[:, :])),
                waits=[wc(f"ektr{c}_3")], inc="V", chk=f"ekcol{c}")

        for kt in range(KT):
            sl = kt % 2
            waits = []
            if kt < 2:
                waits += [wc(f"sqA{kt+30}"), wc(f"xabV{kt+30}")]
            else:
                waits.append(wc(f"xn{kt-2}"))
            op("sync", (lambda e, kt=kt, sl=sl: e.dma_start(
                out=xf[:, sl * R:(sl + 1) * R], in_=d_xT[kt * 128:(kt + 1) * 128, :])),
                waits=waits, inc=f"S_xb{sl}", chk=f"xb{kt}")
            wv = [wc(f"xb{kt}"), wc("bc3V"), wc("bc2V")]
            if kt == 0:
                wv.append(wc("mrq7"))
            op("vector", (lambda e, sl=sl: e.tensor_tensor(
                tmpf[:, 0:R], xf[:, sl * R:(sl + 1) * R], meanB[:, :], op=ALU.subtract)),
                waits=wv, inc="V")
            op("vector", (lambda e, kt=kt: e.tensor_tensor(
                xnT[:, kt * R:(kt + 1) * R], tmpf[:, 0:R], rstdB[:, :], op=ALU.mult)),
                inc="V", chk=f"xn{kt}")
            op("vector", (lambda e, kt=kt: e.tensor_tensor(
                wmbuf[:, kt * NGW:(kt + 1) * NGW].rearrange("p (nb w) -> p nb w", w=32),
                wstage[:, (kt % 16) * NGW:((kt % 16) + 1) * NGW]
                    .rearrange("p (nb w) -> p nb w", w=32),
                EKcol[:, kt * 128:kt * 128 + 8].unsqueeze(-1).broadcast_to([128, 8, 32]),
                op=ALU.mult)),
                waits=[("G_wst", 16 * (kt // 16 + 1)), wc(f"ekcol{kt//4}")],
                inc="V", chk=f"vwm0k{kt}")
            sp = (kt == KT - 1)
            wz = [wc(f"xn{kt}")]
            if kt == 0:
                wz += [wc("cmn"), wc("bc2V"), wc("bc3V"), wc("ekcol7"), wc("bthr")]
            op("tensor", (lambda e, kt=kt, sp=sp: e.matmul(
                pa[0:32, :], A_bf[:, kt * 32:(kt + 1) * 32], xnT[:, kt * R:kt * R + 512],
                start=(kt == 0), stop=sp, skip_group_check=True)), waits=wz)
            op("tensor", (lambda e, kt=kt, sp=sp: e.matmul(
                pb[0:32, :], A_bf[:, kt * 32:(kt + 1) * 32],
                xnT[:, kt * R + 512:kt * R + 1024],
                start=(kt == 0), stop=sp, skip_group_check=True)))
            op("tensor", (lambda e, kt=kt, sp=sp: e.matmul(
                pc[0:8, :], W8_bf[:, kt * 8:(kt + 1) * 8], xnT[:, kt * R:kt * R + 512],
                start=(kt == 0), stop=sp, skip_group_check=True)),
                inc=("T" if sp else None), chk=("zsw" if sp else None))
            for j in (0, 1):
                pmA = pm[2 * j]
                pmB = pm[2 * j + 1]
                wj = [wc(f"vwm0k{kt}")]
                if kt == 0:
                    wj.append(wc("stev"))
                if kt % 8 == 0:
                    wj.append((f"G_wbf{kt//8}", 16))
                wof = kt * NGW + j * 128
                mkchk = (j == 1 and kt % 8 == 7)
                op("tensor", (lambda e, pmA=pmA, wof=wof, kt=kt, st_=(kt == 0): e.matmul(
                    pmA[:, :], wmbuf[:, wof:wof + 128], xnT[:, kt * R:kt * R + 512],
                    start=st_, stop=False, skip_group_check=True)), waits=wj)
                op("tensor", (lambda e, pmB=pmB, wof=wof, kt=kt, st_=(kt == 0): e.matmul(
                    pmB[:, :], wmbuf[:, wof:wof + 128], xnT[:, kt * R + 512:kt * R + 1024],
                    start=st_, stop=False, skip_group_check=True)),
                    inc=("T" if mkchk else None),
                    chk=(f"tWMj1k{kt}" if mkchk else None))
                op("tensor", (lambda e, pmA=pmA, wof=wof, kt=kt: e.matmul(
                    pmA[:, :], Wbuf[:, wof:wof + 128], xnT[:, kt * R:kt * R + 512],
                    start=False, stop=False, skip_group_check=True)))
                op("tensor", (lambda e, pmB=pmB, wof=wof, kt=kt: e.matmul(
                    pmB[:, :], Wbuf[:, wof:wof + 128], xnT[:, kt * R + 512:kt * R + 1024],
                    start=False, stop=False, skip_group_check=True)),
                    inc=("T" if mkchk else None),
                    chk=(f"tWj1k{kt}" if mkchk else None))

        op("gpsimd", lambda e: e.dma_start(out=wstage[:, :], in_=wm3[:, 0, 16:32]),
           waits=[wc("vwm0k15")], inc="G_wst", chk="wst0h1")
        op("gpsimd", lambda e: e.dma_start(out=wstage[:, :], in_=wm3[:, 1, 0:16]),
           waits=[wc(f"vwm0k{KT-1}")], inc="G_wst", chk="wst1h0")

        op("vector", lambda e: e.tensor_copy(zTb[:, 0:512], pa[0:32, :]),
           waits=[wc("zsw")], inc="V", chk="zloV")
        op("vector", lambda e: e.tensor_copy(zTb[:, 512:1024], pb[0:32, :]), inc="V",
           chk="zhiV")
        op("vector", lambda e: e.tensor_copy(H8T[:, 0:512], pc[0:8, :]), inc="V", chk="zevV")
        CHK["zevS"] = CHK["zevV"]
        for kt in range(KT):
            sp = (kt == KT - 1)
            waits = [wc("zevV")] if kt == 0 else []
            op("tensor", (lambda e, kt=kt, sp=sp: e.matmul(
                pc[0:8, :], W8_bf[:, kt * 8:(kt + 1) * 8],
                xnT[:, kt * R + 512:kt * R + 1024],
                start=(kt == 0), stop=sp, skip_group_check=True)),
                waits=waits, inc=("T" if sp else None), chk=("h8hi" if sp else None))
        op("vector", lambda e: e.tensor_copy(H8T[:, 512:1024], pc[0:8, :]),
           waits=[wc("h8hi")], inc="V", chk="h8hiV")

        op("scalar", lambda e: e.activation(LiM_bf[:, :], LiM[:, :], AF.Identity),
           waits=[wc("csts")], inc="SC", chk="limS")
        op("scalar", lambda e: e.activation(LiMs_bf[:, :], LiM[:, :], AF.Identity,
                                            scale=SCALING), inc="SC", chk="limsS")
        op("tensor", lambda e: e.transpose(pbf[0:32, 0:8], LiM_bf[:, :], idn[0:8, 0:8]),
           waits=[wc("limS"), wc("ekcol7")], inc="T", chk="limT")
        op("vector", lambda e: e.tensor_copy(LiMT_bf[:, :], pbf[0:32, 0:8]),
           waits=[wc("limT")], inc="V", chk="limV")

        op("tensor", lambda e: e.matmul(pa[0:8, :], LiMT_bf[:, :], zTb[:, 0:512],
                                        start=True, stop=True),
           waits=[wc("zevV"), wc("zevS"), wc("limV")], inc="T", chk="ds0")
        op("tensor", lambda e: e.matmul(pb[0:8, :], LiMT_bf[:, :], zTb[:, 512:1024],
                                        start=True, stop=True), inc="T", chk="ds1")
        op("vector", lambda e: e.tensor_copy(dsT[:, 0:512], pa[0:8, :]),
           waits=[wc("ds0")], inc="V")
        op("vector", lambda e: e.tensor_copy(dsT[:, 512:1024], pb[0:8, :]),
           waits=[wc("ds1")], inc="V", chk="dsV")

        for mt in range(8):
            waits = [wc("dsV"), wc("h8hiV")] if mt == 0 else []
            op("tensor", (lambda e, mt=mt: e.transpose(
                pa[:, mt * 8:(mt + 1) * 8], H8T[0:8, mt * 128:(mt + 1) * 128],
                idnf[0:8, 0:8])), waits=waits, inc="T")
            op("tensor", (lambda e, mt=mt: e.transpose(
                pb[:, mt * 8:(mt + 1) * 8], dsT[0:8, mt * 128:(mt + 1) * 128],
                idnf[0:8, 0:8])), inc="T", chk=("tokT" if mt == 7 else None))
        op("vector", lambda e: e.tensor_copy(H8tok[:, :], pa[:, 0:64]),
           waits=[wc("tokT")], inc="V")
        op("vector", lambda e: e.tensor_copy(dstok[:, :], pb[:, 0:64]), inc="V", chk="tokV")
        op("vector", lambda e: e.tensor_tensor(
            H8tok[:, :].rearrange("p (mt e) -> p mt e", e=8),
            H8tok[:, :].rearrange("p (mt e) -> p mt e", e=8),
            b8B[:, :].unsqueeze(1).broadcast_to([128, 8, 8]), op=ALU.add),
            waits=[wc("b8V")], inc="V", chk="h8bias")

        op("vector", lambda e: e.reduce_max(gat[:, 0:1], H8tok[:, :], axis=AX.X,
                                            apply_absolute_value=True),
           waits=[wc("h8bias")], inc="V")
        op("vector", lambda e: e.reduce_max(gat[:, 1:2], dstok[:, :], axis=AX.X,
                                            apply_absolute_value=True),
           waits=[wc("tokV")], inc="V")
        op("vector", lambda e: e.tensor_tensor(gat[:, 0:2], gat[:, 0:2],
                                               oddm[:, :].broadcast_to([128, 2]),
                                               op=ALU.mult),
           waits=[wc("csts")], inc="V", chk="rmax")
        op("tensor", lambda e: e.transpose(pc[0:2, 0:128], gat[:, 0:2], idnf[:, :]),
           waits=[wc("rmax"), wc("bthr")], inc="T", chk="mxT")
        op("vector", lambda e: e.reduce_max(gat[0:2, 8:9], pc[0:2, 0:128], axis=AX.X),
           waits=["mxT"], inc="V", chk="mxV")
        op("sync", lambda e: e.dma_start(out=cc_m_in.ap(), in_=gat[0:2, 8:9]),
           waits=[wc("mxV")], inc="S_ccm", chk="ccmin")
        if SKIP_CC:
            op("gpsimd", lambda e: e.dma_start(out=cc_m_out.ap(), in_=cc_m_in.ap()),
               waits=[wc("ccmin")], inc="CC2", chk="ar")
        else:
            op("gpsimd", lambda e: e.collective_compute(
                "AllReduce", ALU.max, replica_groups=[list(range(NCORES))],
                ins=[cc_m_in.ap().opt()], outs=[cc_m_out.ap().opt()]),
                waits=[wc("ccmin")], inc="CC2", chk="ar")
        op("gpsimd", lambda e: e.dma_start(out=gmax[0:1, 0:2], in_=cc_m_out.ap()),
           waits=[("CC2", 1)], inc="G_cc2", chk="gmaxin")

        op("vector", lambda e: e.tensor_scalar(gmax[0:1, 2:4], gmax[0:1, 0:2], EPS_G, None,
                                               op0=ALU.max),
           waits=[wc("gmaxin")], inc="V")
        op("vector", lambda e: e.reciprocal(gmax[0:1, 4:6], gmax[0:1, 2:4]), inc="V")
        op("vector", lambda e: e.tensor_scalar(gmax[0:1, 6:8], gmax[0:1, 4:6], GAMMA, None,
                                               op0=ALU.mult), inc="V", chk="scV")
        op("tensor", lambda e: e.matmul(pc[:, 0:2], onesc[:, :], gmax[0:1, 6:8],
                                        start=True, stop=True),
           waits=[wc("scV")], inc="T", chk="scT")
        op("vector", lambda e: e.tensor_copy(scB[:, :], pc[:, 0:2]),
           waits=["scT"], inc="V", chk="scBV")

        GA = gat

        def V3(ap):
            return ap.rearrange("p (mt e) -> p mt e", e=8)

        def bc8(ap):
            return ap.unsqueeze(-1).broadcast_to([128, 8, 8])

        op("vector", lambda e: e.tensor_scalar(GA[:, 64:128], H8tok[:, :], scB[:, 0:1], None,
                                               op0=ALU.mult),
           waits=[wc("scBV")], inc="V")
        op("vector", lambda e: e.scalar_tensor_tensor(GA[:, 128:192], dstok[:, :],
                                                      scB[:, 1:2], GA[:, 64:128],
                                                      op0=ALU.mult, op1=ALU.add),
           inc="V", chk="logit")
        op("scalar", lambda e: e.activation(GA[:, 192:256], GA[:, 128:192], AF.Exp),
           waits=[wc("logit")], inc="SC", chk="expS")
        op("vector", lambda e: e.tensor_reduce(GA[:, 256:264], V3(GA[:, 192:256]),
                                               axis=AX.X, op=ALU.add),
           waits=[wc("expS")], inc="V")
        op("vector", lambda e: e.reciprocal(GA[:, 336:344], GA[:, 256:264]), inc="V")
        op("vector", lambda e: e.tensor_tensor(V3(GA[:, 320:384]), V3(GA[:, 192:256]),
                                               bc8(GA[:, 336:344]), op=ALU.mult),
           inc="V", chk="probs")
        op("vector", lambda e: e.tensor_reduce(GA[:, 272:280], V3(GA[:, 320:384]),
                                               axis=AX.X, op=ALU.max), inc="V")
        op("vector", lambda e: e.tensor_tensor(V3(GA[:, 384:448]), V3(GA[:, 320:384]),
                                               bc8(GA[:, 272:280]), op=ALU.is_ge), inc="V")
        op("vector", lambda e: e.tensor_reduce(GA[:, 264:272], V3(GA[:, 384:448]),
                                               axis=AX.X, op=ALU.add), inc="V")
        op("vector", lambda e: e.scalar_tensor_tensor(GA[:, 448:512], GA[:, 384:448], -1e30,
                                                      GA[:, 320:384], op0=ALU.mult,
                                                      op1=ALU.add), inc="V")
        op("vector", lambda e: e.tensor_reduce(GA[:, 280:288], V3(GA[:, 448:512]),
                                               axis=AX.X, op=ALU.max), inc="V")
        op("vector", lambda e: e.tensor_scalar(GA[:, 288:296], GA[:, 264:272], 2.0, None,
                                               op0=ALU.is_ge), inc="V")
        op("vector", lambda e: e.tensor_tensor(GA[:, 296:304], GA[:, 272:280],
                                               GA[:, 280:288], op=ALU.subtract), inc="V")
        op("vector", lambda e: e.tensor_tensor(GA[:, 296:304], GA[:, 296:304],
                                               GA[:, 288:296], op=ALU.mult), inc="V")
        op("vector", lambda e: e.tensor_tensor(GA[:, 304:312], GA[:, 296:304],
                                               GA[:, 280:288], op=ALU.add), inc="V",
           chk="thr")
        op("vector", lambda e: e.tensor_tensor(V3(GA[:, 512:576]), bc8(GA[:, 304:312]),
                                               V3(GA[:, 320:384]), op=ALU.subtract),
           inc="V", chk="preV")
        op("scalar", lambda e: e.activation(GA[:, 576:640], GA[:, 512:576], AF.Exp,
                                            scale=2.0),
           waits=[wc("preV"), wc("ekcol7")], inc="SC", chk="qS")
        op("vector", lambda e: e.tensor_scalar(GA[:, 640:704], GA[:, 576:640], 1.0, None,
                                               op0=ALU.add),
           waits=[wc("qS")], inc="V")
        op("vector", lambda e: e.reciprocal(GA[:, 704:768], GA[:, 640:704]), inc="V")
        op("vector", lambda e: e.tensor_tensor(GA[:, 768:832], GA[:, 320:384],
                                               GA[:, 704:768], op=ALU.mult), inc="V")
        op("vector", lambda e: e.tensor_reduce(GA[:, 312:320], V3(GA[:, 768:832]),
                                               axis=AX.X, op=ALU.add), inc="V")
        op("vector", lambda e: e.tensor_scalar(GA[:, 312:320], GA[:, 312:320], 1e-9, None,
                                               op0=ALU.add), inc="V")
        op("vector", lambda e: e.reciprocal(GA[:, 328:336], GA[:, 312:320]), inc="V")
        op("vector", lambda e: e.tensor_tensor(V3(wnb[:, :]), V3(GA[:, 768:832]),
                                               bc8(GA[:, 328:336]), op=ALU.mult),
           inc="V", chk="wnV")

        ev_cnt = {}
        out_cnt = {}

        def plan_main_block(j):
            g = j // 2
            par = j % 2
            pmA = pm[2 * par]
            pmB = pm[2 * par + 1]
            gbase = (g % 2) * (KT * NGW)
            for kt in range(KT):
                waits = [wc(f"vwm{g}k{kt}")]
                if kt == 0:
                    waits.append(("V", ev_cnt[j - 2]))
                if kt % 8 == 0:
                    waits.append((f"G_wbf{kt//8}", 16 * (g + 1)))
                wof = gbase + kt * NGW + par * 128
                wofW = kt * NGW + par * 128
                mkchk = (par == 1 and kt % 8 == 7)
                op("tensor", (lambda e, pmA=pmA, wof=wof, kt=kt, st_=(kt == 0): e.matmul(
                    pmA[:, :], wmbuf[:, wof:wof + 128], xnT[:, kt * R:kt * R + 512],
                    start=st_, stop=False, skip_group_check=True)), waits=waits)
                op("tensor", (lambda e, pmB=pmB, wof=wof, kt=kt, st_=(kt == 0): e.matmul(
                    pmB[:, :], wmbuf[:, wof:wof + 128], xnT[:, kt * R + 512:kt * R + 1024],
                    start=st_, stop=False, skip_group_check=True)),
                    inc=("T" if mkchk else None),
                    chk=(f"tWMj{j}k{kt}" if mkchk else None))
                op("tensor", (lambda e, pmA=pmA, wofW=wofW, kt=kt: e.matmul(
                    pmA[:, :], Wbuf[:, wofW:wofW + 128], xnT[:, kt * R:kt * R + 512],
                    start=False, stop=False, skip_group_check=True)))
                op("tensor", (lambda e, pmB=pmB, wofW=wofW, kt=kt: e.matmul(
                    pmB[:, :], Wbuf[:, wofW:wofW + 128], xnT[:, kt * R + 512:kt * R + 1024],
                    start=False, stop=False, skip_group_check=True)),
                    inc=("T" if mkchk else None),
                    chk=(f"tWj{j}k{kt}" if mkchk else None))

        def plan_delta_and_evac(j):
            par = j % 2
            pmA = pm[2 * par]
            pmB = pm[2 * par + 1]
            op("tensor", (lambda e, pmA=pmA, j=j: e.matmul(
                pmA[:, :], Bm_bf[:, j * 128:(j + 1) * 128], mulT[:, 0:512],
                start=False, stop=True, skip_group_check=True)),
                waits=[wc("vmul")], inc="T")
            op("tensor", (lambda e, pmB=pmB, j=j: e.matmul(
                pmB[:, :], Bm_bf[:, j * 128:(j + 1) * 128], mulT[:, 512:1024],
                start=False, stop=True, skip_group_check=True)), inc="T", chk=f"tDj{j}")
            rsl = j % 2
            wres = [wc(f"vwm0k{KT-1}")] if j < 2 else [("V", ev_cnt[j - 2])]
            if j < 2:
                wres.append(wc(f"xn{KT-1}"))
            op("sync", (lambda e, j=j, rsl=rsl: e.dma_start(
                out=xf[:, rsl * R:(rsl + 1) * R], in_=d_xT[j * 128:(j + 1) * 128, :])),
                waits=wres, inc=f"S_res{rsl}", chk=f"res{j}")
            for h in range(2):
                bank = pmA if h == 0 else pmB
                osl = (2 * j + h) % 2
                waits = [wc(f"tDj{j}"), wc(f"res{j}")]
                if 2 * j + h >= 2:
                    prev = 2 * j + h - 2
                    waits.append(out_cnt[(prev // 2, prev % 2)])
                op("vector", (lambda e, bank=bank, osl=osl, rsl=rsl, j=j, h=h:
                    e.scalar_tensor_tensor(
                        outstg[:, osl * 512:(osl + 1) * 512], bank[:, :], b_col[:, j:j + 1],
                        xf[:, rsl * R + h * 512:rsl * R + (h + 1) * 512],
                        op0=ALU.add, op1=ALU.add)),
                    waits=waits, inc="V", chk=f"ev{j}_{h}")
                op("scalar", (lambda e, j=j, h=h, osl=osl: e.dma_start(
                    out=d_out[j * 128:(j + 1) * 128, h * 512:(h + 1) * 512],
                    in_=outstg[:, osl * 512:(osl + 1) * 512])),
                    waits=[wc(f"ev{j}_{h}")], inc=f"SCQ{osl}")
                out_cnt[(j, h)] = (f"SCQ{osl}", CNT[f"SCQ{osl}"])
            ev_cnt[j] = CHK[f"ev{j}_1"][1]

        def plan_vwm(g):
            base = (g % 2) * (KT * NGW)
            for kt in range(KT):
                waits = [("G_wst", 16 * (2 * g + kt // 16 + 1))]
                if g >= 2:
                    waits.append(wc(f"tWMj{2*g-3}k{8*(kt//8)+7}"))
                op("vector", (lambda e, g=g, kt=kt, base=base: e.tensor_tensor(
                    wmbuf[:, base + kt * NGW:base + (kt + 1) * NGW]
                        .rearrange("p (nb w) -> p nb w", w=32),
                    wstage[:, (kt % 16) * NGW:((kt % 16) + 1) * NGW]
                        .rearrange("p (nb w) -> p nb w", w=32),
                    EKcol[:, kt * 128 + 8 * g:kt * 128 + 8 * g + 8]
                        .unsqueeze(-1).broadcast_to([128, 8, 32]),
                    op=ALU.mult)),
                    waits=waits, inc="V", chk=f"vwm{g}k{kt}")

        for mt in range(8):
            waits = [wc("wnV")] if mt == 0 else []
            op("tensor", (lambda e, mt=mt: e.matmul(
                pa[:, mt * 8:(mt + 1) * 8], pairP[:, :], wnb[:, mt * 8:(mt + 1) * 8],
                start=True, stop=True)), waits=waits,
                inc=("T" if mt == 7 else None), chk=("wf8" if mt == 7 else None))
        op("vector", lambda e: e.tensor_copy(wfb[:, :], pa[:, 0:64]),
           waits=[wc("wf8")], inc="V", chk="wfbV")
        for rnd in range(2):
            for t in range(4):
                mt = rnd * 4 + t
                waits = [wc("wfbV")]
                if rnd == 1 and t == 0:
                    waits.append(wc("wfT0V"))
                op("tensor", (lambda e, mt=mt, t=t: e.transpose(
                    pbf[0:8, t * 128:(t + 1) * 128], wfb[:, mt * 8:(mt + 1) * 8],
                    idn[:, :])),
                    waits=waits, inc=("T" if t == 3 else None),
                    chk=(f"wfTr{rnd}" if t == 3 else None))
            op("vector", (lambda e, rnd=rnd: e.tensor_copy(
                wfT[0:8, rnd * 512:(rnd + 1) * 512], pbf[0:8, 0:512])),
                waits=[wc(f"wfTr{rnd}")], inc="V", chk=f"wfT{rnd}V")
        op("tensor", lambda e: e.matmul(pa[0:32, :], LiMs_bf[:, :], wfT[0:8, 0:512],
                                        start=True, stop=True),
           waits=[wc("wfT1V"), wc("limsS")], inc="T", chk="pmx0")
        op("tensor", lambda e: e.matmul(pb[0:32, :], LiMs_bf[:, :], wfT[0:8, 512:1024],
                                        start=True, stop=True), inc="T", chk="pmx1")
        op("vector", lambda e: e.tensor_tensor(mulT[:, 0:512], pa[0:32, :], zTb[:, 0:512],
                                               op=ALU.mult),
           waits=[wc("pmx0")], inc="V")
        op("vector", lambda e: e.tensor_tensor(mulT[:, 512:1024], pb[0:32, :],
                                               zTb[:, 512:1024], op=ALU.mult),
           waits=[wc("pmx1")], inc="V", chk="vmul")

        plan_vwm(1)
        plan_delta_and_evac(0)
        plan_delta_and_evac(1)
        plan_vwm(2)
        nb_eff = NB if NB_EFF is None else NB_EFF
        for j in range(2, nb_eff):
            plan_main_block(j)
            plan_delta_and_evac(j)
            if j % 2 == 1 and (j + 3) // 2 < NG:
                plan_vwm((j + 3) // 2)

        for g in range(1, (NB if NB_EFF is None else NB_EFF) // 2):
            for sb_ in range(4):
                op("gpsimd", (lambda e, g=g, sb_=sb_: e.dma_start(
                    out=Wbuf[:, sb_ * 8 * NGW:(sb_ + 1) * 8 * NGW],
                    in_=W3[:, g, 8 * sb_:8 * (sb_ + 1)])),
                    waits=[wc(f"tWj{2*g-1}k{8*sb_+7}")], inc=f"G_wbf{sb_}")
            op("gpsimd", (lambda e, g=g: e.dma_start(out=wstage[:, :],
                                                     in_=wm3[:, g, 16:32])),
               waits=[wc(f"vwm{g}k15")], inc="G_wst", chk=f"wst{g}h1")
            if g + 1 < NG:
                op("gpsimd", (lambda e, g=g: e.dma_start(out=wstage[:, :],
                                                         in_=wm3[:, g + 1, 0:16])),
                   waits=[wc(f"vwm{g}k{KT-1}")], inc="G_wst", chk=f"wst{g+1}h0")

        if debug_taps:
            tap_list = [
                ("t_xnT", xnT[:, :], [wc(f"xn{KT-1}")]),
                ("t_EKcol", EKcol[:, :], [wc("ekcol7")]),
                ("t_Ball", Ball[:, :], [wc("ballin")]),
                ("t_H8tok", H8tok[:, :], [wc("h8bias")]),
                ("t_dstok", dstok[:, :], [wc("tokV")]),
                ("t_wnb", wnb[:, :], [wc("wnV")]),
                ("t_mulT", mulT[:, :], [wc("vmul")]),
                ("t_meanB", meanB[:, :], [wc("bc1V")]),
                ("t_rstdB", rstdB[:, :], [wc("bc3V")]),
                ("t_gmax", gmax[0:1, 0:8], [wc("scV")]),
                ("t_Sn", Sn[:, :], [wc("mrq7")]),
                ("t_zT", zTb[:, :], [wc("zhiV")]),
                ("t_H8T", H8T[:, :], [wc("zevV")]),
            ]
            for nm, src, waits in tap_list:
                op("sync", (lambda e, nm=nm, src=src: e.dma_start(out=taps[nm], in_=src)),
                   waits=waits, inc="S_cst")

        PLAN["scalar"].append((None, [("SCQ0", CNT["SCQ0"]), ("SCQ1", CNT["SCQ1"])], None))
        PLAN["sync"].append((None, [("S_cst", CNT["S_cst"])], None))

        def make_runner(eng_name):
            own_sem = {"vector": "V", "scalar": "SC"}.get(eng_name)
            def run(engine):
                seen = {}
                own = 0
                for emit, waits, inc in PLAN[eng_name]:
                    if own_sem is not None and emit is not None and own > 0:
                        if seen.get(own_sem, 0) < own:
                            engine.wait_ge(SEM[own_sem], own)
                            seen[own_sem] = own
                    for wspec in waits:
                        if isinstance(wspec, str):
                            if wspec not in CHK and STOP_N is not None:
                                continue
                            semn, val = CHK[wspec]
                        else:
                            semn, val = wspec
                        if seen.get(semn, 0) < val:
                            engine.wait_ge(SEM[semn], val)
                            seen[semn] = val
                    if emit is None:
                        continue
                    inst = emit(engine)
                    if inc == own_sem:
                        own += 1
                    if inc is not None:
                        if inc in CC_SEMS:
                            if SKIP_CC:
                                inst.then_inc(SEM[inc], 16)
                            else:
                                inst.then_inc(SEM[inc])
                        elif inc in DMA_SEMS:
                            inst.then_inc(SEM[inc], 16)
                        else:
                            inst.then_inc(SEM[inc], 1)
            return run

        block.sync(make_runner("sync"))
        block.gpsimd(make_runner("gpsimd"))
        block.scalar(make_runner("scalar"))
        block.tensor(make_runner("tensor"))
        block.vector(make_runner("vector"))

    return nc


def _consts():
    idn = np.eye(128, dtype=ml_dtypes.bfloat16)
    idnf = np.eye(128, dtype=np.float32)
    avgp = np.zeros((128, 128), dtype=ml_dtypes.bfloat16)
    avgp[:, 0] = ml_dtypes.bfloat16(1.0 / 4096.0)
    EXP = np.zeros((16, 512), dtype=ml_dtypes.bfloat16)
    for kb in range(16):
        EXP[kb, kb * 32:(kb + 1) * 32] = 1
    Q4 = np.zeros((128, 4), dtype=ml_dtypes.bfloat16)
    for p in range(128):
        Q4[p, p // 32] = 1
    oddm = np.zeros((128, 1), np.float32)
    oddm[1::2, 0] = 1.0
    onesc = np.ones((1, 128), np.float32)
    pairP = np.zeros((128, 128), dtype=ml_dtypes.bfloat16)
    for m in range(128):
        pairP[m | 1, m] = 1
    return idn, idnf, avgp, EXP, Q4, oddm, onesc, pairP


def _tile_khost(M):
    K, C = M.shape
    return np.ascontiguousarray(M.reshape(32, 128, C).transpose(1, 0, 2).reshape(128, 32 * C))


def _tile_weight(Wmat):
    arr = Wmat.reshape(NG, NGW, KT, 128)
    return np.ascontiguousarray(arr.transpose(3, 0, 2, 1).reshape(128, NG * KT * NGW))


def kernel(L0_attn_data, L0_attn_weight, L0_attn_mask, L0_ffn_W, L0_ffn_b,
           L0_ffn_A, L0_ffn_Bmat, L0_ffn_LiMEs):
    X = np.ascontiguousarray(np.asarray(L0_attn_data, np.float32).reshape(2 * 4096, D))
    weight = np.asarray(L0_attn_weight, np.float32)
    mask = np.asarray(L0_attn_mask, np.int32)
    W = np.asarray(L0_ffn_W, np.float32)
    bvec = np.asarray(L0_ffn_b, np.float32)
    A = np.asarray(L0_ffn_A, np.float32)
    Bmat = np.ascontiguousarray(np.asarray(L0_ffn_Bmat, np.float32))
    LiMa = np.ascontiguousarray(np.asarray(L0_ffn_LiMEs, np.float32))

    idn, idnf, avgp, EXP, Q4, oddm, onesc, pairP = _consts()

    wmh = _tile_weight(weight)
    Wh = _tile_weight(W)
    Ah = _tile_khost(A)
    W8h = _tile_khost(np.ascontiguousarray(W[0:8, :].T))
    b32 = np.ascontiguousarray(bvec.reshape(32, 128))
    b8 = np.ascontiguousarray(bvec[0:8].reshape(1, 8))
    maskT = np.ascontiguousarray(mask.T)

    nc = build_nc(debug_taps=DEBUG_TAPS)

    in_maps = []
    for c in range(NCORES):
        xT_c = np.ascontiguousarray(X[c * R:(c + 1) * R, :].T)
        msk_c = _tile_khost(np.ascontiguousarray(maskT[:, c * 512:(c + 1) * 512]))
        in_maps.append({
            "xT": xT_c, "wmh": wmh, "Wh": Wh, "mskh": msk_c,
            "Ah": Ah, "W8h": W8h, "b32": b32, "b8": b8, "Bm": Bmat, "LiM": LiMa,
            "idn": idn, "idnf": idnf, "avgp": avgp, "EXPc": EXP, "Q4c": Q4,
            "oddm": oddm, "onesc": onesc, "pairP": pairP,
        })

    res = run_bass_kernel_spmd(nc, in_maps, core_ids=list(range(NCORES)))
    out = np.empty((2 * 4096, D), np.float32)
    for c in range(NCORES):
        out[c * R:(c + 1) * R, :] = np.asarray(res.results[c]["outT"]).T
    if DEBUG_TAPS:
        kernel.last_results = res
    return out.reshape(2, 4096, D)


# version 03
# speedup vs baseline: 1.2272x; 1.2272x over previous
import numpy as np
import ml_dtypes
import concourse.bass as bass
import concourse.mybir as mybir
from concourse.bass_utils import run_bass_kernel_spmd

F32 = mybir.dt.float32
BF16 = mybir.dt.bfloat16
I32 = mybir.dt.int32
AF = mybir.ActivationFunctionType
ALU = mybir.AluOpType
AX = mybir.AxisListType

NCORES = 8
D = 4096
R = 1024
KT = 32
NGW = 256
NG = D // NGW
NB = 32
EPS_LN = 1e-5
EPS_G = 1e-6
GAMMA = 0.5
SCALING = 16.0 / 32.0

DEBUG_TAPS = False
SKIP_CC = False
NB_EFF = None
PHASE1_ONLY = False
STOP_N = None

ENGS = ("sync", "gpsimd", "scalar", "tensor", "vector")


def build_nc(debug_taps=False):
    nc = bass.Bass("TRN2", target_bir_lowering=False, debug=False,
                   dynamic_dma_scratch_size=16384)

    d_xT = nc.dram_tensor("xT", [D, R], F32, kind="ExternalInput").ap()
    d_wm = nc.dram_tensor("wmh", [128, KT * D], F32, kind="ExternalInput").ap()
    d_W = nc.dram_tensor("Wh", [128, KT * D], F32, kind="ExternalInput").ap()
    d_msk = nc.dram_tensor("mskh", [128, KT * 512], I32, kind="ExternalInput").ap()
    d_A = nc.dram_tensor("Ah", [128, KT * 32], F32, kind="ExternalInput").ap()
    d_W8 = nc.dram_tensor("W8h", [128, KT * 8], F32, kind="ExternalInput").ap()
    d_b32 = nc.dram_tensor("b32", [32, 128], F32, kind="ExternalInput").ap()
    d_b8 = nc.dram_tensor("b8", [1, 8], F32, kind="ExternalInput").ap()
    d_Bm = nc.dram_tensor("Bm", [32, D], F32, kind="ExternalInput").ap()
    d_LiM = nc.dram_tensor("LiM", [8, 32], F32, kind="ExternalInput").ap()
    d_idn = nc.dram_tensor("idn", [128, 128], BF16, kind="ExternalInput").ap()
    d_idnf = nc.dram_tensor("idnf", [128, 128], F32, kind="ExternalInput").ap()
    d_avgp = nc.dram_tensor("avgp", [128, 128], BF16, kind="ExternalInput").ap()
    d_EXP = nc.dram_tensor("EXPc", [16, 512], BF16, kind="ExternalInput").ap()
    d_Q4 = nc.dram_tensor("Q4c", [128, 4], BF16, kind="ExternalInput").ap()
    d_oddm = nc.dram_tensor("oddm", [128, 1], F32, kind="ExternalInput").ap()
    d_onesc = nc.dram_tensor("onesc", [1, 128], F32, kind="ExternalInput").ap()
    d_pairP = nc.dram_tensor("pairP", [128, 128], BF16, kind="ExternalInput").ap()

    d_out = nc.dram_tensor("outT", [D, R], F32, kind="ExternalOutput").ap()

    cc_b_in = nc.dram_tensor("cc_b_in", [16, 128], BF16)
    cc_b_out = nc.dram_tensor("cc_b_out", [128, 128], BF16)
    cc_m_in = nc.dram_tensor("cc_m_in", [1, 2], F32)
    cc_m_out = nc.dram_tensor("cc_m_out", [1, 2], F32)

    taps = {}
    if debug_taps:
        for nm, shp, dt in [("t_xnT", [128, KT * R], BF16), ("t_EKcol", [128, KT * 128], BF16),
                            ("t_Ball", [128, 128], BF16), ("t_H8tok", [128, 64], F32),
                            ("t_dstok", [128, 64], F32), ("t_wnb", [128, 64], BF16),
                            ("t_mulT", [32, R], BF16), ("t_meanB", [128, R], F32),
                            ("t_rstdB", [128, R], F32), ("t_gmax", [1, 8], F32),
                            ("t_Sn", [128, 512], BF16), ("t_zT", [32, R], BF16),
                            ("t_H8T", [8, R], F32)]:
            taps[nm] = nc.dram_tensor(nm, shp, dt, kind="ExternalOutput").ap()

    from contextlib import ExitStack
    with ExitStack() as st:
        E = st.enter_context

        xnT = E(nc.sbuf_tensor("xnT", [128, KT * R], BF16))
        wstage = E(nc.sbuf_tensor("wstage", [128, 16 * NGW], BF16))
        wmbuf = E(nc.sbuf_tensor("wmbuf", [128, 2 * KT * NGW], BF16))
        Wbuf = E(nc.sbuf_tensor("Wbuf", [128, KT * NGW], BF16))
        mstage = E(nc.sbuf_tensor("mstage", [128, 4 * 512], BF16))
        EKcol = E(nc.sbuf_tensor("EKcol", [128, KT * 128], BF16))
        xf = E(nc.sbuf_tensor("xf", [128, 2 * R], F32))

        meanB = E(nc.sbuf_tensor("meanB", [128, R], F32))
        rstdB = E(nc.sbuf_tensor("rstdB", [128, R], F32))
        stats = E(nc.sbuf_tensor("stats", [1, 3072], F32))
        outstg = E(nc.sbuf_tensor("outstg", [128, 1024], F32))
        xw2 = E(nc.sbuf_tensor("xw2", [128, 16 * NGW], BF16))
        xab = xw2[:, 0:2 * R]
        xsqb = xw2[:, 2 * R:4 * R]
        gat = E(nc.sbuf_tensor("gat", [128, 832], F32))
        Sn = E(nc.sbuf_tensor("Sn", [128, 512], BF16))
        B16 = E(nc.sbuf_tensor("B16", [16, 128], BF16))
        Ball = E(nc.sbuf_tensor("Ball", [128, 128], BF16))
        BallT = E(nc.sbuf_tensor("BallT", [128, 128], BF16))
        zTb = E(nc.sbuf_tensor("zTb", [32, R], BF16))
        H8T = E(nc.sbuf_tensor("H8T", [8, R], F32))
        dsT = E(nc.sbuf_tensor("dsT", [8, R], F32))
        H8tok = E(nc.sbuf_tensor("H8tok", [128, 64], F32))
        dstok = E(nc.sbuf_tensor("dstok", [128, 64], F32))
        wnb = E(nc.sbuf_tensor("wnb", [128, 64], BF16))
        wfb = E(nc.sbuf_tensor("wfb", [128, 64], BF16))
        wfT = E(nc.sbuf_tensor("wfT", [8, R], BF16))
        mulT = E(nc.sbuf_tensor("mulT", [32, R], BF16))
        A_bf = E(nc.sbuf_tensor("A_bf", [128, KT * 32], BF16))
        W8_bf = E(nc.sbuf_tensor("W8_bf", [128, KT * 8], BF16))
        Bm_bf = E(nc.sbuf_tensor("Bm_bf", [32, D], BF16))
        b_sb = E(nc.sbuf_tensor("b_sb", [32, 128], F32))
        b_col = E(nc.sbuf_tensor("b_col", [128, 32], F32))
        b8B = E(nc.sbuf_tensor("b8B", [128, 8], F32))
        b8r = E(nc.sbuf_tensor("b8r", [1, 8], F32))
        LiM = E(nc.sbuf_tensor("LiMs", [8, 32], F32))
        LiM_bf = E(nc.sbuf_tensor("LiM_bf", [8, 32], BF16))
        LiMs_bf = E(nc.sbuf_tensor("LiMsc_bf", [8, 32], BF16))
        LiMT_bf = E(nc.sbuf_tensor("LiMT_bf", [32, 8], BF16))
        idn = E(nc.sbuf_tensor("idn_sb", [128, 128], BF16))
        idnf = E(nc.sbuf_tensor("idnf_sb", [128, 128], F32))
        avgp = E(nc.sbuf_tensor("avgp_sb", [128, 128], BF16))
        EXPc = E(nc.sbuf_tensor("EXP_sb", [16, 512], BF16))
        balltc = E(nc.sbuf_tensor("balltc", [16, 128], BF16))
        Q4 = E(nc.sbuf_tensor("Q4_sb", [128, 4], BF16))
        oddm = E(nc.sbuf_tensor("oddm_sb", [128, 1], F32))
        onesc = E(nc.sbuf_tensor("onesc_sb", [1, 128], F32))
        pairP = E(nc.sbuf_tensor("pairP_sb", [128, 128], BF16))
        gmax = E(nc.sbuf_tensor("gmax", [1, 16], F32))
        scB = E(nc.sbuf_tensor("scB", [128, 2], F32))

        wslot = [wstage, xw2]
        tmpf = mstage[:, :].bitcast(F32)
        ekstrip = gat[:, 576:832].bitcast(BF16)

        pm = [E(nc.psum_tensor(f"pm{i}", [128, 512], F32)) for i in range(4)]
        pa = E(nc.psum_tensor("pa", [128, 512], F32))
        pb = E(nc.psum_tensor("pb", [128, 512], F32))
        pc = E(nc.psum_tensor("pc", [128, 512], F32))
        pbf = E(nc.psum_tensor("pbf", [128, 512], BF16))

        sem_names = ["S_cst", "S_xa0", "S_xa1", "S_xb0", "S_xb1", "S_res0", "S_res1",
                     "S_ccb", "S_ccm", "G_msk", "G_wst0", "G_wst1", "G_wbf0", "G_wbf1", "G_wbf2", "G_wbf3", "G_cmn", "G_cc", "G_cc2",
                     "SCQ0", "SCQ1", "CC1", "CC2", "T", "V", "SC"]
        SEM = {nm: E(nc.semaphore(nm)) for nm in sem_names}
        DMA_SEMS = {"S_cst", "S_xa0", "S_xa1", "S_xb0", "S_xb1", "S_res0", "S_res1",
                    "S_ccb", "S_ccm", "G_msk", "G_wst0", "G_wst1", "G_wbf0", "G_wbf1", "G_wbf2", "G_wbf3", "G_cmn", "G_cc", "G_cc2",
                    "SCQ0", "SCQ1"}
        CC_SEMS = {"CC1", "CC2"}

        block = E(nc.Block())

        PLAN = {e: [] for e in ENGS}
        CNT = {nm: 0 for nm in sem_names}
        CHK = {}

        nops = [0]

        def op(eng, emit, waits=(), inc=None, chk=None):
            nops[0] += 1
            if STOP_N is not None and nops[0] > STOP_N:
                return
            PLAN[eng].append((emit, list(waits), inc))
            if inc is not None:
                CNT[inc] += 16 if inc in DMA_SEMS else 1
                if chk is not None:
                    CHK[chk] = (inc, CNT[inc])

        def wc(name):
            if name not in CHK and STOP_N is not None:
                return ("S_cst", 0)
            return CHK[name]

        wm3 = d_wm.rearrange("p (g t n) -> p g t n", g=NG, t=KT)
        W3 = d_W.rearrange("p (g t n) -> p g t n", g=NG, t=KT)


        for dst, src in [(idn[:, :], d_idn), (idnf[:, :], d_idnf), (avgp[:, :], d_avgp),
                         (EXPc[:, :], d_EXP), (Q4[:, :], d_Q4), (oddm[:, :], d_oddm),
                         (onesc[:, :], d_onesc), (pairP[:, :], d_pairP),
                         (b_sb[:, :], d_b32), (LiM[:, :], d_LiM), (b8r[:, :], d_b8)]:
            op("sync", (lambda e, a=dst, b=src: e.dma_start(out=a, in_=b)), inc="S_cst")
        CHK["csts"] = ("S_cst", CNT["S_cst"])

        def em_mask(q):
            def f(e):
                return e.dma_start(out=mstage[:, :],
                                   in_=d_msk[:, q * 2048:(q + 1) * 2048],
                                   max_dma_last_dim=8192)
            return f
        op("gpsimd", em_mask(0), inc="G_msk", chk="mq0")
        op("gpsimd", lambda e: e.dma_start(out=wstage[:, :], in_=wm3[:, 0, 0:16]),
           inc="G_wst0", chk="wst0h0")
        for sb_ in range(4):
            op("gpsimd", (lambda e, sb_=sb_: e.dma_start(
                out=Wbuf[:, sb_ * 8 * NGW:(sb_ + 1) * 8 * NGW],
                in_=W3[:, 0, 8 * sb_:8 * (sb_ + 1)])),
                inc=f"G_wbf{sb_}")
        op("gpsimd", lambda e: e.dma_start(out=A_bf[:, :], in_=d_A), inc="G_cmn")
        op("gpsimd", lambda e: e.dma_start(out=W8_bf[:, :], in_=d_W8), inc="G_cmn")
        op("gpsimd", lambda e: e.dma_start(out=Bm_bf[:, :], in_=d_Bm), inc="G_cmn", chk="cmn")

        def emit_mask_batch(q):
            for t in range(4):
                ktg = 4 * q + t

                def em_red(e, t=t, ktg=ktg):
                    with nc.allow_low_precision(reason="mask block-sums <=32 exact in bf16"):
                        return e.tensor_reduce(
                            Sn[:, ktg * 16:(ktg + 1) * 16],
                            mstage[:, t * 512:(t + 1) * 512]
                                .rearrange("p (nb w) -> p nb w", w=32),
                            axis=AX.X, op=ALU.add)
                op("vector", em_red, waits=[wc(f"mq{q}")], inc="V")
            CHK[f"mrq{q}"] = ("V", CNT["V"])
            if q < 7:
                op("gpsimd", em_mask(q + 1), waits=[wc(f"mrq{q}")], inc="G_msk",
                   chk=f"mq{q+1}")

        mask_trigger = {3: 0, 6: 1, 9: 2, 12: 3, 15: 4, 18: 5, 21: 6, 24: 7}
        for kt in range(KT):
            sl = kt % 2
            waits = []
            if kt >= 2:
                waits += [wc(f"sqA{kt-2}"), wc(f"xabV{kt-2}")]
            op("sync", (lambda e, kt=kt, sl=sl: e.dma_start(
                out=xf[:, sl * R:(sl + 1) * R], in_=d_xT[kt * 128:(kt + 1) * 128, :])),
                waits=waits, inc=f"S_xa{sl}", chk=f"xa{kt}")
            waits = [wc(f"xa{kt}")]
            if kt >= 2:
                waits.append(wc(f"stT{kt-2}"))
            op("vector", (lambda e, sl=sl: e.tensor_copy(
                xab[:, sl * R:(sl + 1) * R], xf[:, sl * R:(sl + 1) * R])),
                waits=waits, inc="V", chk=f"xabV{kt}")
            op("scalar", (lambda e, sl=sl: e.activation(
                xsqb[:, sl * R:(sl + 1) * R], xf[:, sl * R:(sl + 1) * R], AF.Square)),
                waits=waits, inc="SC", chk=f"sqA{kt}")
            st_ = (kt == 0)
            sp = (kt == KT - 1)
            wst_ = [wc(f"sqA{kt}"), wc(f"xabV{kt}")]
            if kt == 0:
                wst_.append(wc("csts"))
            op("tensor", (lambda e, sl=sl, st_=st_, sp=sp: e.matmul(
                pm[0][:, :], avgp[:, :], xab[:, sl * R:sl * R + 512], start=st_, stop=sp)),
                waits=wst_)
            op("tensor", (lambda e, sl=sl, st_=st_, sp=sp: e.matmul(
                pm[1][:, :], avgp[:, :], xab[:, sl * R + 512:sl * R + 1024],
                start=st_, stop=sp)))
            op("tensor", (lambda e, sl=sl, st_=st_, sp=sp: e.matmul(
                pm[2][:, :], avgp[:, :], xsqb[:, sl * R:sl * R + 512], start=st_, stop=sp)))
            op("tensor", (lambda e, sl=sl, st_=st_, sp=sp: e.matmul(
                pm[3][:, :], avgp[:, :], xsqb[:, sl * R + 512:sl * R + 1024],
                start=st_, stop=sp)), inc="T", chk=f"stT{kt}")
            if kt in mask_trigger:
                emit_mask_batch(mask_trigger[kt])
            if kt == 26:
                op("tensor", lambda e: e.transpose(pa[:, 0:32], b_sb[:, :],
                                                   idnf[0:32, 0:32]),
                   waits=[wc("csts")], inc="T", chk="bcolT")
                op("tensor", lambda e: e.matmul(pc[:, 0:8], onesc[:, :], b8r[:, :],
                                                start=True, stop=True), inc="T", chk="b8T")
                op("vector", lambda e: e.tensor_copy(b_col[:, :], pa[:, 0:32]),
                   waits=[wc("bcolT")], inc="V", chk="bcolV")
                op("vector", lambda e: e.tensor_copy(b8B[:, :], pc[:, 0:8]),
                   waits=[wc("b8T")], inc="V", chk="b8V")
                for ktb in range(KT):
                    w2 = [wc("mrq7"), wc("b8V")] if ktb == 0 else []
                    op("tensor", (lambda e, ktb=ktb: e.matmul(
                        pc[0:16, 4 * ktb:4 * ktb + 4], Sn[:, ktb * 16:(ktb + 1) * 16],
                        Q4[:, :], start=True, stop=True)), waits=w2,
                        inc=("T" if ktb == KT - 1 else None),
                        chk=(f"b16T{KT-1}" if ktb == KT - 1 else None))
                op("vector", lambda e: e.tensor_scalar(B16[:, :], pc[0:16, 0:128], 0.0,
                                                       None, op0=ALU.is_gt),
                   waits=[wc(f"b16T{KT-1}")], inc="V", chk="bthr")
                op("sync", lambda e: e.dma_start(out=cc_b_in.ap(), in_=B16[:, :]),
                   waits=[wc("bthr")], inc="S_ccb", chk="ccbin")
                if SKIP_CC:
                    def em_ag(e):
                        return e.dma_start(out=cc_b_out.ap()[0:16, :], in_=cc_b_in.ap())
                    op("gpsimd", em_ag, waits=[wc("ccbin")], inc="CC1", chk="ag")
                else:
                    op("gpsimd", lambda e: e.collective_compute(
                        "AllGather", ALU.bypass, replica_groups=[list(range(NCORES))],
                        ins=[cc_b_in.ap().opt()], outs=[cc_b_out.ap().opt()]),
                        waits=[wc("ccbin")], inc="CC1", chk="ag")
                op("gpsimd", lambda e: e.dma_start(out=Ball[:, :], in_=cc_b_out.ap()),
                   waits=[("CC1", 1)], inc="G_cc", chk="ballin")

        op("vector", lambda e: e.tensor_copy(stats[0:1, 0:512], pm[0][0:1, :]),
           waits=[wc(f"stT{KT-1}")], inc="V")
        op("vector", lambda e: e.tensor_copy(stats[0:1, 512:1024], pm[1][0:1, :]), inc="V")
        op("vector", lambda e: e.tensor_copy(stats[0:1, 1024:1536], pm[2][0:1, :]), inc="V")
        op("vector", lambda e: e.tensor_copy(stats[0:1, 1536:2048], pm[3][0:1, :]), inc="V",
           chk="stev")
        op("vector", lambda e: e.tensor_tensor(stats[0:1, 2048:3072], stats[0:1, 0:1024],
                                               stats[0:1, 0:1024], op=ALU.mult), inc="V")
        op("vector", lambda e: e.tensor_tensor(stats[0:1, 1024:2048], stats[0:1, 1024:2048],
                                               stats[0:1, 2048:3072], op=ALU.subtract),
           inc="V")
        op("vector", lambda e: e.tensor_scalar(stats[0:1, 1024:2048], stats[0:1, 1024:2048],
                                               EPS_LN, None, op0=ALU.add), inc="V",
           chk="varV")
        op("scalar", lambda e: e.activation(stats[0:1, 2048:3072], stats[0:1, 1024:2048],
                                            AF.Ln),
           waits=[wc("varV")], inc="SC")
        op("scalar", lambda e: e.activation(stats[0:1, 1024:2048], stats[0:1, 2048:3072],
                                            AF.Exp, scale=-0.5), inc="SC", chk="rstdS")
        op("tensor", lambda e: e.matmul(pa[:, :], onesc[:, :], stats[0:1, 0:512],
                                        start=True, stop=True),
           waits=[wc("stev"), wc("bcolV")], inc="T", chk="bc0")
        op("tensor", lambda e: e.matmul(pb[:, :], onesc[:, :], stats[0:1, 512:1024],
                                        start=True, stop=True), inc="T", chk="bc1")
        op("vector", lambda e: e.tensor_copy(meanB[:, 0:512], pa[:, :]),
           waits=[wc("bc0")], inc="V", chk="bc0V")
        op("vector", lambda e: e.tensor_copy(meanB[:, 512:1024], pb[:, :]),
           waits=[wc("bc1")], inc="V", chk="bc1V")
        op("tensor", lambda e: e.matmul(pa[:, :], onesc[:, :], stats[0:1, 1024:1536],
                                        start=True, stop=True),
           waits=[wc("rstdS"), wc("bc0V")], inc="T", chk="bc2")
        op("tensor", lambda e: e.matmul(pb[:, :], onesc[:, :], stats[0:1, 1536:2048],
                                        start=True, stop=True),
           waits=[wc("bc1V")], inc="T", chk="bc3")
        op("vector", lambda e: e.tensor_copy(rstdB[:, 0:512], pa[:, :]),
           waits=[wc("bc2")], inc="V", chk="bc2V")
        op("vector", lambda e: e.tensor_copy(rstdB[:, 512:1024], pb[:, :]),
           waits=[wc("bc3")], inc="V", chk="bc3V")

        op("tensor", lambda e: e.transpose(pbf[:, 0:128], Ball[:, :], idn[:, :]),
           waits=[wc("ballin")], inc="T", chk="ballT")
        op("vector", lambda e: e.tensor_copy(BallT[:, :], pbf[:, 0:128]),
           waits=[wc("ballT")], inc="V", chk="ballTV")
        for c in range(8):
            bank = pa if c % 2 == 0 else pb
            wdma = [wc("ballTV")]
            if c >= 1:
                wdma.append(wc(f"ekmm{c-1}"))
            op("sync", (lambda e, c=c: e.dma_start(
                out=balltc[:, :], in_=BallT[16 * c:16 * (c + 1), :])),
                waits=wdma, inc="S_ccb", chk=f"btc{c}")
            waits = [wc(f"btc{c}")]
            if c >= 2:
                waits.append(wc(f"ekstrip{c-2}"))
            if c < 2:
                waits.append(wc("bc2V") if c == 0 else wc("bc3V"))
            op("tensor", (lambda e, bank=bank: e.matmul(
                bank[:, :], balltc[:, :], EXPc[:, :],
                start=True, stop=True)), waits=waits, inc="T", chk=f"ekmm{c}")
            wv = [wc(f"ekmm{c}")]
            if c >= 1:
                wv.append(wc(f"ektr{c-1}_3"))
            op("vector", (lambda e, bank=bank: e.tensor_copy(
                ekstrip[:, 0:512], bank[:, :])),
                waits=wv, inc="V", chk=f"ekstrip{c}")
            for t in range(4):
                wt = [wc(f"ekstrip{c}")]
                if c > 0 and t == 0:
                    wt.append(wc(f"ekcol{c-1}"))
                op("tensor", (lambda e, t=t: e.transpose(
                    pbf[:, t * 128:(t + 1) * 128],
                    ekstrip[:, t * 128:(t + 1) * 128], idn[:, :])),
                    waits=wt, inc="T", chk=f"ektr{c}_{t}")
            op("vector", (lambda e, c=c: e.tensor_copy(
                EKcol[:, c * 512:(c + 1) * 512], pbf[:, :])),
                waits=[wc(f"ektr{c}_3")], inc="V", chk=f"ekcol{c}")

        for kt in range(KT):
            sl = kt % 2
            waits = []
            if kt < 2:
                waits += [wc(f"sqA{kt+30}"), wc(f"xabV{kt+30}")]
            else:
                waits.append(wc(f"xn{kt-2}"))
            op("sync", (lambda e, kt=kt, sl=sl: e.dma_start(
                out=xf[:, sl * R:(sl + 1) * R], in_=d_xT[kt * 128:(kt + 1) * 128, :])),
                waits=waits, inc=f"S_xb{sl}", chk=f"xb{kt}")
            wv = [wc(f"xb{kt}"), wc("bc3V"), wc("bc2V")]
            if kt == 0:
                wv.append(wc("mrq7"))
            op("vector", (lambda e, sl=sl: e.tensor_tensor(
                tmpf[:, 0:R], xf[:, sl * R:(sl + 1) * R], meanB[:, :], op=ALU.subtract)),
                waits=wv, inc="V")
            op("vector", (lambda e, kt=kt: e.tensor_tensor(
                xnT[:, kt * R:(kt + 1) * R], tmpf[:, 0:R], rstdB[:, :], op=ALU.mult)),
                inc="V", chk=f"xn{kt}")
            Xh = kt // 16
            op("vector", (lambda e, kt=kt, Xh=Xh: e.tensor_tensor(
                wmbuf[:, kt * NGW:(kt + 1) * NGW].rearrange("p (nb w) -> p nb w", w=32),
                wslot[Xh % 2][:, (kt % 16) * NGW:((kt % 16) + 1) * NGW]
                    .rearrange("p (nb w) -> p nb w", w=32),
                EKcol[:, kt * 128:kt * 128 + 8].unsqueeze(-1).broadcast_to([128, 8, 32]),
                op=ALU.mult)),
                waits=[(f"G_wst{Xh % 2}", 16 * (Xh // 2 + 1)), wc(f"ekcol{kt//4}")],
                inc="V", chk=f"vwm0k{kt}")
            sp = (kt == KT - 1)
            wz = [wc(f"xn{kt}")]
            if kt == 0:
                wz += [wc("cmn"), wc("bc2V"), wc("bc3V"), wc("ekcol7"), wc("bthr")]
            op("tensor", (lambda e, kt=kt, sp=sp: e.matmul(
                pa[0:32, :], A_bf[:, kt * 32:(kt + 1) * 32], xnT[:, kt * R:kt * R + 512],
                start=(kt == 0), stop=sp, skip_group_check=True)), waits=wz)
            op("tensor", (lambda e, kt=kt, sp=sp: e.matmul(
                pb[0:32, :], A_bf[:, kt * 32:(kt + 1) * 32],
                xnT[:, kt * R + 512:kt * R + 1024],
                start=(kt == 0), stop=sp, skip_group_check=True)))
            op("tensor", (lambda e, kt=kt, sp=sp: e.matmul(
                pc[0:8, :], W8_bf[:, kt * 8:(kt + 1) * 8], xnT[:, kt * R:kt * R + 512],
                start=(kt == 0), stop=sp, skip_group_check=True)),
                inc=("T" if sp else None), chk=("zsw" if sp else None))
            for j in (0, 1):
                pmA = pm[2 * j]
                pmB = pm[2 * j + 1]
                wj = [wc(f"vwm0k{kt}")]
                if kt == 0:
                    wj.append(wc("stev"))
                if kt % 8 == 0:
                    wj.append((f"G_wbf{kt//8}", 16))
                wof = kt * NGW + j * 128
                mkchk = (j == 1 and kt % 8 == 7)
                op("tensor", (lambda e, pmA=pmA, wof=wof, kt=kt, st_=(kt == 0): e.matmul(
                    pmA[:, :], wmbuf[:, wof:wof + 128], xnT[:, kt * R:kt * R + 512],
                    start=st_, stop=False, skip_group_check=True)), waits=wj)
                op("tensor", (lambda e, pmB=pmB, wof=wof, kt=kt, st_=(kt == 0): e.matmul(
                    pmB[:, :], wmbuf[:, wof:wof + 128], xnT[:, kt * R + 512:kt * R + 1024],
                    start=st_, stop=False, skip_group_check=True)),
                    inc=("T" if mkchk else None),
                    chk=(f"tWMj1k{kt}" if mkchk else None))
                op("tensor", (lambda e, pmA=pmA, wof=wof, kt=kt: e.matmul(
                    pmA[:, :], Wbuf[:, wof:wof + 128], xnT[:, kt * R:kt * R + 512],
                    start=False, stop=False, skip_group_check=True)))
                op("tensor", (lambda e, pmB=pmB, wof=wof, kt=kt: e.matmul(
                    pmB[:, :], Wbuf[:, wof:wof + 128], xnT[:, kt * R + 512:kt * R + 1024],
                    start=False, stop=False, skip_group_check=True)),
                    inc=("T" if mkchk else None),
                    chk=(f"tWj1k{kt}" if mkchk else None))

        op("gpsimd", lambda e: e.dma_start(out=xw2[:, :], in_=wm3[:, 0, 16:32]),
           waits=[wc(f"stT{KT-1}")], inc="G_wst1", chk="wst0h1")
        op("gpsimd", lambda e: e.dma_start(out=wstage[:, :], in_=wm3[:, 1, 0:16]),
           waits=[wc("vwm0k15")], inc="G_wst0", chk="wst1h0")
        op("gpsimd", lambda e: e.dma_start(out=xw2[:, :], in_=wm3[:, 1, 16:32]),
           waits=[wc(f"vwm0k{KT-1}")], inc="G_wst1", chk="wst1h1")

        op("vector", lambda e: e.tensor_copy(zTb[:, 0:512], pa[0:32, :]),
           waits=[wc("zsw")], inc="V", chk="zloV")
        op("vector", lambda e: e.tensor_copy(zTb[:, 512:1024], pb[0:32, :]), inc="V",
           chk="zhiV")
        op("vector", lambda e: e.tensor_copy(H8T[:, 0:512], pc[0:8, :]), inc="V", chk="zevV")
        CHK["zevS"] = CHK["zevV"]
        for kt in range(KT):
            sp = (kt == KT - 1)
            waits = [wc("zevV")] if kt == 0 else []
            op("tensor", (lambda e, kt=kt, sp=sp: e.matmul(
                pc[0:8, :], W8_bf[:, kt * 8:(kt + 1) * 8],
                xnT[:, kt * R + 512:kt * R + 1024],
                start=(kt == 0), stop=sp, skip_group_check=True)),
                waits=waits, inc=("T" if sp else None), chk=("h8hi" if sp else None))
        op("vector", lambda e: e.tensor_copy(H8T[:, 512:1024], pc[0:8, :]),
           waits=[wc("h8hi")], inc="V", chk="h8hiV")

        op("scalar", lambda e: e.activation(LiM_bf[:, :], LiM[:, :], AF.Identity),
           waits=[wc("csts")], inc="SC", chk="limS")
        op("scalar", lambda e: e.activation(LiMs_bf[:, :], LiM[:, :], AF.Identity,
                                            scale=SCALING), inc="SC", chk="limsS")
        op("tensor", lambda e: e.transpose(pbf[0:32, 0:8], LiM_bf[:, :], idn[0:8, 0:8]),
           waits=[wc("limS"), wc("ekcol7")], inc="T", chk="limT")
        op("vector", lambda e: e.tensor_copy(LiMT_bf[:, :], pbf[0:32, 0:8]),
           waits=[wc("limT")], inc="V", chk="limV")

        op("tensor", lambda e: e.matmul(pa[0:8, :], LiMT_bf[:, :], zTb[:, 0:512],
                                        start=True, stop=True),
           waits=[wc("zevV"), wc("zevS"), wc("limV")], inc="T", chk="ds0")
        op("tensor", lambda e: e.matmul(pb[0:8, :], LiMT_bf[:, :], zTb[:, 512:1024],
                                        start=True, stop=True), inc="T", chk="ds1")
        op("vector", lambda e: e.tensor_copy(dsT[:, 0:512], pa[0:8, :]),
           waits=[wc("ds0")], inc="V")
        op("vector", lambda e: e.tensor_copy(dsT[:, 512:1024], pb[0:8, :]),
           waits=[wc("ds1")], inc="V", chk="dsV")

        for mt in range(8):
            waits = [wc("dsV"), wc("h8hiV")] if mt == 0 else []
            op("tensor", (lambda e, mt=mt: e.transpose(
                pa[:, mt * 8:(mt + 1) * 8], H8T[0:8, mt * 128:(mt + 1) * 128],
                idnf[0:8, 0:8])), waits=waits, inc="T")
            op("tensor", (lambda e, mt=mt: e.transpose(
                pb[:, mt * 8:(mt + 1) * 8], dsT[0:8, mt * 128:(mt + 1) * 128],
                idnf[0:8, 0:8])), inc="T", chk=("tokT" if mt == 7 else None))
        op("vector", lambda e: e.tensor_copy(H8tok[:, :], pa[:, 0:64]),
           waits=[wc("tokT")], inc="V")
        op("vector", lambda e: e.tensor_copy(dstok[:, :], pb[:, 0:64]), inc="V", chk="tokV")
        op("vector", lambda e: e.tensor_tensor(
            H8tok[:, :].rearrange("p (mt e) -> p mt e", e=8),
            H8tok[:, :].rearrange("p (mt e) -> p mt e", e=8),
            b8B[:, :].unsqueeze(1).broadcast_to([128, 8, 8]), op=ALU.add),
            waits=[wc("b8V")], inc="V", chk="h8bias")

        op("vector", lambda e: e.reduce_max(gat[:, 0:1], H8tok[:, :], axis=AX.X,
                                            apply_absolute_value=True),
           waits=[wc("h8bias")], inc="V")
        op("vector", lambda e: e.reduce_max(gat[:, 1:2], dstok[:, :], axis=AX.X,
                                            apply_absolute_value=True),
           waits=[wc("tokV")], inc="V")
        op("vector", lambda e: e.tensor_tensor(gat[:, 0:2], gat[:, 0:2],
                                               oddm[:, :].broadcast_to([128, 2]),
                                               op=ALU.mult),
           waits=[wc("csts")], inc="V", chk="rmax")
        op("tensor", lambda e: e.transpose(pc[0:2, 0:128], gat[:, 0:2], idnf[:, :]),
           waits=[wc("rmax"), wc("bthr")], inc="T", chk="mxT")
        op("vector", lambda e: e.reduce_max(gat[0:2, 8:9], pc[0:2, 0:128], axis=AX.X),
           waits=["mxT"], inc="V", chk="mxV")
        op("sync", lambda e: e.dma_start(out=cc_m_in.ap(), in_=gat[0:2, 8:9]),
           waits=[wc("mxV")], inc="S_ccm", chk="ccmin")
        if SKIP_CC:
            op("gpsimd", lambda e: e.dma_start(out=cc_m_out.ap(), in_=cc_m_in.ap()),
               waits=[wc("ccmin")], inc="CC2", chk="ar")
        else:
            op("gpsimd", lambda e: e.collective_compute(
                "AllReduce", ALU.max, replica_groups=[list(range(NCORES))],
                ins=[cc_m_in.ap().opt()], outs=[cc_m_out.ap().opt()]),
                waits=[wc("ccmin")], inc="CC2", chk="ar")
        op("gpsimd", lambda e: e.dma_start(out=gmax[0:1, 0:2], in_=cc_m_out.ap()),
           waits=[("CC2", 1)], inc="G_cc2", chk="gmaxin")

        op("vector", lambda e: e.tensor_scalar(gmax[0:1, 2:4], gmax[0:1, 0:2], EPS_G, None,
                                               op0=ALU.max),
           waits=[wc("gmaxin")], inc="V")
        op("vector", lambda e: e.reciprocal(gmax[0:1, 4:6], gmax[0:1, 2:4]), inc="V")
        op("vector", lambda e: e.tensor_scalar(gmax[0:1, 6:8], gmax[0:1, 4:6], GAMMA, None,
                                               op0=ALU.mult), inc="V", chk="scV")
        op("tensor", lambda e: e.matmul(pc[:, 0:2], onesc[:, :], gmax[0:1, 6:8],
                                        start=True, stop=True),
           waits=[wc("scV")], inc="T", chk="scT")
        op("vector", lambda e: e.tensor_copy(scB[:, :], pc[:, 0:2]),
           waits=["scT"], inc="V", chk="scBV")

        GA = gat

        def V3(ap):
            return ap.rearrange("p (mt e) -> p mt e", e=8)

        def bc8(ap):
            return ap.unsqueeze(-1).broadcast_to([128, 8, 8])

        op("vector", lambda e: e.tensor_scalar(GA[:, 64:128], H8tok[:, :], scB[:, 0:1], None,
                                               op0=ALU.mult),
           waits=[wc("scBV")], inc="V")
        op("vector", lambda e: e.scalar_tensor_tensor(GA[:, 128:192], dstok[:, :],
                                                      scB[:, 1:2], GA[:, 64:128],
                                                      op0=ALU.mult, op1=ALU.add),
           inc="V", chk="logit")
        op("scalar", lambda e: e.activation(GA[:, 192:256], GA[:, 128:192], AF.Exp),
           waits=[wc("logit")], inc="SC", chk="expS")
        op("vector", lambda e: e.tensor_reduce(GA[:, 256:264], V3(GA[:, 192:256]),
                                               axis=AX.X, op=ALU.add),
           waits=[wc("expS")], inc="V")
        op("vector", lambda e: e.reciprocal(GA[:, 336:344], GA[:, 256:264]), inc="V")
        op("vector", lambda e: e.tensor_tensor(V3(GA[:, 320:384]), V3(GA[:, 192:256]),
                                               bc8(GA[:, 336:344]), op=ALU.mult),
           inc="V", chk="probs")
        op("vector", lambda e: e.tensor_reduce(GA[:, 272:280], V3(GA[:, 320:384]),
                                               axis=AX.X, op=ALU.max), inc="V")
        op("vector", lambda e: e.tensor_tensor(V3(GA[:, 384:448]), V3(GA[:, 320:384]),
                                               bc8(GA[:, 272:280]), op=ALU.is_ge), inc="V")
        op("vector", lambda e: e.tensor_reduce(GA[:, 264:272], V3(GA[:, 384:448]),
                                               axis=AX.X, op=ALU.add), inc="V")
        op("vector", lambda e: e.scalar_tensor_tensor(GA[:, 448:512], GA[:, 384:448], -1e30,
                                                      GA[:, 320:384], op0=ALU.mult,
                                                      op1=ALU.add), inc="V")
        op("vector", lambda e: e.tensor_reduce(GA[:, 280:288], V3(GA[:, 448:512]),
                                               axis=AX.X, op=ALU.max), inc="V")
        op("vector", lambda e: e.tensor_scalar(GA[:, 288:296], GA[:, 264:272], 2.0, None,
                                               op0=ALU.is_ge), inc="V")
        op("vector", lambda e: e.tensor_tensor(GA[:, 296:304], GA[:, 272:280],
                                               GA[:, 280:288], op=ALU.subtract), inc="V")
        op("vector", lambda e: e.tensor_tensor(GA[:, 296:304], GA[:, 296:304],
                                               GA[:, 288:296], op=ALU.mult), inc="V")
        op("vector", lambda e: e.tensor_tensor(GA[:, 304:312], GA[:, 296:304],
                                               GA[:, 280:288], op=ALU.add), inc="V",
           chk="thr")
        op("vector", lambda e: e.tensor_tensor(V3(GA[:, 512:576]), bc8(GA[:, 304:312]),
                                               V3(GA[:, 320:384]), op=ALU.subtract),
           inc="V", chk="preV")
        op("scalar", lambda e: e.activation(GA[:, 576:640], GA[:, 512:576], AF.Exp,
                                            scale=2.0),
           waits=[wc("preV"), wc("ekcol7")], inc="SC", chk="qS")
        op("vector", lambda e: e.tensor_scalar(GA[:, 640:704], GA[:, 576:640], 1.0, None,
                                               op0=ALU.add),
           waits=[wc("qS")], inc="V")
        op("vector", lambda e: e.reciprocal(GA[:, 704:768], GA[:, 640:704]), inc="V")
        op("vector", lambda e: e.tensor_tensor(GA[:, 768:832], GA[:, 320:384],
                                               GA[:, 704:768], op=ALU.mult), inc="V")
        op("vector", lambda e: e.tensor_reduce(GA[:, 312:320], V3(GA[:, 768:832]),
                                               axis=AX.X, op=ALU.add), inc="V")
        op("vector", lambda e: e.tensor_scalar(GA[:, 312:320], GA[:, 312:320], 1e-9, None,
                                               op0=ALU.add), inc="V")
        op("vector", lambda e: e.reciprocal(GA[:, 328:336], GA[:, 312:320]), inc="V")
        op("vector", lambda e: e.tensor_tensor(V3(wnb[:, :]), V3(GA[:, 768:832]),
                                               bc8(GA[:, 328:336]), op=ALU.mult),
           inc="V", chk="wnV")

        ev_cnt = {}
        out_cnt = {}

        def plan_main_block(j):
            g = j // 2
            par = j % 2
            pmA = pm[2 * par]
            pmB = pm[2 * par + 1]
            gbase = (g % 2) * (KT * NGW)
            for kt in range(KT):
                waits = [wc(f"vwm{g}k{kt}")]
                if kt == 0:
                    waits.append(("V", ev_cnt[j - 2]))
                if kt % 8 == 0:
                    waits.append((f"G_wbf{kt//8}", 16 * (g + 1)))
                wof = gbase + kt * NGW + par * 128
                wofW = kt * NGW + par * 128
                mkchk = (par == 1 and kt % 8 == 7)
                op("tensor", (lambda e, pmA=pmA, wof=wof, kt=kt, st_=(kt == 0): e.matmul(
                    pmA[:, :], wmbuf[:, wof:wof + 128], xnT[:, kt * R:kt * R + 512],
                    start=st_, stop=False, skip_group_check=True)), waits=waits)
                op("tensor", (lambda e, pmB=pmB, wof=wof, kt=kt, st_=(kt == 0): e.matmul(
                    pmB[:, :], wmbuf[:, wof:wof + 128], xnT[:, kt * R + 512:kt * R + 1024],
                    start=st_, stop=False, skip_group_check=True)),
                    inc=("T" if mkchk else None),
                    chk=(f"tWMj{j}k{kt}" if mkchk else None))
                op("tensor", (lambda e, pmA=pmA, wofW=wofW, kt=kt: e.matmul(
                    pmA[:, :], Wbuf[:, wofW:wofW + 128], xnT[:, kt * R:kt * R + 512],
                    start=False, stop=False, skip_group_check=True)))
                op("tensor", (lambda e, pmB=pmB, wofW=wofW, kt=kt: e.matmul(
                    pmB[:, :], Wbuf[:, wofW:wofW + 128], xnT[:, kt * R + 512:kt * R + 1024],
                    start=False, stop=False, skip_group_check=True)),
                    inc=("T" if mkchk else None),
                    chk=(f"tWj{j}k{kt}" if mkchk else None))

        def plan_delta_and_evac(j):
            par = j % 2
            pmA = pm[2 * par]
            pmB = pm[2 * par + 1]
            op("tensor", (lambda e, pmA=pmA, j=j: e.matmul(
                pmA[:, :], Bm_bf[:, j * 128:(j + 1) * 128], mulT[:, 0:512],
                start=False, stop=True, skip_group_check=True)),
                waits=[wc("vmul")], inc="T")
            op("tensor", (lambda e, pmB=pmB, j=j: e.matmul(
                pmB[:, :], Bm_bf[:, j * 128:(j + 1) * 128], mulT[:, 512:1024],
                start=False, stop=True, skip_group_check=True)), inc="T", chk=f"tDj{j}")
            rsl = j % 2
            wres = [wc(f"vwm0k{KT-1}")] if j < 2 else [("V", ev_cnt[j - 2])]
            if j < 2:
                wres.append(wc(f"xn{KT-1}"))
            op("sync", (lambda e, j=j, rsl=rsl: e.dma_start(
                out=xf[:, rsl * R:(rsl + 1) * R], in_=d_xT[j * 128:(j + 1) * 128, :])),
                waits=wres, inc=f"S_res{rsl}", chk=f"res{j}")
            for h in range(2):
                bank = pmA if h == 0 else pmB
                osl = (2 * j + h) % 2
                waits = [wc(f"tDj{j}"), wc(f"res{j}")]
                if 2 * j + h >= 2:
                    prev = 2 * j + h - 2
                    waits.append(out_cnt[(prev // 2, prev % 2)])
                op("vector", (lambda e, bank=bank, osl=osl, rsl=rsl, j=j, h=h:
                    e.scalar_tensor_tensor(
                        outstg[:, osl * 512:(osl + 1) * 512], bank[:, :], b_col[:, j:j + 1],
                        xf[:, rsl * R + h * 512:rsl * R + (h + 1) * 512],
                        op0=ALU.add, op1=ALU.add)),
                    waits=waits, inc="V", chk=f"ev{j}_{h}")
                op("scalar", (lambda e, j=j, h=h, osl=osl: e.dma_start(
                    out=d_out[j * 128:(j + 1) * 128, h * 512:(h + 1) * 512],
                    in_=outstg[:, osl * 512:(osl + 1) * 512])),
                    waits=[wc(f"ev{j}_{h}")], inc=f"SCQ{osl}")
                out_cnt[(j, h)] = (f"SCQ{osl}", CNT[f"SCQ{osl}"])
            ev_cnt[j] = CHK[f"ev{j}_1"][1]

        def plan_vwm(g):
            base = (g % 2) * (KT * NGW)
            for kt in range(KT):
                X = 2 * g + kt // 16
                waits = [(f"G_wst{X % 2}", 16 * (X // 2 + 1))]
                if g >= 2:
                    waits.append(wc(f"tWMj{2*g-3}k{8*(kt//8)+7}"))
                op("vector", (lambda e, g=g, kt=kt, base=base, X=X: e.tensor_tensor(
                    wmbuf[:, base + kt * NGW:base + (kt + 1) * NGW]
                        .rearrange("p (nb w) -> p nb w", w=32),
                    wslot[X % 2][:, (kt % 16) * NGW:((kt % 16) + 1) * NGW]
                        .rearrange("p (nb w) -> p nb w", w=32),
                    EKcol[:, kt * 128 + 8 * g:kt * 128 + 8 * g + 8]
                        .unsqueeze(-1).broadcast_to([128, 8, 32]),
                    op=ALU.mult)),
                    waits=waits, inc="V", chk=f"vwm{g}k{kt}")

        for mt in range(8):
            waits = [wc("wnV")] if mt == 0 else []
            op("tensor", (lambda e, mt=mt: e.matmul(
                pa[:, mt * 8:(mt + 1) * 8], pairP[:, :], wnb[:, mt * 8:(mt + 1) * 8],
                start=True, stop=True)), waits=waits,
                inc=("T" if mt == 7 else None), chk=("wf8" if mt == 7 else None))
        op("vector", lambda e: e.tensor_copy(wfb[:, :], pa[:, 0:64]),
           waits=[wc("wf8")], inc="V", chk="wfbV")
        for rnd in range(2):
            for t in range(4):
                mt = rnd * 4 + t
                waits = [wc("wfbV")]
                if rnd == 1 and t == 0:
                    waits.append(wc("wfT0V"))
                op("tensor", (lambda e, mt=mt, t=t: e.transpose(
                    pbf[0:8, t * 128:(t + 1) * 128], wfb[:, mt * 8:(mt + 1) * 8],
                    idn[:, :])),
                    waits=waits, inc=("T" if t == 3 else None),
                    chk=(f"wfTr{rnd}" if t == 3 else None))
            op("vector", (lambda e, rnd=rnd: e.tensor_copy(
                wfT[0:8, rnd * 512:(rnd + 1) * 512], pbf[0:8, 0:512])),
                waits=[wc(f"wfTr{rnd}")], inc="V", chk=f"wfT{rnd}V")
        op("tensor", lambda e: e.matmul(pa[0:32, :], LiMs_bf[:, :], wfT[0:8, 0:512],
                                        start=True, stop=True),
           waits=[wc("wfT1V"), wc("limsS")], inc="T", chk="pmx0")
        op("tensor", lambda e: e.matmul(pb[0:32, :], LiMs_bf[:, :], wfT[0:8, 512:1024],
                                        start=True, stop=True), inc="T", chk="pmx1")
        op("vector", lambda e: e.tensor_tensor(mulT[:, 0:512], pa[0:32, :], zTb[:, 0:512],
                                               op=ALU.mult),
           waits=[wc("pmx0")], inc="V")
        op("vector", lambda e: e.tensor_tensor(mulT[:, 512:1024], pb[0:32, :],
                                               zTb[:, 512:1024], op=ALU.mult),
           waits=[wc("pmx1")], inc="V", chk="vmul")

        plan_vwm(1)
        plan_delta_and_evac(0)
        plan_delta_and_evac(1)
        plan_vwm(2)
        nb_eff = NB if NB_EFF is None else NB_EFF
        for j in range(2, nb_eff):
            plan_main_block(j)
            plan_delta_and_evac(j)
            if j % 2 == 1 and (j + 3) // 2 < NG:
                plan_vwm((j + 3) // 2)

        for g in range(1, (NB if NB_EFF is None else NB_EFF) // 2):
            for sb_ in range(4):
                op("gpsimd", (lambda e, g=g, sb_=sb_: e.dma_start(
                    out=Wbuf[:, sb_ * 8 * NGW:(sb_ + 1) * 8 * NGW],
                    in_=W3[:, g, 8 * sb_:8 * (sb_ + 1)])),
                    waits=[wc(f"tWj{2*g-1}k{8*sb_+7}")], inc=f"G_wbf{sb_}")
            if g + 1 < NG:
                op("gpsimd", (lambda e, g=g: e.dma_start(
                    out=wslot[0][:, :], in_=wm3[:, g + 1, 0:16])),
                   waits=[wc(f"vwm{g}k15")], inc="G_wst0", chk=f"wst{g+1}h0")
                op("gpsimd", (lambda e, g=g: e.dma_start(
                    out=wslot[1][:, :], in_=wm3[:, g + 1, 16:32])),
                   waits=[wc(f"vwm{g}k{KT-1}")], inc="G_wst1", chk=f"wst{g+1}h1")

        if debug_taps:
            tap_list = [
                ("t_xnT", xnT[:, :], [wc(f"xn{KT-1}")]),
                ("t_EKcol", EKcol[:, :], [wc("ekcol7")]),
                ("t_Ball", Ball[:, :], [wc("ballin")]),
                ("t_H8tok", H8tok[:, :], [wc("h8bias")]),
                ("t_dstok", dstok[:, :], [wc("tokV")]),
                ("t_wnb", wnb[:, :], [wc("wnV")]),
                ("t_mulT", mulT[:, :], [wc("vmul")]),
                ("t_meanB", meanB[:, :], [wc("bc1V")]),
                ("t_rstdB", rstdB[:, :], [wc("bc3V")]),
                ("t_gmax", gmax[0:1, 0:8], [wc("scV")]),
                ("t_Sn", Sn[:, :], [wc("mrq7")]),
                ("t_zT", zTb[:, :], [wc("zhiV")]),
                ("t_H8T", H8T[:, :], [wc("zevV")]),
            ]
            for nm, src, waits in tap_list:
                op("sync", (lambda e, nm=nm, src=src: e.dma_start(out=taps[nm], in_=src)),
                   waits=waits, inc="S_cst")

        PLAN["scalar"].append((None, [("SCQ0", CNT["SCQ0"]), ("SCQ1", CNT["SCQ1"])], None))
        PLAN["sync"].append((None, [("S_cst", CNT["S_cst"])], None))

        def make_runner(eng_name):
            own_sem = {"vector": "V", "scalar": "SC"}.get(eng_name)
            def run(engine):
                seen = {}
                own = 0
                for emit, waits, inc in PLAN[eng_name]:
                    if own_sem is not None and emit is not None and own > 0:
                        if seen.get(own_sem, 0) < own:
                            engine.wait_ge(SEM[own_sem], own)
                            seen[own_sem] = own
                    for wspec in waits:
                        if isinstance(wspec, str):
                            if wspec not in CHK and STOP_N is not None:
                                continue
                            semn, val = CHK[wspec]
                        else:
                            semn, val = wspec
                        if seen.get(semn, 0) < val:
                            engine.wait_ge(SEM[semn], val)
                            seen[semn] = val
                    if emit is None:
                        continue
                    inst = emit(engine)
                    if inc == own_sem:
                        own += 1
                    if inc is not None:
                        if inc in CC_SEMS:
                            if SKIP_CC:
                                inst.then_inc(SEM[inc], 16)
                            else:
                                inst.then_inc(SEM[inc])
                        elif inc in DMA_SEMS:
                            inst.then_inc(SEM[inc], 16)
                        else:
                            inst.then_inc(SEM[inc], 1)
            return run

        block.sync(make_runner("sync"))
        block.gpsimd(make_runner("gpsimd"))
        block.scalar(make_runner("scalar"))
        block.tensor(make_runner("tensor"))
        block.vector(make_runner("vector"))

    return nc


def _consts():
    idn = np.eye(128, dtype=ml_dtypes.bfloat16)
    idnf = np.eye(128, dtype=np.float32)
    avgp = np.zeros((128, 128), dtype=ml_dtypes.bfloat16)
    avgp[:, 0] = ml_dtypes.bfloat16(1.0 / 4096.0)
    EXP = np.zeros((16, 512), dtype=ml_dtypes.bfloat16)
    for kb in range(16):
        EXP[kb, kb * 32:(kb + 1) * 32] = 1
    Q4 = np.zeros((128, 4), dtype=ml_dtypes.bfloat16)
    for p in range(128):
        Q4[p, p // 32] = 1
    oddm = np.zeros((128, 1), np.float32)
    oddm[1::2, 0] = 1.0
    onesc = np.ones((1, 128), np.float32)
    pairP = np.zeros((128, 128), dtype=ml_dtypes.bfloat16)
    for m in range(128):
        pairP[m | 1, m] = 1
    return idn, idnf, avgp, EXP, Q4, oddm, onesc, pairP


def _tile_khost(M):
    K, C = M.shape
    return np.ascontiguousarray(M.reshape(32, 128, C).transpose(1, 0, 2).reshape(128, 32 * C))


def _tile_weight(Wmat):
    arr = Wmat.reshape(NG, NGW, KT, 128)
    return np.ascontiguousarray(arr.transpose(3, 0, 2, 1).reshape(128, NG * KT * NGW))


def kernel(L0_attn_data, L0_attn_weight, L0_attn_mask, L0_ffn_W, L0_ffn_b,
           L0_ffn_A, L0_ffn_Bmat, L0_ffn_LiMEs):
    X = np.ascontiguousarray(np.asarray(L0_attn_data, np.float32).reshape(2 * 4096, D))
    weight = np.asarray(L0_attn_weight, np.float32)
    mask = np.asarray(L0_attn_mask, np.int32)
    W = np.asarray(L0_ffn_W, np.float32)
    bvec = np.asarray(L0_ffn_b, np.float32)
    A = np.asarray(L0_ffn_A, np.float32)
    Bmat = np.ascontiguousarray(np.asarray(L0_ffn_Bmat, np.float32))
    LiMa = np.ascontiguousarray(np.asarray(L0_ffn_LiMEs, np.float32))

    idn, idnf, avgp, EXP, Q4, oddm, onesc, pairP = _consts()

    wmh = _tile_weight(weight)
    Wh = _tile_weight(W)
    Ah = _tile_khost(A)
    W8h = _tile_khost(np.ascontiguousarray(W[0:8, :].T))
    b32 = np.ascontiguousarray(bvec.reshape(32, 128))
    b8 = np.ascontiguousarray(bvec[0:8].reshape(1, 8))
    maskT = np.ascontiguousarray(mask.T)

    nc = build_nc(debug_taps=DEBUG_TAPS)

    in_maps = []
    for c in range(NCORES):
        xT_c = np.ascontiguousarray(X[c * R:(c + 1) * R, :].T)
        msk_c = _tile_khost(np.ascontiguousarray(maskT[:, c * 512:(c + 1) * 512]))
        in_maps.append({
            "xT": xT_c, "wmh": wmh, "Wh": Wh, "mskh": msk_c,
            "Ah": Ah, "W8h": W8h, "b32": b32, "b8": b8, "Bm": Bmat, "LiM": LiMa,
            "idn": idn, "idnf": idnf, "avgp": avgp, "EXPc": EXP, "Q4c": Q4,
            "oddm": oddm, "onesc": onesc, "pairP": pairP,
        })

    res = run_bass_kernel_spmd(nc, in_maps, core_ids=list(range(NCORES)))
    out = np.empty((2 * 4096, D), np.float32)
    for c in range(NCORES):
        out[c * R:(c + 1) * R, :] = np.asarray(res.results[c]["outT"]).T
    if DEBUG_TAPS:
        kernel.last_results = res
    return out.reshape(2, 4096, D)


# version 04
# speedup vs baseline: 1.2587x; 1.0257x over previous
import numpy as np
import ml_dtypes
import concourse.bass as bass
import concourse.mybir as mybir
from concourse.bass_utils import run_bass_kernel_spmd

F32 = mybir.dt.float32
BF16 = mybir.dt.bfloat16
I32 = mybir.dt.int32
AF = mybir.ActivationFunctionType
ALU = mybir.AluOpType
AX = mybir.AxisListType

NCORES = 8
D = 4096
R = 1024
KT = 32
NGW = 256
NG = D // NGW
NB = 32
EPS_LN = 1e-5
EPS_G = 1e-6
GAMMA = 0.5
SCALING = 16.0 / 32.0

DEBUG_TAPS = False
SKIP_CC = False
NB_EFF = None
PHASE1_ONLY = False
STOP_N = None

ENGS = ("sync", "gpsimd", "scalar", "tensor", "vector")


def build_nc(debug_taps=False):
    nc = bass.Bass("TRN2", target_bir_lowering=False, debug=False,
                   dynamic_dma_scratch_size=16384)

    d_xT = nc.dram_tensor("xT", [D, R], F32, kind="ExternalInput").ap()
    d_wm = nc.dram_tensor("wmh", [128, KT * D], F32, kind="ExternalInput").ap()
    d_W = nc.dram_tensor("Wh", [128, KT * D], F32, kind="ExternalInput").ap()
    d_msk = nc.dram_tensor("mskh", [128, KT * 512], I32, kind="ExternalInput").ap()
    d_A = nc.dram_tensor("Ah", [128, KT * 32], F32, kind="ExternalInput").ap()
    d_W8 = nc.dram_tensor("W8h", [128, KT * 8], F32, kind="ExternalInput").ap()
    d_b32 = nc.dram_tensor("b32", [32, 128], F32, kind="ExternalInput").ap()
    d_b8 = nc.dram_tensor("b8", [1, 8], F32, kind="ExternalInput").ap()
    d_Bm = nc.dram_tensor("Bm", [32, D], F32, kind="ExternalInput").ap()
    d_LiM = nc.dram_tensor("LiM", [8, 32], F32, kind="ExternalInput").ap()
    d_idn = nc.dram_tensor("idn", [128, 128], BF16, kind="ExternalInput").ap()
    d_idnf = nc.dram_tensor("idnf", [128, 128], F32, kind="ExternalInput").ap()
    d_avgp = nc.dram_tensor("avgp", [128, 128], BF16, kind="ExternalInput").ap()
    d_EXP = nc.dram_tensor("EXPc", [16, 512], BF16, kind="ExternalInput").ap()
    d_Q4 = nc.dram_tensor("Q4c", [128, 4], BF16, kind="ExternalInput").ap()
    d_oddm = nc.dram_tensor("oddm", [128, 1], F32, kind="ExternalInput").ap()
    d_onesc = nc.dram_tensor("onesc", [1, 128], F32, kind="ExternalInput").ap()
    d_pairP = nc.dram_tensor("pairP", [128, 128], BF16, kind="ExternalInput").ap()

    d_out = nc.dram_tensor("outT", [D, R], F32, kind="ExternalOutput").ap()

    cc_b_in = nc.dram_tensor("cc_b_in", [16, 128], BF16)
    cc_b_out = nc.dram_tensor("cc_b_out", [128, 128], BF16)
    cc_m_in = nc.dram_tensor("cc_m_in", [1, 2], F32)
    cc_m_out = nc.dram_tensor("cc_m_out", [1, 2], F32)
    cc_d_in = nc.dram_tensor("cc_d_in", [1, 1], F32)
    cc_d_out = nc.dram_tensor("cc_d_out", [8, 1], F32)

    taps = {}
    if debug_taps:
        for nm, shp, dt in [("t_xnT", [128, KT * R], BF16), ("t_EKcol", [128, KT * 128], BF16),
                            ("t_Ball", [128, 128], BF16), ("t_H8tok", [128, 64], F32),
                            ("t_dstok", [128, 64], F32), ("t_wnb", [128, 64], BF16),
                            ("t_mulT", [32, R], BF16), ("t_gmax", [1, 8], F32),
                            ("t_Sn", [128, 512], BF16), ("t_zT", [32, R], BF16)]:
            taps[nm] = nc.dram_tensor(nm, shp, dt, kind="ExternalOutput").ap()

    from contextlib import ExitStack
    with ExitStack() as st:
        E = st.enter_context

        xnT = E(nc.sbuf_tensor("xnT", [128, KT * R], BF16))
        wstage = E(nc.sbuf_tensor("wstage", [128, 16 * NGW], BF16))
        wmbuf = E(nc.sbuf_tensor("wmbuf", [128, 2 * KT * NGW], BF16))
        Wbuf = E(nc.sbuf_tensor("Wbuf", [128, KT * NGW], BF16))
        mstage = E(nc.sbuf_tensor("mstage", [128, 8 * 512], BF16))
        EKcol = E(nc.sbuf_tensor("EKcol", [128, KT * 128], BF16))
        xf = E(nc.sbuf_tensor("xf", [128, 4 * R], F32))

        rstdB = E(nc.sbuf_tensor("rstdB", [128, R], F32))
        stats = E(nc.sbuf_tensor("stats", [1, 3072], F32))
        outstg = E(nc.sbuf_tensor("outstg", [128, 1024], F32))
        xw2 = E(nc.sbuf_tensor("xw2", [128, 16 * NGW], BF16))
        xab = xw2[:, :]
        xsqb = None
        gat = E(nc.sbuf_tensor("gat", [128, 832], F32))
        Sn = E(nc.sbuf_tensor("Sn", [128, 512], BF16))
        B16 = E(nc.sbuf_tensor("B16", [16, 128], BF16))
        Ball = E(nc.sbuf_tensor("Ball", [128, 128], BF16))
        BallT = E(nc.sbuf_tensor("BallT", [128, 128], BF16))
        zTb = E(nc.sbuf_tensor("zTb", [32, R], BF16))
        H8tok = E(nc.sbuf_tensor("H8tok", [128, 64], F32))
        dstok = E(nc.sbuf_tensor("dstok", [128, 64], F32))
        wnb = E(nc.sbuf_tensor("wnb", [128, 64], BF16))
        wfb = E(nc.sbuf_tensor("wfb", [128, 64], BF16))
        mulT = E(nc.sbuf_tensor("mulT", [32, R], BF16))
        A_bf = E(nc.sbuf_tensor("A_bf", [128, KT * 32], BF16))
        W8_bf = E(nc.sbuf_tensor("W8_bf", [128, KT * 8], BF16))
        Bm_bf = E(nc.sbuf_tensor("Bm_bf", [32, D], BF16))
        b_sb = E(nc.sbuf_tensor("b_sb", [32, 128], F32))
        b_col = E(nc.sbuf_tensor("b_col", [128, 32], F32))
        b8B = E(nc.sbuf_tensor("b8B", [128, 8], F32))
        b8r = E(nc.sbuf_tensor("b8r", [1, 8], F32))
        LiM = E(nc.sbuf_tensor("LiMs", [8, 32], F32))
        LiM_bf = E(nc.sbuf_tensor("LiM_bf", [8, 32], BF16))
        LiMs_bf = E(nc.sbuf_tensor("LiMsc_bf", [8, 32], BF16))
        LiMT_bf = E(nc.sbuf_tensor("LiMT_bf", [32, 8], BF16))
        idn = E(nc.sbuf_tensor("idn_sb", [128, 128], BF16))
        idnf = E(nc.sbuf_tensor("idnf_sb", [128, 128], F32))
        avgp = E(nc.sbuf_tensor("avgp_sb", [128, 128], BF16))
        EXPc = E(nc.sbuf_tensor("EXP_sb", [16, 512], BF16))
        balltc = E(nc.sbuf_tensor("balltc", [16, 128], BF16))
        Q4 = E(nc.sbuf_tensor("Q4_sb", [128, 4], BF16))
        oddm = E(nc.sbuf_tensor("oddm_sb", [128, 1], F32))
        onesc = E(nc.sbuf_tensor("onesc_sb", [1, 128], F32))
        pairP = E(nc.sbuf_tensor("pairP_sb", [128, 128], BF16))
        gmax = E(nc.sbuf_tensor("gmax", [1, 16], F32))
        scB = E(nc.sbuf_tensor("scB", [128, 2], F32))

        xsqb = wmbuf[:, 0:16 * NGW]
        meanB = outstg[:, :]
        dsT = outstg[0:8, :]
        H8T = rstdB[0:8, :]
        wfT = mulT[0:8, :]
        wslot = [wstage, xw2]
        tmpf = mstage[:, :].bitcast(F32)
        ekstrip = gat[:, 576:832].bitcast(BF16)

        pm = [E(nc.psum_tensor(f"pm{i}", [128, 512], F32)) for i in range(4)]
        pa = E(nc.psum_tensor("pa", [128, 512], F32))
        pb = E(nc.psum_tensor("pb", [128, 512], F32))
        pc = E(nc.psum_tensor("pc", [128, 512], F32))
        pbf = E(nc.psum_tensor("pbf", [128, 512], BF16))

        sem_names = ["S_cst", "S_xa0", "S_xa1", "S_xb0", "S_xb1", "S_res0", "S_res1",
                     "S_ccb", "S_ccm", "G_msk0", "G_msk1", "G_wst0", "G_wst1", "G_wbf0", "G_wbf1", "G_wbf2", "G_wbf3", "G_cmn", "G_cc", "G_cc2",
                     "SCQ0", "SCQ1", "CC0", "CC1", "CC2", "T", "V", "SC"]
        SEM = {nm: E(nc.semaphore(nm)) for nm in sem_names}
        DMA_SEMS = {"S_cst", "S_xa0", "S_xa1", "S_xb0", "S_xb1", "S_res0", "S_res1",
                    "S_ccb", "S_ccm", "G_msk0", "G_msk1", "G_wst0", "G_wst1", "G_wbf0", "G_wbf1", "G_wbf2", "G_wbf3", "G_cmn", "G_cc", "G_cc2",
                    "SCQ0", "SCQ1"}
        CC_SEMS = {"CC0", "CC1", "CC2"}

        block = E(nc.Block())

        PLAN = {e: [] for e in ENGS}
        CNT = {nm: 0 for nm in sem_names}
        CHK = {}

        nops = [0]

        def op(eng, emit, waits=(), inc=None, chk=None):
            nops[0] += 1
            if STOP_N is not None and nops[0] > STOP_N:
                return
            PLAN[eng].append((emit, list(waits), inc))
            if inc is not None:
                CNT[inc] += 16 if inc in DMA_SEMS else 1
                if chk is not None:
                    CHK[chk] = (inc, CNT[inc])

        def wc(name):
            if name not in CHK and STOP_N is not None:
                return ("S_cst", 0)
            return CHK[name]

        wm3 = d_wm.rearrange("p (g t n) -> p g t n", g=NG, t=KT)
        W3 = d_W.rearrange("p (g t n) -> p g t n", g=NG, t=KT)


        for dst, src in [(idn[:, :], d_idn), (idnf[:, :], d_idnf), (avgp[:, :], d_avgp),
                         (EXPc[:, :], d_EXP), (Q4[:, :], d_Q4), (oddm[:, :], d_oddm),
                         (onesc[:, :], d_onesc), (pairP[:, :], d_pairP),
                         (b_sb[:, :], d_b32), (LiM[:, :], d_LiM), (b8r[:, :], d_b8)]:
            op("sync", (lambda e, a=dst, b=src: e.dma_start(out=a, in_=b)), inc="S_cst")
        CHK["csts"] = ("S_cst", CNT["S_cst"])

        def em_mask(q):
            sl = q % 2
            def f(e):
                return e.dma_start(out=mstage[:, sl * 2048:(sl + 1) * 2048],
                                   in_=d_msk[:, q * 2048:(q + 1) * 2048],
                                   max_dma_last_dim=8192)
            return f
        if not SKIP_CC:
            op("gpsimd", lambda e: e.dma_start(out=cc_d_in.ap(), in_=d_b8[0:1, 0:1]),
               inc="G_cmn")
            op("gpsimd", lambda e: e.collective_compute(
                "AllGather", ALU.bypass, replica_groups=[list(range(NCORES))],
                ins=[cc_d_in.ap().opt()], outs=[cc_d_out.ap().opt()]),
                waits=[("G_cmn", 16)], inc="CC0")
        op("gpsimd", em_mask(0), inc="G_msk0", chk="mq0")
        op("gpsimd", em_mask(1), inc="G_msk1", chk="mq1")
        op("gpsimd", lambda e: e.dma_start(out=wstage[:, :], in_=wm3[:, 0, 0:16]),
           inc="G_wst0", chk="wst0h0")
        for sb_ in range(4):
            op("gpsimd", (lambda e, sb_=sb_: e.dma_start(
                out=Wbuf[:, sb_ * 8 * NGW:(sb_ + 1) * 8 * NGW],
                in_=W3[:, 0, 8 * sb_:8 * (sb_ + 1)])),
                inc=f"G_wbf{sb_}")
        op("gpsimd", lambda e: e.dma_start(out=A_bf[:, :], in_=d_A), inc="G_cmn")
        op("gpsimd", lambda e: e.dma_start(out=W8_bf[:, :], in_=d_W8), inc="G_cmn")
        op("gpsimd", lambda e: e.dma_start(out=Bm_bf[:, :], in_=d_Bm), inc="G_cmn", chk="cmn")

        def emit_mask_batch(q):
            sl = q % 2
            for t in range(4):
                ktg = 4 * q + t

                def em_red(e, t=t, ktg=ktg, sl=sl):
                    with nc.allow_low_precision(reason="mask block-sums <=32 exact in bf16"):
                        return e.tensor_reduce(
                            Sn[:, ktg * 16:(ktg + 1) * 16],
                            mstage[:, sl * 2048 + t * 512:sl * 2048 + (t + 1) * 512]
                                .rearrange("p (nb w) -> p nb w", w=32),
                            axis=AX.X, op=ALU.add)
                op("vector", em_red, waits=[wc(f"mq{q}")], inc="V")
            CHK[f"mrq{q}"] = ("V", CNT["V"])
            if q < 6:
                op("gpsimd", em_mask(q + 2), waits=[wc(f"mrq{q}")], inc=f"G_msk{q % 2}",
                   chk=f"mq{q+2}")

        mask_trigger = {1: 0, 3: 1, 5: 2, 7: 3, 9: 4, 11: 5, 13: 6, 15: 7}
        for kt in range(KT):
            pr = kt // 2
            P = pr % 2
            if kt % 2 == 0:
                waits = []
                if kt >= 4:
                    waits.append(wc(f"stT{kt-3}"))
                op("sync", (lambda e, kt=kt, P=P: e.dma_start(
                    out=xf[:, P * 2048:(P + 1) * 2048].rearrange("p (t n) -> p t n", t=2),
                    in_=d_xT[kt * 128:(kt + 2) * 128, :].rearrange("(t p) n -> p t n",
                                                                  p=128))),
                    waits=waits, inc=f"S_xa{P}", chk=f"xa{pr}")
                waits = [wc(f"xa{pr}")]
                if kt >= 4:
                    waits.append(wc(f"stT{kt-3}"))
                op("vector", (lambda e, P=P: e.tensor_copy(
                    xab[:, P * 2048:(P + 1) * 2048], xf[:, P * 2048:(P + 1) * 2048])),
                    waits=waits, inc="V", chk=f"xabV{pr}")
                op("scalar", (lambda e, P=P: e.activation(
                    xsqb[:, P * 2048:(P + 1) * 2048], xf[:, P * 2048:(P + 1) * 2048],
                    AF.Square)),
                    waits=waits, inc="SC", chk=f"sqA{pr}")
            st_ = (kt == 0)
            sp = (kt == KT - 1)
            xoff = P * 2048 + (kt % 2) * 1024
            wst_ = [wc(f"sqA{pr}"), wc(f"xabV{pr}")]
            if kt == 0:
                wst_.append(wc("csts"))
            op("tensor", (lambda e, xoff=xoff, st_=st_, sp=sp: e.matmul(
                pm[0][:, :], avgp[:, :], xab[:, xoff:xoff + 512], start=st_, stop=sp)),
                waits=wst_)
            op("tensor", (lambda e, xoff=xoff, st_=st_, sp=sp: e.matmul(
                pm[1][:, :], avgp[:, :], xab[:, xoff + 512:xoff + 1024],
                start=st_, stop=sp)))
            op("tensor", (lambda e, xoff=xoff, st_=st_, sp=sp: e.matmul(
                pm[2][:, :], avgp[:, :], xsqb[:, xoff:xoff + 512], start=st_, stop=sp)))
            op("tensor", (lambda e, xoff=xoff, st_=st_, sp=sp: e.matmul(
                pm[3][:, :], avgp[:, :], xsqb[:, xoff + 512:xoff + 1024],
                start=st_, stop=sp)), inc="T", chk=f"stT{kt}")
            if kt in mask_trigger:
                emit_mask_batch(mask_trigger[kt])
            if kt == 18:
                op("tensor", lambda e: e.transpose(pa[:, 0:32], b_sb[:, :],
                                                   idnf[0:32, 0:32]),
                   waits=[wc("csts")], inc="T", chk="bcolT")
                op("tensor", lambda e: e.matmul(pc[:, 0:8], onesc[:, :], b8r[:, :],
                                                start=True, stop=True), inc="T", chk="b8T")
                op("vector", lambda e: e.tensor_copy(b_col[:, :], pa[:, 0:32]),
                   waits=[wc("bcolT")], inc="V", chk="bcolV")
                op("vector", lambda e: e.tensor_copy(b8B[:, :], pc[:, 0:8]),
                   waits=[wc("b8T")], inc="V", chk="b8V")
                for ktb in range(KT):
                    w2 = [wc("mrq7"), wc("b8V")] if ktb == 0 else []
                    op("tensor", (lambda e, ktb=ktb: e.matmul(
                        pc[0:16, 4 * ktb:4 * ktb + 4], Sn[:, ktb * 16:(ktb + 1) * 16],
                        Q4[:, :], start=True, stop=True)), waits=w2,
                        inc=("T" if ktb == KT - 1 else None),
                        chk=(f"b16T{KT-1}" if ktb == KT - 1 else None))
                op("vector", lambda e: e.tensor_scalar(B16[:, :], pc[0:16, 0:128], 0.0,
                                                       None, op0=ALU.is_gt),
                   waits=[wc(f"b16T{KT-1}")], inc="V", chk="bthr")
                op("sync", lambda e: e.dma_start(out=cc_b_in.ap(), in_=B16[:, :]),
                   waits=[wc("bthr")], inc="S_ccb", chk="ccbin")
                if SKIP_CC:
                    def em_ag(e):
                        return e.dma_start(out=cc_b_out.ap()[0:16, :], in_=cc_b_in.ap())
                    op("gpsimd", em_ag, waits=[wc("ccbin")], inc="CC1", chk="ag")
                else:
                    op("gpsimd", lambda e: e.collective_compute(
                        "AllGather", ALU.bypass, replica_groups=[list(range(NCORES))],
                        ins=[cc_b_in.ap().opt()], outs=[cc_b_out.ap().opt()]),
                        waits=[wc("ccbin")], inc="CC1", chk="ag")
                op("gpsimd", lambda e: e.dma_start(out=Ball[:, :], in_=cc_b_out.ap()),
                   waits=[("CC1", 1)], inc="G_cc", chk="ballin")

        op("vector", lambda e: e.tensor_copy(stats[0:1, 0:512], pm[0][0:1, :]),
           waits=[wc(f"stT{KT-1}")], inc="V")
        op("vector", lambda e: e.tensor_copy(stats[0:1, 512:1024], pm[1][0:1, :]), inc="V")
        op("vector", lambda e: e.tensor_copy(stats[0:1, 1024:1536], pm[2][0:1, :]), inc="V")
        op("vector", lambda e: e.tensor_copy(stats[0:1, 1536:2048], pm[3][0:1, :]), inc="V",
           chk="stev")
        op("vector", lambda e: e.tensor_tensor(stats[0:1, 2048:3072], stats[0:1, 0:1024],
                                               stats[0:1, 0:1024], op=ALU.mult), inc="V")
        op("vector", lambda e: e.tensor_tensor(stats[0:1, 1024:2048], stats[0:1, 1024:2048],
                                               stats[0:1, 2048:3072], op=ALU.subtract),
           inc="V")
        op("vector", lambda e: e.tensor_scalar(stats[0:1, 1024:2048], stats[0:1, 1024:2048],
                                               EPS_LN, None, op0=ALU.add), inc="V",
           chk="varV")
        op("scalar", lambda e: e.activation(stats[0:1, 2048:3072], stats[0:1, 1024:2048],
                                            AF.Ln),
           waits=[wc("varV")], inc="SC")
        op("scalar", lambda e: e.activation(stats[0:1, 1024:2048], stats[0:1, 2048:3072],
                                            AF.Exp, scale=-0.5), inc="SC", chk="rstdS")
        op("tensor", lambda e: e.matmul(pa[:, :], onesc[:, :], stats[0:1, 0:512],
                                        start=True, stop=True),
           waits=[wc("stev"), wc("bcolV")], inc="T", chk="bc0")
        op("tensor", lambda e: e.matmul(pb[:, :], onesc[:, :], stats[0:1, 512:1024],
                                        start=True, stop=True), inc="T", chk="bc1")
        op("vector", lambda e: e.tensor_copy(meanB[:, 0:512], pa[:, :]),
           waits=[wc("bc0")], inc="V", chk="bc0V")
        op("vector", lambda e: e.tensor_copy(meanB[:, 512:1024], pb[:, :]),
           waits=[wc("bc1")], inc="V", chk="bc1V")
        op("tensor", lambda e: e.matmul(pa[:, :], onesc[:, :], stats[0:1, 1024:1536],
                                        start=True, stop=True),
           waits=[wc("rstdS"), wc("bc0V")], inc="T", chk="bc2")
        op("tensor", lambda e: e.matmul(pb[:, :], onesc[:, :], stats[0:1, 1536:2048],
                                        start=True, stop=True),
           waits=[wc("bc1V")], inc="T", chk="bc3")
        op("vector", lambda e: e.tensor_copy(rstdB[:, 0:512], pa[:, :]),
           waits=[wc("bc2")], inc="V", chk="bc2V")
        op("vector", lambda e: e.tensor_copy(rstdB[:, 512:1024], pb[:, :]),
           waits=[wc("bc3")], inc="V", chk="bc3V")

        op("tensor", lambda e: e.transpose(pbf[:, 0:128], Ball[:, :], idn[:, :]),
           waits=[wc("ballin")], inc="T", chk="ballT")
        op("vector", lambda e: e.tensor_copy(BallT[:, :], pbf[:, 0:128]),
           waits=[wc("ballT")], inc="V", chk="ballTV")
        for c in range(8):
            bank = pa if c % 2 == 0 else pb
            wdma = [wc("ballTV")]
            if c >= 1:
                wdma.append(wc(f"ekmm{c-1}"))
            op("sync", (lambda e, c=c: e.dma_start(
                out=balltc[:, :], in_=BallT[16 * c:16 * (c + 1), :])),
                waits=wdma, inc="S_ccb", chk=f"btc{c}")
            waits = [wc(f"btc{c}")]
            if c >= 2:
                waits.append(wc(f"ekstrip{c-2}"))
            if c < 2:
                waits.append(wc("bc2V") if c == 0 else wc("bc3V"))
            op("tensor", (lambda e, bank=bank: e.matmul(
                bank[:, :], balltc[:, :], EXPc[:, :],
                start=True, stop=True)), waits=waits, inc="T", chk=f"ekmm{c}")
            wv = [wc(f"ekmm{c}")]
            if c >= 1:
                wv.append(wc(f"ektr{c-1}_3"))
            op("vector", (lambda e, bank=bank: e.tensor_copy(
                ekstrip[:, 0:512], bank[:, :])),
                waits=wv, inc="V", chk=f"ekstrip{c}")
            for t in range(4):
                wt = [wc(f"ekstrip{c}")]
                if c > 0 and t == 0:
                    wt.append(wc(f"ekcol{c-1}"))
                op("tensor", (lambda e, t=t: e.transpose(
                    pbf[:, t * 128:(t + 1) * 128],
                    ekstrip[:, t * 128:(t + 1) * 128], idn[:, :])),
                    waits=wt, inc="T", chk=f"ektr{c}_{t}")
            op("vector", (lambda e, c=c: e.tensor_copy(
                EKcol[:, c * 512:(c + 1) * 512], pbf[:, :])),
                waits=[wc(f"ektr{c}_3")], inc="V", chk=f"ekcol{c}")

        for kt in range(KT):
            sl = kt % 2
            waits = []
            if kt < 2:
                waits += [wc(f"sqA{(kt+30)//2}"), wc(f"xabV{(kt+30)//2}")]
            else:
                waits.append(wc(f"xn{kt-2}"))
            op("sync", (lambda e, kt=kt, sl=sl: e.dma_start(
                out=xf[:, sl * R:(sl + 1) * R], in_=d_xT[kt * 128:(kt + 1) * 128, :])),
                waits=waits, inc=f"S_xb{sl}", chk=f"xb{kt}")
            wv = [wc(f"xb{kt}"), wc("bc3V"), wc("bc2V")]
            if kt == 0:
                wv.append(wc("mrq7"))
            op("vector", (lambda e, sl=sl: e.tensor_tensor(
                tmpf[:, 0:R], xf[:, sl * R:(sl + 1) * R], meanB[:, :], op=ALU.subtract)),
                waits=wv, inc="V")
            op("vector", (lambda e, kt=kt: e.tensor_tensor(
                xnT[:, kt * R:(kt + 1) * R], tmpf[:, 0:R], rstdB[:, :], op=ALU.mult)),
                inc="V", chk=f"xn{kt}")
            Xh = kt // 16
            op("vector", (lambda e, kt=kt, Xh=Xh: e.tensor_tensor(
                wmbuf[:, kt * NGW:(kt + 1) * NGW].rearrange("p (nb w) -> p nb w", w=32),
                wslot[Xh % 2][:, (kt % 16) * NGW:((kt % 16) + 1) * NGW]
                    .rearrange("p (nb w) -> p nb w", w=32),
                EKcol[:, kt * 128:kt * 128 + 8].unsqueeze(-1).broadcast_to([128, 8, 32]),
                op=ALU.mult)),
                waits=([wc(f"stT{KT-1}")] if kt == 0 else []) +
                      [(f"G_wst{Xh % 2}", 16 * (Xh // 2 + 1)), wc(f"ekcol{kt//4}")],
                inc="V", chk=f"vwm0k{kt}")
            sp = (kt == KT - 1)
            wz = [wc(f"xn{kt}")]
            if kt == 0:
                wz += [wc("cmn"), wc("bc2V"), wc("bc3V"), wc("ekcol7"), wc("bthr")]
            op("tensor", (lambda e, kt=kt, sp=sp: e.matmul(
                pa[0:32, :], A_bf[:, kt * 32:(kt + 1) * 32], xnT[:, kt * R:kt * R + 512],
                start=(kt == 0), stop=sp, skip_group_check=True)), waits=wz)
            op("tensor", (lambda e, kt=kt, sp=sp: e.matmul(
                pb[0:32, :], A_bf[:, kt * 32:(kt + 1) * 32],
                xnT[:, kt * R + 512:kt * R + 1024],
                start=(kt == 0), stop=sp, skip_group_check=True)))
            op("tensor", (lambda e, kt=kt, sp=sp: e.matmul(
                pc[0:8, :], W8_bf[:, kt * 8:(kt + 1) * 8], xnT[:, kt * R:kt * R + 512],
                start=(kt == 0), stop=sp, skip_group_check=True)),
                inc=("T" if sp else None), chk=("zsw" if sp else None))
            for j in (0, 1):
                pmA = pm[2 * j]
                pmB = pm[2 * j + 1]
                wj = [wc(f"vwm0k{kt}")]
                if kt == 0:
                    wj.append(wc("stev"))
                if kt % 8 == 0:
                    wj.append((f"G_wbf{kt//8}", 16))
                wof = kt * NGW + j * 128
                mkchk = (j == 1 and kt % 8 == 7)
                op("tensor", (lambda e, pmA=pmA, wof=wof, kt=kt, st_=(kt == 0): e.matmul(
                    pmA[:, :], wmbuf[:, wof:wof + 128], xnT[:, kt * R:kt * R + 512],
                    start=st_, stop=False, skip_group_check=True)), waits=wj)
                op("tensor", (lambda e, pmB=pmB, wof=wof, kt=kt, st_=(kt == 0): e.matmul(
                    pmB[:, :], wmbuf[:, wof:wof + 128], xnT[:, kt * R + 512:kt * R + 1024],
                    start=st_, stop=False, skip_group_check=True)),
                    inc=("T" if mkchk else None),
                    chk=(f"tWMj1k{kt}" if mkchk else None))
                op("tensor", (lambda e, pmA=pmA, wof=wof, kt=kt: e.matmul(
                    pmA[:, :], Wbuf[:, wof:wof + 128], xnT[:, kt * R:kt * R + 512],
                    start=False, stop=False, skip_group_check=True)))
                op("tensor", (lambda e, pmB=pmB, wof=wof, kt=kt: e.matmul(
                    pmB[:, :], Wbuf[:, wof:wof + 128], xnT[:, kt * R + 512:kt * R + 1024],
                    start=False, stop=False, skip_group_check=True)),
                    inc=("T" if mkchk else None),
                    chk=(f"tWj1k{kt}" if mkchk else None))

        op("gpsimd", lambda e: e.dma_start(out=xw2[:, :], in_=wm3[:, 0, 16:32]),
           waits=[wc(f"stT{KT-1}")], inc="G_wst1", chk="wst0h1")
        op("gpsimd", lambda e: e.dma_start(out=wstage[:, :], in_=wm3[:, 1, 0:16]),
           waits=[wc("vwm0k15")], inc="G_wst0", chk="wst1h0")
        op("gpsimd", lambda e: e.dma_start(out=xw2[:, :], in_=wm3[:, 1, 16:32]),
           waits=[wc(f"vwm0k{KT-1}")], inc="G_wst1", chk="wst1h1")

        op("vector", lambda e: e.tensor_copy(zTb[:, 0:512], pa[0:32, :]),
           waits=[wc("zsw")], inc="V", chk="zloV")
        op("vector", lambda e: e.tensor_copy(zTb[:, 512:1024], pb[0:32, :]), inc="V",
           chk="zhiV")
        op("vector", lambda e: e.tensor_copy(H8T[:, 0:512], pc[0:8, :]), inc="V", chk="zevV")
        CHK["zevS"] = CHK["zevV"]
        for kt in range(KT):
            sp = (kt == KT - 1)
            waits = [wc("zevV")] if kt == 0 else []
            op("tensor", (lambda e, kt=kt, sp=sp: e.matmul(
                pc[0:8, :], W8_bf[:, kt * 8:(kt + 1) * 8],
                xnT[:, kt * R + 512:kt * R + 1024],
                start=(kt == 0), stop=sp, skip_group_check=True)),
                waits=waits, inc=("T" if sp else None), chk=("h8hi" if sp else None))
        op("vector", lambda e: e.tensor_copy(H8T[:, 512:1024], pc[0:8, :]),
           waits=[wc("h8hi")], inc="V", chk="h8hiV")

        op("scalar", lambda e: e.activation(LiM_bf[:, :], LiM[:, :], AF.Identity),
           waits=[wc("csts")], inc="SC", chk="limS")
        op("scalar", lambda e: e.activation(LiMs_bf[:, :], LiM[:, :], AF.Identity,
                                            scale=SCALING), inc="SC", chk="limsS")
        op("tensor", lambda e: e.transpose(pbf[0:32, 0:8], LiM_bf[:, :], idn[0:8, 0:8]),
           waits=[wc("limS"), wc("ekcol7")], inc="T", chk="limT")
        op("vector", lambda e: e.tensor_copy(LiMT_bf[:, :], pbf[0:32, 0:8]),
           waits=[wc("limT")], inc="V", chk="limV")

        op("tensor", lambda e: e.matmul(pa[0:8, :], LiMT_bf[:, :], zTb[:, 0:512],
                                        start=True, stop=True),
           waits=[wc("zevV"), wc("zevS"), wc("limV")], inc="T", chk="ds0")
        op("tensor", lambda e: e.matmul(pb[0:8, :], LiMT_bf[:, :], zTb[:, 512:1024],
                                        start=True, stop=True), inc="T", chk="ds1")
        op("vector", lambda e: e.tensor_copy(dsT[:, 0:512], pa[0:8, :]),
           waits=[wc("ds0")], inc="V")
        op("vector", lambda e: e.tensor_copy(dsT[:, 512:1024], pb[0:8, :]),
           waits=[wc("ds1")], inc="V", chk="dsV")

        for mt in range(8):
            waits = [wc("dsV"), wc("h8hiV")] if mt == 0 else []
            op("tensor", (lambda e, mt=mt: e.transpose(
                pa[:, mt * 8:(mt + 1) * 8], H8T[0:8, mt * 128:(mt + 1) * 128],
                idnf[0:8, 0:8])), waits=waits, inc="T")
            op("tensor", (lambda e, mt=mt: e.transpose(
                pb[:, mt * 8:(mt + 1) * 8], dsT[0:8, mt * 128:(mt + 1) * 128],
                idnf[0:8, 0:8])), inc="T", chk=("tokT" if mt == 7 else None))
        op("vector", lambda e: e.tensor_copy(H8tok[:, :], pa[:, 0:64]),
           waits=[wc("tokT")], inc="V")
        op("vector", lambda e: e.tensor_copy(dstok[:, :], pb[:, 0:64]), inc="V", chk="tokV")
        op("vector", lambda e: e.tensor_tensor(
            H8tok[:, :].rearrange("p (mt e) -> p mt e", e=8),
            H8tok[:, :].rearrange("p (mt e) -> p mt e", e=8),
            b8B[:, :].unsqueeze(1).broadcast_to([128, 8, 8]), op=ALU.add),
            waits=[wc("b8V")], inc="V", chk="h8bias")

        op("vector", lambda e: e.reduce_max(gat[:, 0:1], H8tok[:, :], axis=AX.X,
                                            apply_absolute_value=True),
           waits=[wc("h8bias")], inc="V")
        op("vector", lambda e: e.reduce_max(gat[:, 1:2], dstok[:, :], axis=AX.X,
                                            apply_absolute_value=True),
           waits=[wc("tokV")], inc="V")
        op("vector", lambda e: e.tensor_tensor(gat[:, 0:2], gat[:, 0:2],
                                               oddm[:, :].broadcast_to([128, 2]),
                                               op=ALU.mult),
           waits=[wc("csts")], inc="V", chk="rmax")
        op("tensor", lambda e: e.transpose(pc[0:2, 0:128], gat[:, 0:2], idnf[:, :]),
           waits=[wc("rmax"), wc("bthr")], inc="T", chk="mxT")
        op("vector", lambda e: e.reduce_max(gat[0:2, 8:9], pc[0:2, 0:128], axis=AX.X),
           waits=["mxT"], inc="V", chk="mxV")
        op("sync", lambda e: e.dma_start(out=cc_m_in.ap(), in_=gat[0:2, 8:9]),
           waits=[wc("mxV")], inc="S_ccm", chk="ccmin")
        if SKIP_CC:
            op("gpsimd", lambda e: e.dma_start(out=cc_m_out.ap(), in_=cc_m_in.ap()),
               waits=[wc("ccmin")], inc="CC2", chk="ar")
        else:
            op("gpsimd", lambda e: e.collective_compute(
                "AllReduce", ALU.max, replica_groups=[list(range(NCORES))],
                ins=[cc_m_in.ap().opt()], outs=[cc_m_out.ap().opt()]),
                waits=[wc("ccmin")], inc="CC2", chk="ar")
        op("gpsimd", lambda e: e.dma_start(out=gmax[0:1, 0:2], in_=cc_m_out.ap()),
           waits=[("CC2", 1)], inc="G_cc2", chk="gmaxin")

        op("vector", lambda e: e.tensor_scalar(gmax[0:1, 2:4], gmax[0:1, 0:2], EPS_G, None,
                                               op0=ALU.max),
           waits=[wc("gmaxin")], inc="V")
        op("vector", lambda e: e.reciprocal(gmax[0:1, 4:6], gmax[0:1, 2:4]), inc="V")
        op("vector", lambda e: e.tensor_scalar(gmax[0:1, 6:8], gmax[0:1, 4:6], GAMMA, None,
                                               op0=ALU.mult), inc="V", chk="scV")
        op("tensor", lambda e: e.matmul(pc[:, 0:2], onesc[:, :], gmax[0:1, 6:8],
                                        start=True, stop=True),
           waits=[wc("scV")], inc="T", chk="scT")
        op("vector", lambda e: e.tensor_copy(scB[:, :], pc[:, 0:2]),
           waits=["scT"], inc="V", chk="scBV")

        GA = gat

        def V3(ap):
            return ap.rearrange("p (mt e) -> p mt e", e=8)

        def bc8(ap):
            return ap.unsqueeze(-1).broadcast_to([128, 8, 8])

        op("vector", lambda e: e.tensor_scalar(GA[:, 64:128], H8tok[:, :], scB[:, 0:1], None,
                                               op0=ALU.mult),
           waits=[wc("scBV")], inc="V")
        op("vector", lambda e: e.scalar_tensor_tensor(GA[:, 128:192], dstok[:, :],
                                                      scB[:, 1:2], GA[:, 64:128],
                                                      op0=ALU.mult, op1=ALU.add),
           inc="V", chk="logit")
        op("scalar", lambda e: e.activation(GA[:, 192:256], GA[:, 128:192], AF.Exp),
           waits=[wc("logit")], inc="SC", chk="expS")
        op("vector", lambda e: e.tensor_reduce(GA[:, 256:264], V3(GA[:, 192:256]),
                                               axis=AX.X, op=ALU.add),
           waits=[wc("expS")], inc="V")
        op("vector", lambda e: e.reciprocal(GA[:, 336:344], GA[:, 256:264]), inc="V")
        op("vector", lambda e: e.tensor_tensor(V3(GA[:, 320:384]), V3(GA[:, 192:256]),
                                               bc8(GA[:, 336:344]), op=ALU.mult),
           inc="V", chk="probs")
        op("vector", lambda e: e.tensor_reduce(GA[:, 272:280], V3(GA[:, 320:384]),
                                               axis=AX.X, op=ALU.max), inc="V")
        op("vector", lambda e: e.tensor_tensor(V3(GA[:, 384:448]), V3(GA[:, 320:384]),
                                               bc8(GA[:, 272:280]), op=ALU.is_ge), inc="V")
        op("vector", lambda e: e.tensor_reduce(GA[:, 264:272], V3(GA[:, 384:448]),
                                               axis=AX.X, op=ALU.add), inc="V")
        op("vector", lambda e: e.scalar_tensor_tensor(GA[:, 448:512], GA[:, 384:448], -1e30,
                                                      GA[:, 320:384], op0=ALU.mult,
                                                      op1=ALU.add), inc="V")
        op("vector", lambda e: e.tensor_reduce(GA[:, 280:288], V3(GA[:, 448:512]),
                                               axis=AX.X, op=ALU.max), inc="V")
        op("vector", lambda e: e.tensor_scalar(GA[:, 288:296], GA[:, 264:272], 2.0, None,
                                               op0=ALU.is_ge), inc="V")
        op("vector", lambda e: e.tensor_tensor(GA[:, 296:304], GA[:, 272:280],
                                               GA[:, 280:288], op=ALU.subtract), inc="V")
        op("vector", lambda e: e.tensor_tensor(GA[:, 296:304], GA[:, 296:304],
                                               GA[:, 288:296], op=ALU.mult), inc="V")
        op("vector", lambda e: e.tensor_tensor(GA[:, 304:312], GA[:, 296:304],
                                               GA[:, 280:288], op=ALU.add), inc="V",
           chk="thr")
        op("vector", lambda e: e.tensor_tensor(V3(GA[:, 512:576]), bc8(GA[:, 304:312]),
                                               V3(GA[:, 320:384]), op=ALU.subtract),
           inc="V", chk="preV")
        op("scalar", lambda e: e.activation(GA[:, 576:640], GA[:, 512:576], AF.Exp,
                                            scale=2.0),
           waits=[wc("preV"), wc("ekcol7")], inc="SC", chk="qS")
        op("vector", lambda e: e.tensor_scalar(GA[:, 640:704], GA[:, 576:640], 1.0, None,
                                               op0=ALU.add),
           waits=[wc("qS")], inc="V")
        op("vector", lambda e: e.reciprocal(GA[:, 704:768], GA[:, 640:704]), inc="V")
        op("vector", lambda e: e.tensor_tensor(GA[:, 768:832], GA[:, 320:384],
                                               GA[:, 704:768], op=ALU.mult), inc="V")
        op("vector", lambda e: e.tensor_reduce(GA[:, 312:320], V3(GA[:, 768:832]),
                                               axis=AX.X, op=ALU.add), inc="V")
        op("vector", lambda e: e.tensor_scalar(GA[:, 312:320], GA[:, 312:320], 1e-9, None,
                                               op0=ALU.add), inc="V")
        op("vector", lambda e: e.reciprocal(GA[:, 328:336], GA[:, 312:320]), inc="V")
        op("vector", lambda e: e.tensor_tensor(V3(wnb[:, :]), V3(GA[:, 768:832]),
                                               bc8(GA[:, 328:336]), op=ALU.mult),
           inc="V", chk="wnV")

        ev_cnt = {}
        out_cnt = {}

        def plan_main_block(j):
            g = j // 2
            par = j % 2
            pmA = pm[2 * par]
            pmB = pm[2 * par + 1]
            gbase = (g % 2) * (KT * NGW)
            for kt in range(KT):
                waits = [wc(f"vwm{g}k{kt}")]
                if kt == 0:
                    waits.append(("V", ev_cnt[j - 2]))
                if kt % 8 == 0:
                    waits.append((f"G_wbf{kt//8}", 16 * (g + 1)))
                wof = gbase + kt * NGW + par * 128
                wofW = kt * NGW + par * 128
                mkchk = (par == 1 and kt % 8 == 7)
                op("tensor", (lambda e, pmA=pmA, wof=wof, kt=kt, st_=(kt == 0): e.matmul(
                    pmA[:, :], wmbuf[:, wof:wof + 128], xnT[:, kt * R:kt * R + 512],
                    start=st_, stop=False, skip_group_check=True)), waits=waits)
                op("tensor", (lambda e, pmB=pmB, wof=wof, kt=kt, st_=(kt == 0): e.matmul(
                    pmB[:, :], wmbuf[:, wof:wof + 128], xnT[:, kt * R + 512:kt * R + 1024],
                    start=st_, stop=False, skip_group_check=True)),
                    inc=("T" if mkchk else None),
                    chk=(f"tWMj{j}k{kt}" if mkchk else None))
                op("tensor", (lambda e, pmA=pmA, wofW=wofW, kt=kt: e.matmul(
                    pmA[:, :], Wbuf[:, wofW:wofW + 128], xnT[:, kt * R:kt * R + 512],
                    start=False, stop=False, skip_group_check=True)))
                op("tensor", (lambda e, pmB=pmB, wofW=wofW, kt=kt: e.matmul(
                    pmB[:, :], Wbuf[:, wofW:wofW + 128], xnT[:, kt * R + 512:kt * R + 1024],
                    start=False, stop=False, skip_group_check=True)),
                    inc=("T" if mkchk else None),
                    chk=(f"tWj{j}k{kt}" if mkchk else None))

        def plan_delta_and_evac(j):
            par = j % 2
            pmA = pm[2 * par]
            pmB = pm[2 * par + 1]
            op("tensor", (lambda e, pmA=pmA, j=j: e.matmul(
                pmA[:, :], Bm_bf[:, j * 128:(j + 1) * 128], mulT[:, 0:512],
                start=False, stop=True, skip_group_check=True)),
                waits=[wc("vmul")], inc="T")
            op("tensor", (lambda e, pmB=pmB, j=j: e.matmul(
                pmB[:, :], Bm_bf[:, j * 128:(j + 1) * 128], mulT[:, 512:1024],
                start=False, stop=True, skip_group_check=True)), inc="T", chk=f"tDj{j}")
            rsl = j % 2
            wres = [wc(f"vwm0k{KT-1}")] if j < 2 else [("V", ev_cnt[j - 2])]
            if j < 2:
                wres.append(wc(f"xn{KT-1}"))
            op("sync", (lambda e, j=j, rsl=rsl: e.dma_start(
                out=xf[:, rsl * R:(rsl + 1) * R], in_=d_xT[j * 128:(j + 1) * 128, :])),
                waits=wres, inc=f"S_res{rsl}", chk=f"res{j}")
            for h in range(2):
                bank = pmA if h == 0 else pmB
                osl = (2 * j + h) % 2
                waits = [wc(f"tDj{j}"), wc(f"res{j}")]
                if 2 * j + h >= 2:
                    prev = 2 * j + h - 2
                    waits.append(out_cnt[(prev // 2, prev % 2)])
                op("vector", (lambda e, bank=bank, osl=osl, rsl=rsl, j=j, h=h:
                    e.scalar_tensor_tensor(
                        outstg[:, osl * 512:(osl + 1) * 512], bank[:, :], b_col[:, j:j + 1],
                        xf[:, rsl * R + h * 512:rsl * R + (h + 1) * 512],
                        op0=ALU.add, op1=ALU.add)),
                    waits=waits, inc="V", chk=f"ev{j}_{h}")
                op("scalar", (lambda e, j=j, h=h, osl=osl: e.dma_start(
                    out=d_out[j * 128:(j + 1) * 128, h * 512:(h + 1) * 512],
                    in_=outstg[:, osl * 512:(osl + 1) * 512])),
                    waits=[wc(f"ev{j}_{h}")], inc=f"SCQ{osl}")
                out_cnt[(j, h)] = (f"SCQ{osl}", CNT[f"SCQ{osl}"])
            ev_cnt[j] = CHK[f"ev{j}_1"][1]

        def plan_vwm(g):
            base = (g % 2) * (KT * NGW)
            for kt in range(KT):
                X = 2 * g + kt // 16
                waits = [(f"G_wst{X % 2}", 16 * (X // 2 + 1))]
                if g >= 2:
                    waits.append(wc(f"tWMj{2*g-3}k{8*(kt//8)+7}"))
                op("vector", (lambda e, g=g, kt=kt, base=base, X=X: e.tensor_tensor(
                    wmbuf[:, base + kt * NGW:base + (kt + 1) * NGW]
                        .rearrange("p (nb w) -> p nb w", w=32),
                    wslot[X % 2][:, (kt % 16) * NGW:((kt % 16) + 1) * NGW]
                        .rearrange("p (nb w) -> p nb w", w=32),
                    EKcol[:, kt * 128 + 8 * g:kt * 128 + 8 * g + 8]
                        .unsqueeze(-1).broadcast_to([128, 8, 32]),
                    op=ALU.mult)),
                    waits=waits, inc="V", chk=f"vwm{g}k{kt}")

        for mt in range(8):
            waits = [wc("wnV")] if mt == 0 else []
            op("tensor", (lambda e, mt=mt: e.matmul(
                pa[:, mt * 8:(mt + 1) * 8], pairP[:, :], wnb[:, mt * 8:(mt + 1) * 8],
                start=True, stop=True)), waits=waits,
                inc=("T" if mt == 7 else None), chk=("wf8" if mt == 7 else None))
        op("vector", lambda e: e.tensor_copy(wfb[:, :], pa[:, 0:64]),
           waits=[wc("wf8")], inc="V", chk="wfbV")
        for rnd in range(2):
            for t in range(4):
                mt = rnd * 4 + t
                waits = [wc("wfbV")]
                if rnd == 1 and t == 0:
                    waits.append(wc("wfT0V"))
                op("tensor", (lambda e, mt=mt, t=t: e.transpose(
                    pbf[0:8, t * 128:(t + 1) * 128], wfb[:, mt * 8:(mt + 1) * 8],
                    idn[:, :])),
                    waits=waits, inc=("T" if t == 3 else None),
                    chk=(f"wfTr{rnd}" if t == 3 else None))
            op("vector", (lambda e, rnd=rnd: e.tensor_copy(
                wfT[0:8, rnd * 512:(rnd + 1) * 512], pbf[0:8, 0:512])),
                waits=[wc(f"wfTr{rnd}")], inc="V", chk=f"wfT{rnd}V")
        op("tensor", lambda e: e.matmul(pa[0:32, :], LiMs_bf[:, :], wfT[0:8, 0:512],
                                        start=True, stop=True),
           waits=[wc("wfT1V"), wc("limsS")], inc="T", chk="pmx0")
        op("tensor", lambda e: e.matmul(pb[0:32, :], LiMs_bf[:, :], wfT[0:8, 512:1024],
                                        start=True, stop=True), inc="T", chk="pmx1")
        op("vector", lambda e: e.tensor_tensor(mulT[:, 0:512], pa[0:32, :], zTb[:, 0:512],
                                               op=ALU.mult),
           waits=[wc("pmx0")], inc="V")
        op("vector", lambda e: e.tensor_tensor(mulT[:, 512:1024], pb[0:32, :],
                                               zTb[:, 512:1024], op=ALU.mult),
           waits=[wc("pmx1")], inc="V", chk="vmul")

        plan_delta_and_evac(0)
        plan_delta_and_evac(1)
        plan_vwm(1)
        plan_vwm(2)
        nb_eff = NB if NB_EFF is None else NB_EFF
        for j in range(2, nb_eff):
            plan_main_block(j)
            plan_delta_and_evac(j)
            if j % 2 == 1 and (j + 3) // 2 < NG:
                plan_vwm((j + 3) // 2)

        for g in range(1, (NB if NB_EFF is None else NB_EFF) // 2):
            for sb_ in range(4):
                op("gpsimd", (lambda e, g=g, sb_=sb_: e.dma_start(
                    out=Wbuf[:, sb_ * 8 * NGW:(sb_ + 1) * 8 * NGW],
                    in_=W3[:, g, 8 * sb_:8 * (sb_ + 1)])),
                    waits=[wc(f"tWj{2*g-1}k{8*sb_+7}")], inc=f"G_wbf{sb_}")
            if g + 1 < NG:
                op("gpsimd", (lambda e, g=g: e.dma_start(
                    out=wslot[0][:, :], in_=wm3[:, g + 1, 0:16])),
                   waits=[wc(f"vwm{g}k15")], inc="G_wst0", chk=f"wst{g+1}h0")
                op("gpsimd", (lambda e, g=g: e.dma_start(
                    out=wslot[1][:, :], in_=wm3[:, g + 1, 16:32])),
                   waits=[wc(f"vwm{g}k{KT-1}")], inc="G_wst1", chk=f"wst{g+1}h1")

        if debug_taps:
            tap_list = [
                ("t_xnT", xnT[:, :], [wc(f"xn{KT-1}")]),
                ("t_EKcol", EKcol[:, :], [wc("ekcol7")]),
                ("t_Ball", Ball[:, :], [wc("ballin")]),
                ("t_H8tok", H8tok[:, :], [wc("h8bias")]),
                ("t_dstok", dstok[:, :], [wc("tokV")]),
                ("t_wnb", wnb[:, :], [wc("wnV")]),
                ("t_mulT", mulT[:, :], [wc("vmul")]),
                ("t_gmax", gmax[0:1, 0:8], [wc("scV")]),
                ("t_Sn", Sn[:, :], [wc("mrq7")]),
                ("t_zT", zTb[:, :], [wc("zhiV")]),

            ]
            for nm, src, waits in tap_list:
                op("sync", (lambda e, nm=nm, src=src: e.dma_start(out=taps[nm], in_=src)),
                   waits=waits, inc="S_cst")

        PLAN["scalar"].append((None, [("SCQ0", CNT["SCQ0"]), ("SCQ1", CNT["SCQ1"])], None))
        PLAN["sync"].append((None, [("S_cst", CNT["S_cst"])], None))

        def make_runner(eng_name):
            own_sem = {"vector": "V", "scalar": "SC"}.get(eng_name)
            def run(engine):
                seen = {}
                own = 0
                for emit, waits, inc in PLAN[eng_name]:
                    if own_sem is not None and emit is not None and own > 0:
                        if seen.get(own_sem, 0) < own:
                            engine.wait_ge(SEM[own_sem], own)
                            seen[own_sem] = own
                    for wspec in waits:
                        if isinstance(wspec, str):
                            if wspec not in CHK and STOP_N is not None:
                                continue
                            semn, val = CHK[wspec]
                        else:
                            semn, val = wspec
                        if seen.get(semn, 0) < val:
                            engine.wait_ge(SEM[semn], val)
                            seen[semn] = val
                    if emit is None:
                        continue
                    inst = emit(engine)
                    if inc == own_sem:
                        own += 1
                    if inc is not None:
                        if inc in CC_SEMS:
                            if SKIP_CC:
                                inst.then_inc(SEM[inc], 16)
                            else:
                                inst.then_inc(SEM[inc])
                        elif inc in DMA_SEMS:
                            inst.then_inc(SEM[inc], 16)
                        else:
                            inst.then_inc(SEM[inc], 1)
            return run

        block.sync(make_runner("sync"))
        block.gpsimd(make_runner("gpsimd"))
        block.scalar(make_runner("scalar"))
        block.tensor(make_runner("tensor"))
        block.vector(make_runner("vector"))

    return nc


def _consts():
    idn = np.eye(128, dtype=ml_dtypes.bfloat16)
    idnf = np.eye(128, dtype=np.float32)
    avgp = np.zeros((128, 128), dtype=ml_dtypes.bfloat16)
    avgp[:, 0] = ml_dtypes.bfloat16(1.0 / 4096.0)
    EXP = np.zeros((16, 512), dtype=ml_dtypes.bfloat16)
    for kb in range(16):
        EXP[kb, kb * 32:(kb + 1) * 32] = 1
    Q4 = np.zeros((128, 4), dtype=ml_dtypes.bfloat16)
    for p in range(128):
        Q4[p, p // 32] = 1
    oddm = np.zeros((128, 1), np.float32)
    oddm[1::2, 0] = 1.0
    onesc = np.ones((1, 128), np.float32)
    pairP = np.zeros((128, 128), dtype=ml_dtypes.bfloat16)
    for m in range(128):
        pairP[m | 1, m] = 1
    return idn, idnf, avgp, EXP, Q4, oddm, onesc, pairP


def _tile_khost(M):
    K, C = M.shape
    return np.ascontiguousarray(M.reshape(32, 128, C).transpose(1, 0, 2).reshape(128, 32 * C))


def _tile_weight(Wmat):
    arr = Wmat.reshape(NG, NGW, KT, 128)
    return np.ascontiguousarray(arr.transpose(3, 0, 2, 1).reshape(128, NG * KT * NGW))


def kernel(L0_attn_data, L0_attn_weight, L0_attn_mask, L0_ffn_W, L0_ffn_b,
           L0_ffn_A, L0_ffn_Bmat, L0_ffn_LiMEs):
    X = np.ascontiguousarray(np.asarray(L0_attn_data, np.float32).reshape(2 * 4096, D))
    weight = np.asarray(L0_attn_weight, np.float32)
    mask = np.asarray(L0_attn_mask, np.int32)
    W = np.asarray(L0_ffn_W, np.float32)
    bvec = np.asarray(L0_ffn_b, np.float32)
    A = np.asarray(L0_ffn_A, np.float32)
    Bmat = np.ascontiguousarray(np.asarray(L0_ffn_Bmat, np.float32))
    LiMa = np.ascontiguousarray(np.asarray(L0_ffn_LiMEs, np.float32))

    idn, idnf, avgp, EXP, Q4, oddm, onesc, pairP = _consts()

    wmh = _tile_weight(weight)
    Wh = _tile_weight(W)
    Ah = _tile_khost(A)
    W8h = _tile_khost(np.ascontiguousarray(W[0:8, :].T))
    b32 = np.ascontiguousarray(bvec.reshape(32, 128))
    b8 = np.ascontiguousarray(bvec[0:8].reshape(1, 8))
    maskT = np.ascontiguousarray(mask.T)

    nc = build_nc(debug_taps=DEBUG_TAPS)

    in_maps = []
    for c in range(NCORES):
        xT_c = np.ascontiguousarray(X[c * R:(c + 1) * R, :].T)
        msk_c = _tile_khost(np.ascontiguousarray(maskT[:, c * 512:(c + 1) * 512]))
        in_maps.append({
            "xT": xT_c, "wmh": wmh, "Wh": Wh, "mskh": msk_c,
            "Ah": Ah, "W8h": W8h, "b32": b32, "b8": b8, "Bm": Bmat, "LiM": LiMa,
            "idn": idn, "idnf": idnf, "avgp": avgp, "EXPc": EXP, "Q4c": Q4,
            "oddm": oddm, "onesc": onesc, "pairP": pairP,
        })

    res = run_bass_kernel_spmd(nc, in_maps, core_ids=list(range(NCORES)))
    out = np.empty((2 * 4096, D), np.float32)
    for c in range(NCORES):
        out[c * R:(c + 1) * R, :] = np.asarray(res.results[c]["outT"]).T
    if DEBUG_TAPS:
        kernel.last_results = res
    return out.reshape(2, 4096, D)


# version 05
# speedup vs baseline: 1.2869x; 1.0224x over previous
import numpy as np
import ml_dtypes
import concourse.bass as bass
import concourse.mybir as mybir
from concourse.bass_utils import run_bass_kernel_spmd

F32 = mybir.dt.float32
BF16 = mybir.dt.bfloat16
I32 = mybir.dt.int32
AF = mybir.ActivationFunctionType
ALU = mybir.AluOpType
AX = mybir.AxisListType

NCORES = 8
D = 4096
R = 1024
KT = 32
NGW = 256
NG = D // NGW
NB = 32
EPS_LN = 1e-5
EPS_G = 1e-6
GAMMA = 0.5
SCALING = 16.0 / 32.0

DEBUG_TAPS = False
SKIP_CC = False
NB_EFF = None
PHASE1_ONLY = False
STOP_N = None

ENGS = ("sync", "gpsimd", "scalar", "tensor", "vector")


def build_nc(debug_taps=False):
    nc = bass.Bass("TRN2", target_bir_lowering=False, debug=False,
                   dynamic_dma_scratch_size=16384)

    d_xT = nc.dram_tensor("xT", [D, R], F32, kind="ExternalInput").ap()
    d_wm = nc.dram_tensor("wmh", [128, KT * D], F32, kind="ExternalInput").ap()
    d_W = nc.dram_tensor("Wh", [128, KT * D], F32, kind="ExternalInput").ap()
    d_msk = nc.dram_tensor("mskh", [128, KT * 512], I32, kind="ExternalInput").ap()
    d_A = nc.dram_tensor("Ah", [128, KT * 32], F32, kind="ExternalInput").ap()
    d_W8 = nc.dram_tensor("W8h", [128, KT * 8], F32, kind="ExternalInput").ap()
    d_b32 = nc.dram_tensor("b32", [32, 128], F32, kind="ExternalInput").ap()
    d_b8 = nc.dram_tensor("b8", [1, 8], F32, kind="ExternalInput").ap()
    d_Bm = nc.dram_tensor("Bm", [32, D], F32, kind="ExternalInput").ap()
    d_LiM = nc.dram_tensor("LiM", [8, 32], F32, kind="ExternalInput").ap()
    d_idn = nc.dram_tensor("idn", [128, 128], BF16, kind="ExternalInput").ap()
    d_idnf = nc.dram_tensor("idnf", [128, 128], F32, kind="ExternalInput").ap()
    d_avgp = nc.dram_tensor("avgp", [128, 128], BF16, kind="ExternalInput").ap()
    d_EXP = nc.dram_tensor("EXPc", [16, 512], BF16, kind="ExternalInput").ap()
    d_Q4 = nc.dram_tensor("Q4c", [128, 4], BF16, kind="ExternalInput").ap()
    d_oddm = nc.dram_tensor("oddm", [128, 1], F32, kind="ExternalInput").ap()
    d_onesc = nc.dram_tensor("onesc", [1, 128], F32, kind="ExternalInput").ap()
    d_pairP = nc.dram_tensor("pairP", [128, 128], BF16, kind="ExternalInput").ap()

    d_out = nc.dram_tensor("outT", [D, R], F32, kind="ExternalOutput").ap()

    cc_b_in = nc.dram_tensor("cc_b_in", [16, 128], BF16)
    cc_b_out = nc.dram_tensor("cc_b_out", [128, 128], BF16)
    cc_m_in = nc.dram_tensor("cc_m_in", [1, 2], F32)
    cc_m_out = nc.dram_tensor("cc_m_out", [1, 2], F32)
    cc_d_in = nc.dram_tensor("cc_d_in", [1, 1], F32)
    cc_d_out = nc.dram_tensor("cc_d_out", [8, 1], F32)

    taps = {}
    if debug_taps:
        for nm, shp, dt in [("t_xnT", [128, KT * R], BF16), ("t_EKcol", [128, KT * 128], BF16),
                            ("t_Ball", [128, 128], BF16), ("t_H8tok", [128, 64], F32),
                            ("t_dstok", [128, 64], F32), ("t_wnb", [128, 64], BF16),
                            ("t_mulT", [32, R], BF16), ("t_gmax", [1, 8], F32),
                            ("t_Sn", [128, 512], BF16), ("t_zT", [32, R], BF16)]:
            taps[nm] = nc.dram_tensor(nm, shp, dt, kind="ExternalOutput").ap()

    from contextlib import ExitStack
    with ExitStack() as st:
        E = st.enter_context

        xnT = E(nc.sbuf_tensor("xnT", [128, KT * R], BF16))
        wstage = E(nc.sbuf_tensor("wstage", [128, 16 * NGW], BF16))
        wmbuf = E(nc.sbuf_tensor("wmbuf", [128, 2 * KT * NGW], BF16))
        Wbuf = E(nc.sbuf_tensor("Wbuf", [128, KT * NGW], BF16))
        mstage = E(nc.sbuf_tensor("mstage", [128, 8 * 512], BF16))
        EKcol = E(nc.sbuf_tensor("EKcol", [128, KT * 128], BF16))
        xf = E(nc.sbuf_tensor("xf", [128, 4 * R], F32))

        rstdB = E(nc.sbuf_tensor("rstdB", [128, R], F32))
        stats = E(nc.sbuf_tensor("stats", [1, 3072], F32))
        outstg = E(nc.sbuf_tensor("outstg", [128, 1024], F32))
        xw2 = E(nc.sbuf_tensor("xw2", [128, 16 * NGW], BF16))
        xab = xw2[:, :]
        xsqb = None
        gat = E(nc.sbuf_tensor("gat", [128, 832], F32))
        Sn = E(nc.sbuf_tensor("Sn", [128, 512], BF16))
        B16 = E(nc.sbuf_tensor("B16", [16, 128], BF16))
        Ball = E(nc.sbuf_tensor("Ball", [128, 128], BF16))
        BallT = E(nc.sbuf_tensor("BallT", [128, 128], BF16))
        zTb = E(nc.sbuf_tensor("zTb", [32, R], BF16))
        H8tok = E(nc.sbuf_tensor("H8tok", [128, 64], F32))
        dstok = E(nc.sbuf_tensor("dstok", [128, 64], F32))
        wnb = E(nc.sbuf_tensor("wnb", [128, 64], BF16))
        wfb = E(nc.sbuf_tensor("wfb", [128, 64], BF16))
        mulT = E(nc.sbuf_tensor("mulT", [32, R], BF16))
        A_bf = E(nc.sbuf_tensor("A_bf", [128, KT * 32], BF16))
        W8_bf = E(nc.sbuf_tensor("W8_bf", [128, KT * 8], BF16))
        Bm_bf = E(nc.sbuf_tensor("Bm_bf", [32, D], BF16))
        b_sb = E(nc.sbuf_tensor("b_sb", [32, 128], F32))
        b_col = E(nc.sbuf_tensor("b_col", [128, 32], F32))
        b8B = E(nc.sbuf_tensor("b8B", [128, 8], F32))
        b8r = E(nc.sbuf_tensor("b8r", [1, 8], F32))
        LiM = E(nc.sbuf_tensor("LiMs", [8, 32], F32))
        LiM_bf = E(nc.sbuf_tensor("LiM_bf", [8, 32], BF16))
        LiMs_bf = E(nc.sbuf_tensor("LiMsc_bf", [8, 32], BF16))
        LiMT_bf = E(nc.sbuf_tensor("LiMT_bf", [32, 8], BF16))
        idn = E(nc.sbuf_tensor("idn_sb", [128, 128], BF16))
        idnf = E(nc.sbuf_tensor("idnf_sb", [128, 128], F32))
        avgp = E(nc.sbuf_tensor("avgp_sb", [128, 128], BF16))
        EXPc = E(nc.sbuf_tensor("EXP_sb", [16, 512], BF16))
        balltc = E(nc.sbuf_tensor("balltc", [16, 128], BF16))
        Q4 = E(nc.sbuf_tensor("Q4_sb", [128, 4], BF16))
        oddm = E(nc.sbuf_tensor("oddm_sb", [128, 1], F32))
        onesc = E(nc.sbuf_tensor("onesc_sb", [1, 128], F32))
        pairP = E(nc.sbuf_tensor("pairP_sb", [128, 128], BF16))
        gmax = E(nc.sbuf_tensor("gmax", [1, 16], F32))
        scB = E(nc.sbuf_tensor("scB", [128, 2], F32))

        xsqb = wmbuf[:, 0:16 * NGW]
        meanB = outstg[:, :]
        dsT = outstg[0:8, :]
        H8T = rstdB[0:8, :]
        wfT = mulT[0:8, :]
        wslot = [wstage, xw2]
        tmpf = mstage[:, :].bitcast(F32)
        ekstrip = gat[:, 576:832].bitcast(BF16)

        pm = [E(nc.psum_tensor(f"pm{i}", [128, 512], F32)) for i in range(4)]
        pa = E(nc.psum_tensor("pa", [128, 512], F32))
        pb = E(nc.psum_tensor("pb", [128, 512], F32))
        pc = E(nc.psum_tensor("pc", [128, 512], F32))
        pbf = E(nc.psum_tensor("pbf", [128, 512], BF16))

        sem_names = ["S_cst", "S_xa0", "S_xa1", "S_xb0", "S_xb1", "S_res0", "S_res1",
                     "S_ccb", "S_ccm", "G_msk0", "G_msk1", "G_wst0", "G_wst1", "G_wbf0", "G_wbf1", "G_wbf2", "G_wbf3", "G_cmn", "G_cc", "G_cc2",
                     "SCQ0", "SCQ1", "CC0", "CC1", "CC2", "T", "V", "SC"]
        SEM = {nm: E(nc.semaphore(nm)) for nm in sem_names}
        DMA_SEMS = {"S_cst", "S_xa0", "S_xa1", "S_xb0", "S_xb1", "S_res0", "S_res1",
                    "S_ccb", "S_ccm", "G_msk0", "G_msk1", "G_wst0", "G_wst1", "G_wbf0", "G_wbf1", "G_wbf2", "G_wbf3", "G_cmn", "G_cc", "G_cc2",
                    "SCQ0", "SCQ1"}
        CC_SEMS = {"CC0", "CC1", "CC2"}

        block = E(nc.Block())

        PLAN = {e: [] for e in ENGS}
        CNT = {nm: 0 for nm in sem_names}
        CHK = {}

        nops = [0]

        def op(eng, emit, waits=(), inc=None, chk=None):
            nops[0] += 1
            if STOP_N is not None and nops[0] > STOP_N:
                return
            PLAN[eng].append((emit, list(waits), inc))
            if inc is not None:
                CNT[inc] += 16 if inc in DMA_SEMS else 1
                if chk is not None:
                    CHK[chk] = (inc, CNT[inc])

        def wc(name):
            if name not in CHK and STOP_N is not None:
                return ("S_cst", 0)
            return CHK[name]

        wm3 = d_wm.rearrange("p (g t n) -> p g t n", g=NG, t=KT)
        W3 = d_W.rearrange("p (g t n) -> p g t n", g=NG, t=KT)


        for dst, src in [(idn[:, :], d_idn), (idnf[:, :], d_idnf), (avgp[:, :], d_avgp),
                         (EXPc[:, :], d_EXP), (Q4[:, :], d_Q4), (oddm[:, :], d_oddm),
                         (onesc[:, :], d_onesc), (pairP[:, :], d_pairP),
                         (b_sb[:, :], d_b32), (LiM[:, :], d_LiM), (b8r[:, :], d_b8)]:
            op("sync", (lambda e, a=dst, b=src: e.dma_start(out=a, in_=b)), inc="S_cst")
        CHK["csts"] = ("S_cst", CNT["S_cst"])

        def em_mask(q):
            sl = q % 2
            def f(e):
                return e.dma_start(out=mstage[:, sl * 2048:(sl + 1) * 2048],
                                   in_=d_msk[:, q * 2048:(q + 1) * 2048],
                                   max_dma_last_dim=8192)
            return f
        if not SKIP_CC:
            op("gpsimd", lambda e: e.dma_start(out=cc_d_in.ap(), in_=d_b8[0:1, 0:1]),
               inc="G_cmn")
            op("gpsimd", lambda e: e.collective_compute(
                "AllGather", ALU.bypass, replica_groups=[list(range(NCORES))],
                ins=[cc_d_in.ap().opt()], outs=[cc_d_out.ap().opt()]),
                waits=[("G_cmn", 16)], inc="CC0")
        op("gpsimd", em_mask(0), inc="G_msk0", chk="mq0")
        op("gpsimd", em_mask(1), inc="G_msk1", chk="mq1")
        op("gpsimd", lambda e: e.dma_start(out=wstage[:, :], in_=wm3[:, 0, 0:16]),
           inc="G_wst0", chk="wst0h0")
        for sb_ in range(4):
            op("gpsimd", (lambda e, sb_=sb_: e.dma_start(
                out=Wbuf[:, sb_ * 8 * NGW:(sb_ + 1) * 8 * NGW],
                in_=W3[:, 0, 8 * sb_:8 * (sb_ + 1)])),
                inc=f"G_wbf{sb_}")
        op("gpsimd", lambda e: e.dma_start(out=A_bf[:, :], in_=d_A), inc="G_cmn")
        op("gpsimd", lambda e: e.dma_start(out=W8_bf[:, :], in_=d_W8), inc="G_cmn")
        op("gpsimd", lambda e: e.dma_start(out=Bm_bf[:, :], in_=d_Bm), inc="G_cmn", chk="cmn")

        def emit_mask_batch(q):
            sl = q % 2
            for t in range(4):
                ktg = 4 * q + t

                def em_red(e, t=t, ktg=ktg, sl=sl):
                    with nc.allow_low_precision(reason="mask block-sums <=32 exact in bf16"):
                        return e.tensor_reduce(
                            Sn[:, ktg * 16:(ktg + 1) * 16],
                            mstage[:, sl * 2048 + t * 512:sl * 2048 + (t + 1) * 512]
                                .rearrange("p (nb w) -> p nb w", w=32),
                            axis=AX.X, op=ALU.add)
                op("vector", em_red, waits=[wc(f"mq{q}")], inc="V")
            CHK[f"mrq{q}"] = ("V", CNT["V"])
            if q < 6:
                op("gpsimd", em_mask(q + 2), waits=[wc(f"mrq{q}")], inc=f"G_msk{q % 2}",
                   chk=f"mq{q+2}")

        mask_trigger = {1: 0, 3: 1, 5: 2, 7: 3, 9: 4, 11: 5, 13: 6, 15: 7}
        for kt in range(KT):
            pr = kt // 2
            P = pr % 2
            if kt % 2 == 0:
                waits = []
                if kt >= 4:
                    waits += [wc(f"sqA{pr-2}"), wc(f"xcv{pr-2}")]
                op("sync", (lambda e, kt=kt, P=P: e.dma_start(
                    out=xf[:, P * 2048:(P + 1) * 2048].rearrange("p (t n) -> p t n", t=2),
                    in_=d_xT[kt * 128:(kt + 2) * 128, :].rearrange("(t p) n -> p t n",
                                                                  p=128))),
                    waits=waits, inc=f"S_xa{P}", chk=f"xa{pr}")
                waits = [wc(f"xa{pr}")]
                op("vector", (lambda e, P=P, kt=kt: e.tensor_copy(
                    xnT[:, kt * R:(kt + 2) * R], xf[:, P * 2048:(P + 1) * 2048])),
                    waits=waits, inc="V", chk=f"xcv{pr}")
                wsq = list(waits)
                if kt >= 4:
                    wsq.append(wc(f"stT{kt-3}"))
                op("scalar", (lambda e, P=P: e.activation(
                    xsqb[:, P * 2048:(P + 1) * 2048], xf[:, P * 2048:(P + 1) * 2048],
                    AF.Square)),
                    waits=wsq, inc="SC", chk=f"sqA{pr}")
            st_ = (kt == 0)
            sp = (kt == KT - 1)
            xqoff = P * 2048 + (kt % 2) * 1024
            wst_ = [wc(f"sqA{pr}"), wc(f"xcv{pr}")]
            if kt == 0:
                wst_.append(wc("csts"))
            op("tensor", (lambda e, kt=kt, st_=st_, sp=sp: e.matmul(
                pm[0][:, :], avgp[:, :], xnT[:, kt * R:kt * R + 512], start=st_, stop=sp)),
                waits=wst_)
            op("tensor", (lambda e, kt=kt, st_=st_, sp=sp: e.matmul(
                pm[1][:, :], avgp[:, :], xnT[:, kt * R + 512:kt * R + 1024],
                start=st_, stop=sp)))
            op("tensor", (lambda e, xqoff=xqoff, st_=st_, sp=sp: e.matmul(
                pm[2][:, :], avgp[:, :], xsqb[:, xqoff:xqoff + 512], start=st_, stop=sp)))
            op("tensor", (lambda e, xqoff=xqoff, st_=st_, sp=sp: e.matmul(
                pm[3][:, :], avgp[:, :], xsqb[:, xqoff + 512:xqoff + 1024],
                start=st_, stop=sp)), inc="T", chk=f"stT{kt}")
            if kt in mask_trigger:
                emit_mask_batch(mask_trigger[kt])
            if kt == 18:
                op("tensor", lambda e: e.transpose(pa[:, 0:32], b_sb[:, :],
                                                   idnf[0:32, 0:32]),
                   waits=[wc("csts")], inc="T", chk="bcolT")
                op("tensor", lambda e: e.matmul(pc[:, 0:8], onesc[:, :], b8r[:, :],
                                                start=True, stop=True), inc="T", chk="b8T")
                op("vector", lambda e: e.tensor_copy(b_col[:, :], pa[:, 0:32]),
                   waits=[wc("bcolT")], inc="V", chk="bcolV")
                op("vector", lambda e: e.tensor_copy(b8B[:, :], pc[:, 0:8]),
                   waits=[wc("b8T")], inc="V", chk="b8V")
                for ktb in range(KT):
                    w2 = [wc("mrq7"), wc("b8V")] if ktb == 0 else []
                    op("tensor", (lambda e, ktb=ktb: e.matmul(
                        pc[0:16, 4 * ktb:4 * ktb + 4], Sn[:, ktb * 16:(ktb + 1) * 16],
                        Q4[:, :], start=True, stop=True)), waits=w2,
                        inc=("T" if ktb == KT - 1 else None),
                        chk=(f"b16T{KT-1}" if ktb == KT - 1 else None))
                op("vector", lambda e: e.tensor_scalar(B16[:, :], pc[0:16, 0:128], 0.0,
                                                       None, op0=ALU.is_gt),
                   waits=[wc(f"b16T{KT-1}")], inc="V", chk="bthr")
                op("sync", lambda e: e.dma_start(out=cc_b_in.ap(), in_=B16[:, :]),
                   waits=[wc("bthr")], inc="S_ccb", chk="ccbin")
                if SKIP_CC:
                    def em_ag(e):
                        return e.dma_start(out=cc_b_out.ap()[0:16, :], in_=cc_b_in.ap())
                    op("gpsimd", em_ag, waits=[wc("ccbin")], inc="CC1", chk="ag")
                else:
                    op("gpsimd", lambda e: e.collective_compute(
                        "AllGather", ALU.bypass, replica_groups=[list(range(NCORES))],
                        ins=[cc_b_in.ap().opt()], outs=[cc_b_out.ap().opt()]),
                        waits=[wc("ccbin")], inc="CC1", chk="ag")
                op("gpsimd", lambda e: e.dma_start(out=Ball[:, :], in_=cc_b_out.ap()),
                   waits=[("CC1", 1)], inc="G_cc", chk="ballin")

        op("vector", lambda e: e.tensor_copy(stats[0:1, 0:512], pm[0][0:1, :]),
           waits=[wc(f"stT{KT-1}")], inc="V")
        op("vector", lambda e: e.tensor_copy(stats[0:1, 512:1024], pm[1][0:1, :]), inc="V")
        op("vector", lambda e: e.tensor_copy(stats[0:1, 1024:1536], pm[2][0:1, :]), inc="V")
        op("vector", lambda e: e.tensor_copy(stats[0:1, 1536:2048], pm[3][0:1, :]), inc="V",
           chk="stev")
        op("vector", lambda e: e.tensor_tensor(stats[0:1, 2048:3072], stats[0:1, 0:1024],
                                               stats[0:1, 0:1024], op=ALU.mult), inc="V")
        op("vector", lambda e: e.tensor_tensor(stats[0:1, 1024:2048], stats[0:1, 1024:2048],
                                               stats[0:1, 2048:3072], op=ALU.subtract),
           inc="V")
        op("vector", lambda e: e.tensor_scalar(stats[0:1, 1024:2048], stats[0:1, 1024:2048],
                                               EPS_LN, None, op0=ALU.add), inc="V",
           chk="varV")
        op("scalar", lambda e: e.activation(stats[0:1, 2048:3072], stats[0:1, 1024:2048],
                                            AF.Ln),
           waits=[wc("varV")], inc="SC")
        op("scalar", lambda e: e.activation(stats[0:1, 1024:2048], stats[0:1, 2048:3072],
                                            AF.Exp, scale=-0.5), inc="SC", chk="rstdS")
        op("tensor", lambda e: e.matmul(pa[:, :], onesc[:, :], stats[0:1, 0:512],
                                        start=True, stop=True),
           waits=[wc("stev"), wc("bcolV")], inc="T", chk="bc0")
        op("tensor", lambda e: e.matmul(pb[:, :], onesc[:, :], stats[0:1, 512:1024],
                                        start=True, stop=True), inc="T", chk="bc1")
        op("vector", lambda e: e.tensor_copy(meanB[:, 0:512], pa[:, :]),
           waits=[wc("bc0")], inc="V", chk="bc0V")
        op("vector", lambda e: e.tensor_copy(meanB[:, 512:1024], pb[:, :]),
           waits=[wc("bc1")], inc="V", chk="bc1V")
        op("tensor", lambda e: e.matmul(pa[:, :], onesc[:, :], stats[0:1, 1024:1536],
                                        start=True, stop=True),
           waits=[wc("rstdS"), wc("bc0V")], inc="T", chk="bc2")
        op("tensor", lambda e: e.matmul(pb[:, :], onesc[:, :], stats[0:1, 1536:2048],
                                        start=True, stop=True),
           waits=[wc("bc1V")], inc="T", chk="bc3")
        op("vector", lambda e: e.tensor_copy(rstdB[:, 0:512], pa[:, :]),
           waits=[wc("bc2")], inc="V", chk="bc2V")
        op("vector", lambda e: e.tensor_copy(rstdB[:, 512:1024], pb[:, :]),
           waits=[wc("bc3")], inc="V", chk="bc3V")

        op("tensor", lambda e: e.transpose(pbf[:, 0:128], Ball[:, :], idn[:, :]),
           waits=[wc("ballin")], inc="T", chk="ballT")
        op("vector", lambda e: e.tensor_copy(BallT[:, :], pbf[:, 0:128]),
           waits=[wc("ballT")], inc="V", chk="ballTV")
        for c in range(8):
            bank = pa if c % 2 == 0 else pb
            wdma = [wc("ballTV")]
            if c >= 1:
                wdma.append(wc(f"ekmm{c-1}"))
            op("sync", (lambda e, c=c: e.dma_start(
                out=balltc[:, :], in_=BallT[16 * c:16 * (c + 1), :])),
                waits=wdma, inc="S_ccb", chk=f"btc{c}")
            waits = [wc(f"btc{c}")]
            if c >= 2:
                waits.append(wc(f"ekstrip{c-2}"))
            if c < 2:
                waits.append(wc("bc2V") if c == 0 else wc("bc3V"))
            op("tensor", (lambda e, bank=bank: e.matmul(
                bank[:, :], balltc[:, :], EXPc[:, :],
                start=True, stop=True)), waits=waits, inc="T", chk=f"ekmm{c}")
            wv = [wc(f"ekmm{c}")]
            if c >= 1:
                wv.append(wc(f"ektr{c-1}_3"))
            op("vector", (lambda e, bank=bank: e.tensor_copy(
                ekstrip[:, 0:512], bank[:, :])),
                waits=wv, inc="V", chk=f"ekstrip{c}")
            for t in range(4):
                wt = [wc(f"ekstrip{c}")]
                if c > 0 and t == 0:
                    wt.append(wc(f"ekcol{c-1}"))
                op("tensor", (lambda e, t=t: e.transpose(
                    pbf[:, t * 128:(t + 1) * 128],
                    ekstrip[:, t * 128:(t + 1) * 128], idn[:, :])),
                    waits=wt, inc="T", chk=f"ektr{c}_{t}")
            op("vector", (lambda e, c=c: e.tensor_copy(
                EKcol[:, c * 512:(c + 1) * 512], pbf[:, :])),
                waits=[wc(f"ektr{c}_3")], inc="V", chk=f"ekcol{c}")

        for kt in range(KT):
            wv = [wc("bc3V"), wc("bc2V")]
            if kt == 0:
                wv.append(wc("mrq7"))
            op("vector", (lambda e, kt=kt: e.tensor_tensor(
                tmpf[:, 0:R], xnT[:, kt * R:(kt + 1) * R], meanB[:, :], op=ALU.subtract)),
                waits=wv, inc="V")
            op("vector", (lambda e, kt=kt: e.tensor_tensor(
                xnT[:, kt * R:(kt + 1) * R], tmpf[:, 0:R], rstdB[:, :], op=ALU.mult)),
                inc="V", chk=f"xn{kt}")
            Xh = kt // 16
            op("vector", (lambda e, kt=kt, Xh=Xh: e.tensor_tensor(
                wmbuf[:, kt * NGW:(kt + 1) * NGW].rearrange("p (nb w) -> p nb w", w=32),
                wslot[Xh % 2][:, (kt % 16) * NGW:((kt % 16) + 1) * NGW]
                    .rearrange("p (nb w) -> p nb w", w=32),
                EKcol[:, kt * 128:kt * 128 + 8].unsqueeze(-1).broadcast_to([128, 8, 32]),
                op=ALU.mult)),
                waits=([wc(f"stT{KT-1}")] if kt == 0 else []) +
                      [(f"G_wst{Xh % 2}", 16 * (Xh // 2 + 1)), wc(f"ekcol{kt//4}")],
                inc="V", chk=f"vwm0k{kt}")
            sp = (kt == KT - 1)
            wz = [wc(f"xn{kt}")]
            if kt == 0:
                wz += [wc("cmn"), wc("bc2V"), wc("bc3V"), wc("ekcol7"), wc("bthr")]
            op("tensor", (lambda e, kt=kt, sp=sp: e.matmul(
                pa[0:32, :], A_bf[:, kt * 32:(kt + 1) * 32], xnT[:, kt * R:kt * R + 512],
                start=(kt == 0), stop=sp, skip_group_check=True)), waits=wz)
            op("tensor", (lambda e, kt=kt, sp=sp: e.matmul(
                pb[0:32, :], A_bf[:, kt * 32:(kt + 1) * 32],
                xnT[:, kt * R + 512:kt * R + 1024],
                start=(kt == 0), stop=sp, skip_group_check=True)))
            op("tensor", (lambda e, kt=kt, sp=sp: e.matmul(
                pc[0:8, :], W8_bf[:, kt * 8:(kt + 1) * 8], xnT[:, kt * R:kt * R + 512],
                start=(kt == 0), stop=sp, skip_group_check=True)),
                inc=("T" if sp else None), chk=("zsw" if sp else None))
            for j in (0, 1):
                pmA = pm[2 * j]
                pmB = pm[2 * j + 1]
                wj = [wc(f"vwm0k{kt}")]
                if kt == 0:
                    wj.append(wc("stev"))
                if kt % 8 == 0:
                    wj.append((f"G_wbf{kt//8}", 16))
                wof = kt * NGW + j * 128
                mkchk = (j == 1 and kt % 8 == 7)
                op("tensor", (lambda e, pmA=pmA, wof=wof, kt=kt, st_=(kt == 0): e.matmul(
                    pmA[:, :], wmbuf[:, wof:wof + 128], xnT[:, kt * R:kt * R + 512],
                    start=st_, stop=False, skip_group_check=True)), waits=wj)
                op("tensor", (lambda e, pmB=pmB, wof=wof, kt=kt, st_=(kt == 0): e.matmul(
                    pmB[:, :], wmbuf[:, wof:wof + 128], xnT[:, kt * R + 512:kt * R + 1024],
                    start=st_, stop=False, skip_group_check=True)),
                    inc=("T" if mkchk else None),
                    chk=(f"tWMj1k{kt}" if mkchk else None))
                op("tensor", (lambda e, pmA=pmA, wof=wof, kt=kt: e.matmul(
                    pmA[:, :], Wbuf[:, wof:wof + 128], xnT[:, kt * R:kt * R + 512],
                    start=False, stop=False, skip_group_check=True)))
                op("tensor", (lambda e, pmB=pmB, wof=wof, kt=kt: e.matmul(
                    pmB[:, :], Wbuf[:, wof:wof + 128], xnT[:, kt * R + 512:kt * R + 1024],
                    start=False, stop=False, skip_group_check=True)),
                    inc=("T" if mkchk else None),
                    chk=(f"tWj1k{kt}" if mkchk else None))

        op("gpsimd", lambda e: e.dma_start(out=xw2[:, :], in_=wm3[:, 0, 16:32]),
           waits=[wc(f"stT{KT-1}")], inc="G_wst1", chk="wst0h1")
        op("gpsimd", lambda e: e.dma_start(out=wstage[:, :], in_=wm3[:, 1, 0:16]),
           waits=[wc("vwm0k15")], inc="G_wst0", chk="wst1h0")
        op("gpsimd", lambda e: e.dma_start(out=xw2[:, :], in_=wm3[:, 1, 16:32]),
           waits=[wc(f"vwm0k{KT-1}")], inc="G_wst1", chk="wst1h1")

        op("vector", lambda e: e.tensor_copy(zTb[:, 0:512], pa[0:32, :]),
           waits=[wc("zsw")], inc="V", chk="zloV")
        op("vector", lambda e: e.tensor_copy(zTb[:, 512:1024], pb[0:32, :]), inc="V",
           chk="zhiV")
        op("vector", lambda e: e.tensor_copy(H8T[:, 0:512], pc[0:8, :]), inc="V", chk="zevV")
        CHK["zevS"] = CHK["zevV"]
        for kt in range(KT):
            sp = (kt == KT - 1)
            waits = [wc("zevV")] if kt == 0 else []
            op("tensor", (lambda e, kt=kt, sp=sp: e.matmul(
                pc[0:8, :], W8_bf[:, kt * 8:(kt + 1) * 8],
                xnT[:, kt * R + 512:kt * R + 1024],
                start=(kt == 0), stop=sp, skip_group_check=True)),
                waits=waits, inc=("T" if sp else None), chk=("h8hi" if sp else None))
        op("vector", lambda e: e.tensor_copy(H8T[:, 512:1024], pc[0:8, :]),
           waits=[wc("h8hi")], inc="V", chk="h8hiV")

        op("scalar", lambda e: e.activation(LiM_bf[:, :], LiM[:, :], AF.Identity),
           waits=[wc("csts")], inc="SC", chk="limS")
        op("scalar", lambda e: e.activation(LiMs_bf[:, :], LiM[:, :], AF.Identity,
                                            scale=SCALING), inc="SC", chk="limsS")
        op("tensor", lambda e: e.transpose(pbf[0:32, 0:8], LiM_bf[:, :], idn[0:8, 0:8]),
           waits=[wc("limS"), wc("ekcol7")], inc="T", chk="limT")
        op("vector", lambda e: e.tensor_copy(LiMT_bf[:, :], pbf[0:32, 0:8]),
           waits=[wc("limT")], inc="V", chk="limV")

        op("tensor", lambda e: e.matmul(pa[0:8, :], LiMT_bf[:, :], zTb[:, 0:512],
                                        start=True, stop=True),
           waits=[wc("zevV"), wc("zevS"), wc("limV")], inc="T", chk="ds0")
        op("tensor", lambda e: e.matmul(pb[0:8, :], LiMT_bf[:, :], zTb[:, 512:1024],
                                        start=True, stop=True), inc="T", chk="ds1")
        op("vector", lambda e: e.tensor_copy(dsT[:, 0:512], pa[0:8, :]),
           waits=[wc("ds0")], inc="V")
        op("vector", lambda e: e.tensor_copy(dsT[:, 512:1024], pb[0:8, :]),
           waits=[wc("ds1")], inc="V", chk="dsV")

        for mt in range(8):
            waits = [wc("dsV"), wc("h8hiV")] if mt == 0 else []
            op("tensor", (lambda e, mt=mt: e.transpose(
                pa[:, mt * 8:(mt + 1) * 8], H8T[0:8, mt * 128:(mt + 1) * 128],
                idnf[0:8, 0:8])), waits=waits, inc="T")
            op("tensor", (lambda e, mt=mt: e.transpose(
                pb[:, mt * 8:(mt + 1) * 8], dsT[0:8, mt * 128:(mt + 1) * 128],
                idnf[0:8, 0:8])), inc="T", chk=("tokT" if mt == 7 else None))
        op("vector", lambda e: e.tensor_copy(H8tok[:, :], pa[:, 0:64]),
           waits=[wc("tokT")], inc="V")
        op("vector", lambda e: e.tensor_copy(dstok[:, :], pb[:, 0:64]), inc="V", chk="tokV")
        op("vector", lambda e: e.tensor_tensor(
            H8tok[:, :].rearrange("p (mt e) -> p mt e", e=8),
            H8tok[:, :].rearrange("p (mt e) -> p mt e", e=8),
            b8B[:, :].unsqueeze(1).broadcast_to([128, 8, 8]), op=ALU.add),
            waits=[wc("b8V")], inc="V", chk="h8bias")

        op("vector", lambda e: e.reduce_max(gat[:, 0:1], H8tok[:, :], axis=AX.X,
                                            apply_absolute_value=True),
           waits=[wc("h8bias")], inc="V")
        op("vector", lambda e: e.reduce_max(gat[:, 1:2], dstok[:, :], axis=AX.X,
                                            apply_absolute_value=True),
           waits=[wc("tokV")], inc="V")
        op("vector", lambda e: e.tensor_tensor(gat[:, 0:2], gat[:, 0:2],
                                               oddm[:, :].broadcast_to([128, 2]),
                                               op=ALU.mult),
           waits=[wc("csts")], inc="V", chk="rmax")
        op("tensor", lambda e: e.transpose(pc[0:2, 0:128], gat[:, 0:2], idnf[:, :]),
           waits=[wc("rmax"), wc("bthr")], inc="T", chk="mxT")
        op("vector", lambda e: e.reduce_max(gat[0:2, 8:9], pc[0:2, 0:128], axis=AX.X),
           waits=["mxT"], inc="V", chk="mxV")
        op("sync", lambda e: e.dma_start(out=cc_m_in.ap(), in_=gat[0:2, 8:9]),
           waits=[wc("mxV")], inc="S_ccm", chk="ccmin")
        if SKIP_CC:
            op("gpsimd", lambda e: e.dma_start(out=cc_m_out.ap(), in_=cc_m_in.ap()),
               waits=[wc("ccmin")], inc="CC2", chk="ar")
        else:
            op("gpsimd", lambda e: e.collective_compute(
                "AllReduce", ALU.max, replica_groups=[list(range(NCORES))],
                ins=[cc_m_in.ap().opt()], outs=[cc_m_out.ap().opt()]),
                waits=[wc("ccmin")], inc="CC2", chk="ar")
        op("gpsimd", lambda e: e.dma_start(out=gmax[0:1, 0:2], in_=cc_m_out.ap()),
           waits=[("CC2", 1)], inc="G_cc2", chk="gmaxin")

        op("vector", lambda e: e.tensor_scalar(gmax[0:1, 2:4], gmax[0:1, 0:2], EPS_G, None,
                                               op0=ALU.max),
           waits=[wc("gmaxin")], inc="V")
        op("vector", lambda e: e.reciprocal(gmax[0:1, 4:6], gmax[0:1, 2:4]), inc="V")
        op("vector", lambda e: e.tensor_scalar(gmax[0:1, 6:8], gmax[0:1, 4:6], GAMMA, None,
                                               op0=ALU.mult), inc="V", chk="scV")
        op("tensor", lambda e: e.matmul(pc[:, 0:2], onesc[:, :], gmax[0:1, 6:8],
                                        start=True, stop=True),
           waits=[wc("scV")], inc="T", chk="scT")
        op("vector", lambda e: e.tensor_copy(scB[:, :], pc[:, 0:2]),
           waits=["scT"], inc="V", chk="scBV")

        GA = gat

        def V3(ap):
            return ap.rearrange("p (mt e) -> p mt e", e=8)

        def bc8(ap):
            return ap.unsqueeze(-1).broadcast_to([128, 8, 8])

        op("vector", lambda e: e.tensor_scalar(GA[:, 64:128], H8tok[:, :], scB[:, 0:1], None,
                                               op0=ALU.mult),
           waits=[wc("scBV")], inc="V")
        op("vector", lambda e: e.scalar_tensor_tensor(GA[:, 128:192], dstok[:, :],
                                                      scB[:, 1:2], GA[:, 64:128],
                                                      op0=ALU.mult, op1=ALU.add),
           inc="V", chk="logit")
        op("scalar", lambda e: e.activation(GA[:, 192:256], GA[:, 128:192], AF.Exp),
           waits=[wc("logit")], inc="SC", chk="expS")
        op("vector", lambda e: e.tensor_reduce(GA[:, 256:264], V3(GA[:, 192:256]),
                                               axis=AX.X, op=ALU.add),
           waits=[wc("expS")], inc="V")
        op("vector", lambda e: e.reciprocal(GA[:, 336:344], GA[:, 256:264]), inc="V")
        op("vector", lambda e: e.tensor_tensor(V3(GA[:, 320:384]), V3(GA[:, 192:256]),
                                               bc8(GA[:, 336:344]), op=ALU.mult),
           inc="V", chk="probs")
        op("vector", lambda e: e.tensor_reduce(GA[:, 272:280], V3(GA[:, 320:384]),
                                               axis=AX.X, op=ALU.max), inc="V")
        op("vector", lambda e: e.tensor_tensor(V3(GA[:, 384:448]), V3(GA[:, 320:384]),
                                               bc8(GA[:, 272:280]), op=ALU.is_ge), inc="V")
        op("vector", lambda e: e.tensor_reduce(GA[:, 264:272], V3(GA[:, 384:448]),
                                               axis=AX.X, op=ALU.add), inc="V")
        op("vector", lambda e: e.scalar_tensor_tensor(GA[:, 448:512], GA[:, 384:448], -1e30,
                                                      GA[:, 320:384], op0=ALU.mult,
                                                      op1=ALU.add), inc="V")
        op("vector", lambda e: e.tensor_reduce(GA[:, 280:288], V3(GA[:, 448:512]),
                                               axis=AX.X, op=ALU.max), inc="V")
        op("vector", lambda e: e.tensor_scalar(GA[:, 288:296], GA[:, 264:272], 2.0, None,
                                               op0=ALU.is_ge), inc="V")
        op("vector", lambda e: e.tensor_tensor(GA[:, 296:304], GA[:, 272:280],
                                               GA[:, 280:288], op=ALU.subtract), inc="V")
        op("vector", lambda e: e.tensor_tensor(GA[:, 296:304], GA[:, 296:304],
                                               GA[:, 288:296], op=ALU.mult), inc="V")
        op("vector", lambda e: e.tensor_tensor(GA[:, 304:312], GA[:, 296:304],
                                               GA[:, 280:288], op=ALU.add), inc="V",
           chk="thr")
        op("vector", lambda e: e.tensor_tensor(V3(GA[:, 512:576]), bc8(GA[:, 304:312]),
                                               V3(GA[:, 320:384]), op=ALU.subtract),
           inc="V", chk="preV")
        op("scalar", lambda e: e.activation(GA[:, 576:640], GA[:, 512:576], AF.Exp,
                                            scale=2.0),
           waits=[wc("preV"), wc("ekcol7")], inc="SC", chk="qS")
        op("vector", lambda e: e.tensor_scalar(GA[:, 640:704], GA[:, 576:640], 1.0, None,
                                               op0=ALU.add),
           waits=[wc("qS")], inc="V")
        op("vector", lambda e: e.reciprocal(GA[:, 704:768], GA[:, 640:704]), inc="V")
        op("vector", lambda e: e.tensor_tensor(GA[:, 768:832], GA[:, 320:384],
                                               GA[:, 704:768], op=ALU.mult), inc="V")
        op("vector", lambda e: e.tensor_reduce(GA[:, 312:320], V3(GA[:, 768:832]),
                                               axis=AX.X, op=ALU.add), inc="V")
        op("vector", lambda e: e.tensor_scalar(GA[:, 312:320], GA[:, 312:320], 1e-9, None,
                                               op0=ALU.add), inc="V")
        op("vector", lambda e: e.reciprocal(GA[:, 328:336], GA[:, 312:320]), inc="V")
        op("vector", lambda e: e.tensor_tensor(V3(wnb[:, :]), V3(GA[:, 768:832]),
                                               bc8(GA[:, 328:336]), op=ALU.mult),
           inc="V", chk="wnV")

        ev_cnt = {}
        out_cnt = {}

        def plan_main_block(j):
            g = j // 2
            par = j % 2
            pmA = pm[2 * par]
            pmB = pm[2 * par + 1]
            gbase = (g % 2) * (KT * NGW)
            for kt in range(KT):
                waits = [wc(f"vwm{g}k{kt}")]
                if kt == 0:
                    waits.append(("V", ev_cnt[j - 2]))
                if kt % 8 == 0:
                    waits.append((f"G_wbf{kt//8}", 16 * (g + 1)))
                wof = gbase + kt * NGW + par * 128
                wofW = kt * NGW + par * 128
                mkchk = (par == 1 and kt % 8 == 7)
                op("tensor", (lambda e, pmA=pmA, wof=wof, kt=kt, st_=(kt == 0): e.matmul(
                    pmA[:, :], wmbuf[:, wof:wof + 128], xnT[:, kt * R:kt * R + 512],
                    start=st_, stop=False, skip_group_check=True)), waits=waits)
                op("tensor", (lambda e, pmB=pmB, wof=wof, kt=kt, st_=(kt == 0): e.matmul(
                    pmB[:, :], wmbuf[:, wof:wof + 128], xnT[:, kt * R + 512:kt * R + 1024],
                    start=st_, stop=False, skip_group_check=True)),
                    inc=("T" if mkchk else None),
                    chk=(f"tWMj{j}k{kt}" if mkchk else None))
                op("tensor", (lambda e, pmA=pmA, wofW=wofW, kt=kt: e.matmul(
                    pmA[:, :], Wbuf[:, wofW:wofW + 128], xnT[:, kt * R:kt * R + 512],
                    start=False, stop=False, skip_group_check=True)))
                op("tensor", (lambda e, pmB=pmB, wofW=wofW, kt=kt: e.matmul(
                    pmB[:, :], Wbuf[:, wofW:wofW + 128], xnT[:, kt * R + 512:kt * R + 1024],
                    start=False, stop=False, skip_group_check=True)),
                    inc=("T" if mkchk else None),
                    chk=(f"tWj{j}k{kt}" if mkchk else None))

        def plan_delta_and_evac(j):
            par = j % 2
            pmA = pm[2 * par]
            pmB = pm[2 * par + 1]
            op("tensor", (lambda e, pmA=pmA, j=j: e.matmul(
                pmA[:, :], Bm_bf[:, j * 128:(j + 1) * 128], mulT[:, 0:512],
                start=False, stop=True, skip_group_check=True)),
                waits=[wc("vmul")], inc="T")
            op("tensor", (lambda e, pmB=pmB, j=j: e.matmul(
                pmB[:, :], Bm_bf[:, j * 128:(j + 1) * 128], mulT[:, 512:1024],
                start=False, stop=True, skip_group_check=True)), inc="T", chk=f"tDj{j}")
            rsl = j % 2
            wres = [wc(f"vwm0k{KT-1}")] if j < 2 else [("V", ev_cnt[j - 2])]
            if j < 2:
                wres.append(wc(f"xn{KT-1}"))
            op("sync", (lambda e, j=j, rsl=rsl: e.dma_start(
                out=xf[:, rsl * R:(rsl + 1) * R], in_=d_xT[j * 128:(j + 1) * 128, :])),
                waits=wres, inc=f"S_res{rsl}", chk=f"res{j}")
            for h in range(2):
                bank = pmA if h == 0 else pmB
                osl = (2 * j + h) % 2
                waits = [wc(f"tDj{j}"), wc(f"res{j}")]
                if 2 * j + h >= 2:
                    prev = 2 * j + h - 2
                    waits.append(out_cnt[(prev // 2, prev % 2)])
                op("vector", (lambda e, bank=bank, osl=osl, rsl=rsl, j=j, h=h:
                    e.scalar_tensor_tensor(
                        outstg[:, osl * 512:(osl + 1) * 512], bank[:, :], b_col[:, j:j + 1],
                        xf[:, rsl * R + h * 512:rsl * R + (h + 1) * 512],
                        op0=ALU.add, op1=ALU.add)),
                    waits=waits, inc="V", chk=f"ev{j}_{h}")
                op("scalar", (lambda e, j=j, h=h, osl=osl: e.dma_start(
                    out=d_out[j * 128:(j + 1) * 128, h * 512:(h + 1) * 512],
                    in_=outstg[:, osl * 512:(osl + 1) * 512])),
                    waits=[wc(f"ev{j}_{h}")], inc=f"SCQ{osl}")
                out_cnt[(j, h)] = (f"SCQ{osl}", CNT[f"SCQ{osl}"])
            ev_cnt[j] = CHK[f"ev{j}_1"][1]

        def plan_vwm(g):
            base = (g % 2) * (KT * NGW)
            for kt in range(KT):
                X = 2 * g + kt // 16
                waits = [(f"G_wst{X % 2}", 16 * (X // 2 + 1))]
                if g >= 2:
                    waits.append(wc(f"tWMj{2*g-3}k{8*(kt//8)+7}"))
                op("vector", (lambda e, g=g, kt=kt, base=base, X=X: e.tensor_tensor(
                    wmbuf[:, base + kt * NGW:base + (kt + 1) * NGW]
                        .rearrange("p (nb w) -> p nb w", w=32),
                    wslot[X % 2][:, (kt % 16) * NGW:((kt % 16) + 1) * NGW]
                        .rearrange("p (nb w) -> p nb w", w=32),
                    EKcol[:, kt * 128 + 8 * g:kt * 128 + 8 * g + 8]
                        .unsqueeze(-1).broadcast_to([128, 8, 32]),
                    op=ALU.mult)),
                    waits=waits, inc="V", chk=f"vwm{g}k{kt}")

        for mt in range(8):
            waits = [wc("wnV")] if mt == 0 else []
            op("tensor", (lambda e, mt=mt: e.matmul(
                pa[:, mt * 8:(mt + 1) * 8], pairP[:, :], wnb[:, mt * 8:(mt + 1) * 8],
                start=True, stop=True)), waits=waits,
                inc=("T" if mt == 7 else None), chk=("wf8" if mt == 7 else None))
        op("vector", lambda e: e.tensor_copy(wfb[:, :], pa[:, 0:64]),
           waits=[wc("wf8")], inc="V", chk="wfbV")
        for rnd in range(2):
            for t in range(4):
                mt = rnd * 4 + t
                waits = [wc("wfbV")]
                if rnd == 1 and t == 0:
                    waits.append(wc("wfT0V"))
                op("tensor", (lambda e, mt=mt, t=t: e.transpose(
                    pbf[0:8, t * 128:(t + 1) * 128], wfb[:, mt * 8:(mt + 1) * 8],
                    idn[:, :])),
                    waits=waits, inc=("T" if t == 3 else None),
                    chk=(f"wfTr{rnd}" if t == 3 else None))
            op("vector", (lambda e, rnd=rnd: e.tensor_copy(
                wfT[0:8, rnd * 512:(rnd + 1) * 512], pbf[0:8, 0:512])),
                waits=[wc(f"wfTr{rnd}")], inc="V", chk=f"wfT{rnd}V")
        op("tensor", lambda e: e.matmul(pa[0:32, :], LiMs_bf[:, :], wfT[0:8, 0:512],
                                        start=True, stop=True),
           waits=[wc("wfT1V"), wc("limsS")], inc="T", chk="pmx0")
        op("tensor", lambda e: e.matmul(pb[0:32, :], LiMs_bf[:, :], wfT[0:8, 512:1024],
                                        start=True, stop=True), inc="T", chk="pmx1")
        op("vector", lambda e: e.tensor_tensor(mulT[:, 0:512], pa[0:32, :], zTb[:, 0:512],
                                               op=ALU.mult),
           waits=[wc("pmx0")], inc="V")
        op("vector", lambda e: e.tensor_tensor(mulT[:, 512:1024], pb[0:32, :],
                                               zTb[:, 512:1024], op=ALU.mult),
           waits=[wc("pmx1")], inc="V", chk="vmul")

        plan_delta_and_evac(0)
        plan_delta_and_evac(1)
        plan_vwm(1)
        plan_vwm(2)
        nb_eff = NB if NB_EFF is None else NB_EFF
        for j in range(2, nb_eff):
            plan_main_block(j)
            plan_delta_and_evac(j)
            if j % 2 == 1 and (j + 3) // 2 < NG:
                plan_vwm((j + 3) // 2)

        for g in range(1, (NB if NB_EFF is None else NB_EFF) // 2):
            for sb_ in range(4):
                op("gpsimd", (lambda e, g=g, sb_=sb_: e.dma_start(
                    out=Wbuf[:, sb_ * 8 * NGW:(sb_ + 1) * 8 * NGW],
                    in_=W3[:, g, 8 * sb_:8 * (sb_ + 1)])),
                    waits=[wc(f"tWj{2*g-1}k{8*sb_+7}")], inc=f"G_wbf{sb_}")
            if g + 1 < NG:
                op("gpsimd", (lambda e, g=g: e.dma_start(
                    out=wslot[0][:, :], in_=wm3[:, g + 1, 0:16])),
                   waits=[wc(f"vwm{g}k15")], inc="G_wst0", chk=f"wst{g+1}h0")
                op("gpsimd", (lambda e, g=g: e.dma_start(
                    out=wslot[1][:, :], in_=wm3[:, g + 1, 16:32])),
                   waits=[wc(f"vwm{g}k{KT-1}")], inc="G_wst1", chk=f"wst{g+1}h1")

        if debug_taps:
            tap_list = [
                ("t_xnT", xnT[:, :], [wc(f"xn{KT-1}")]),
                ("t_EKcol", EKcol[:, :], [wc("ekcol7")]),
                ("t_Ball", Ball[:, :], [wc("ballin")]),
                ("t_H8tok", H8tok[:, :], [wc("h8bias")]),
                ("t_dstok", dstok[:, :], [wc("tokV")]),
                ("t_wnb", wnb[:, :], [wc("wnV")]),
                ("t_mulT", mulT[:, :], [wc("vmul")]),
                ("t_gmax", gmax[0:1, 0:8], [wc("scV")]),
                ("t_Sn", Sn[:, :], [wc("mrq7")]),
                ("t_zT", zTb[:, :], [wc("zhiV")]),

            ]
            for nm, src, waits in tap_list:
                op("sync", (lambda e, nm=nm, src=src: e.dma_start(out=taps[nm], in_=src)),
                   waits=waits, inc="S_cst")

        PLAN["scalar"].append((None, [("SCQ0", CNT["SCQ0"]), ("SCQ1", CNT["SCQ1"])], None))
        PLAN["sync"].append((None, [("S_cst", CNT["S_cst"])], None))

        def make_runner(eng_name):
            own_sem = {"vector": "V", "scalar": "SC"}.get(eng_name)
            def run(engine):
                seen = {}
                own = 0
                for emit, waits, inc in PLAN[eng_name]:
                    if own_sem is not None and emit is not None and own > 0:
                        if seen.get(own_sem, 0) < own:
                            engine.wait_ge(SEM[own_sem], own)
                            seen[own_sem] = own
                    for wspec in waits:
                        if isinstance(wspec, str):
                            if wspec not in CHK and STOP_N is not None:
                                continue
                            semn, val = CHK[wspec]
                        else:
                            semn, val = wspec
                        if seen.get(semn, 0) < val:
                            engine.wait_ge(SEM[semn], val)
                            seen[semn] = val
                    if emit is None:
                        continue
                    inst = emit(engine)
                    if inc == own_sem:
                        own += 1
                    if inc is not None:
                        if inc in CC_SEMS:
                            if SKIP_CC:
                                inst.then_inc(SEM[inc], 16)
                            else:
                                inst.then_inc(SEM[inc])
                        elif inc in DMA_SEMS:
                            inst.then_inc(SEM[inc], 16)
                        else:
                            inst.then_inc(SEM[inc], 1)
            return run

        block.sync(make_runner("sync"))
        block.gpsimd(make_runner("gpsimd"))
        block.scalar(make_runner("scalar"))
        block.tensor(make_runner("tensor"))
        block.vector(make_runner("vector"))

    return nc


def _consts():
    idn = np.eye(128, dtype=ml_dtypes.bfloat16)
    idnf = np.eye(128, dtype=np.float32)
    avgp = np.zeros((128, 128), dtype=ml_dtypes.bfloat16)
    avgp[:, 0] = ml_dtypes.bfloat16(1.0 / 4096.0)
    EXP = np.zeros((16, 512), dtype=ml_dtypes.bfloat16)
    for kb in range(16):
        EXP[kb, kb * 32:(kb + 1) * 32] = 1
    Q4 = np.zeros((128, 4), dtype=ml_dtypes.bfloat16)
    for p in range(128):
        Q4[p, p // 32] = 1
    oddm = np.zeros((128, 1), np.float32)
    oddm[1::2, 0] = 1.0
    onesc = np.ones((1, 128), np.float32)
    pairP = np.zeros((128, 128), dtype=ml_dtypes.bfloat16)
    for m in range(128):
        pairP[m | 1, m] = 1
    return idn, idnf, avgp, EXP, Q4, oddm, onesc, pairP


def _tile_khost(M):
    K, C = M.shape
    return np.ascontiguousarray(M.reshape(32, 128, C).transpose(1, 0, 2).reshape(128, 32 * C))


def _tile_weight(Wmat):
    arr = Wmat.reshape(NG, NGW, KT, 128)
    return np.ascontiguousarray(arr.transpose(3, 0, 2, 1).reshape(128, NG * KT * NGW))


def kernel(L0_attn_data, L0_attn_weight, L0_attn_mask, L0_ffn_W, L0_ffn_b,
           L0_ffn_A, L0_ffn_Bmat, L0_ffn_LiMEs):
    X = np.ascontiguousarray(np.asarray(L0_attn_data, np.float32).reshape(2 * 4096, D))
    weight = np.asarray(L0_attn_weight, np.float32)
    mask = np.asarray(L0_attn_mask, np.int32)
    W = np.asarray(L0_ffn_W, np.float32)
    bvec = np.asarray(L0_ffn_b, np.float32)
    A = np.asarray(L0_ffn_A, np.float32)
    Bmat = np.ascontiguousarray(np.asarray(L0_ffn_Bmat, np.float32))
    LiMa = np.ascontiguousarray(np.asarray(L0_ffn_LiMEs, np.float32))

    idn, idnf, avgp, EXP, Q4, oddm, onesc, pairP = _consts()

    wmh = _tile_weight(weight)
    Wh = _tile_weight(W)
    Ah = _tile_khost(A)
    W8h = _tile_khost(np.ascontiguousarray(W[0:8, :].T))
    b32 = np.ascontiguousarray(bvec.reshape(32, 128))
    b8 = np.ascontiguousarray(bvec[0:8].reshape(1, 8))
    maskT = np.ascontiguousarray(mask.T)

    nc = build_nc(debug_taps=DEBUG_TAPS)

    in_maps = []
    for c in range(NCORES):
        xT_c = np.ascontiguousarray(X[c * R:(c + 1) * R, :].T)
        msk_c = _tile_khost(np.ascontiguousarray(maskT[:, c * 512:(c + 1) * 512]))
        in_maps.append({
            "xT": xT_c, "wmh": wmh, "Wh": Wh, "mskh": msk_c,
            "Ah": Ah, "W8h": W8h, "b32": b32, "b8": b8, "Bm": Bmat, "LiM": LiMa,
            "idn": idn, "idnf": idnf, "avgp": avgp, "EXPc": EXP, "Q4c": Q4,
            "oddm": oddm, "onesc": onesc, "pairP": pairP,
        })

    res = run_bass_kernel_spmd(nc, in_maps, core_ids=list(range(NCORES)))
    out = np.empty((2 * 4096, D), np.float32)
    for c in range(NCORES):
        out[c * R:(c + 1) * R, :] = np.asarray(res.results[c]["outT"]).T
    if DEBUG_TAPS:
        kernel.last_results = res
    return out.reshape(2, 4096, D)


# version 06
# speedup vs baseline: 1.3017x; 1.0115x over previous
import numpy as np
import ml_dtypes
import concourse.bass as bass
import concourse.mybir as mybir
from concourse.bass_utils import run_bass_kernel_spmd

F32 = mybir.dt.float32
BF16 = mybir.dt.bfloat16
I32 = mybir.dt.int32
AF = mybir.ActivationFunctionType
ALU = mybir.AluOpType
AX = mybir.AxisListType

NCORES = 8
D = 4096
R = 1024
KT = 32
NGW = 256
NG = D // NGW
NB = 32
EPS_LN = 1e-5
EPS_G = 1e-6
GAMMA = 0.5
SCALING = 16.0 / 32.0

DEBUG_TAPS = False
SKIP_CC = False
NB_EFF = None
PHASE1_ONLY = False
STOP_N = None

ENGS = ("sync", "gpsimd", "scalar", "tensor", "vector")


def build_nc(debug_taps=False):
    nc = bass.Bass("TRN2", target_bir_lowering=False, debug=False,
                   dynamic_dma_scratch_size=16384)

    d_xT = nc.dram_tensor("xT", [D, R], F32, kind="ExternalInput").ap()
    d_wm = nc.dram_tensor("wmh", [128, KT * D], F32, kind="ExternalInput").ap()
    d_W = nc.dram_tensor("Wh", [128, KT * D], F32, kind="ExternalInput").ap()
    d_msk = nc.dram_tensor("mskh", [128, KT * 512], I32, kind="ExternalInput").ap()
    d_A = nc.dram_tensor("Ah", [128, KT * 32], F32, kind="ExternalInput").ap()
    d_W8 = nc.dram_tensor("W8h", [128, KT * 8], F32, kind="ExternalInput").ap()
    d_b32 = nc.dram_tensor("b32", [32, 128], F32, kind="ExternalInput").ap()
    d_b8 = nc.dram_tensor("b8", [1, 8], F32, kind="ExternalInput").ap()
    d_Bm = nc.dram_tensor("Bm", [32, D], F32, kind="ExternalInput").ap()
    d_LiM = nc.dram_tensor("LiM", [8, 32], F32, kind="ExternalInput").ap()
    d_idn = nc.dram_tensor("idn", [128, 128], BF16, kind="ExternalInput").ap()
    d_idnf = nc.dram_tensor("idnf", [128, 128], F32, kind="ExternalInput").ap()
    d_avgp = nc.dram_tensor("avgp", [128, 128], BF16, kind="ExternalInput").ap()
    d_EXP = nc.dram_tensor("EXPc", [16, 512], BF16, kind="ExternalInput").ap()
    d_Q4 = nc.dram_tensor("Q4c", [128, 4], BF16, kind="ExternalInput").ap()
    d_oddm = nc.dram_tensor("oddm", [128, 1], F32, kind="ExternalInput").ap()
    d_onesc = nc.dram_tensor("onesc", [1, 128], F32, kind="ExternalInput").ap()
    d_pairP = nc.dram_tensor("pairP", [128, 128], BF16, kind="ExternalInput").ap()

    d_out = nc.dram_tensor("outT", [D, R], F32, kind="ExternalOutput").ap()

    cc_b_in = nc.dram_tensor("cc_b_in", [16, 128], BF16)
    cc_b_out = nc.dram_tensor("cc_b_out", [128, 128], BF16)
    cc_m_in = nc.dram_tensor("cc_m_in", [1, 2], F32)
    cc_m_out = nc.dram_tensor("cc_m_out", [1, 2], F32)
    cc_d_in = nc.dram_tensor("cc_d_in", [1, 1], F32)
    cc_d_out = nc.dram_tensor("cc_d_out", [8, 1], F32)

    taps = {}
    if debug_taps:
        for nm, shp, dt in [("t_xnT", [128, KT * R], BF16), ("t_EKcol", [128, KT * 128], BF16),
                            ("t_Ball", [128, 128], BF16), ("t_H8tok", [128, 64], F32),
                            ("t_dstok", [128, 64], F32), ("t_wnb", [128, 64], BF16),
                            ("t_mulT", [32, R], BF16), ("t_gmax", [1, 8], F32),
                            ("t_Sn", [128, 512], BF16), ("t_zT", [32, R], BF16)]:
            taps[nm] = nc.dram_tensor(nm, shp, dt, kind="ExternalOutput").ap()

    from contextlib import ExitStack
    with ExitStack() as st:
        E = st.enter_context

        xnT = E(nc.sbuf_tensor("xnT", [128, KT * R], BF16))
        wstage = E(nc.sbuf_tensor("wstage", [128, 16 * NGW], BF16))
        wmbuf = E(nc.sbuf_tensor("wmbuf", [128, 2 * KT * NGW], BF16))
        Wbuf = E(nc.sbuf_tensor("Wbuf", [128, KT * NGW], BF16))
        mstage = E(nc.sbuf_tensor("mstage", [128, 8 * 512], BF16))
        EKcol = E(nc.sbuf_tensor("EKcol", [128, KT * 128], BF16))
        xf = E(nc.sbuf_tensor("xf", [128, 4 * R], F32))

        rstdB = E(nc.sbuf_tensor("rstdB", [128, R], F32))
        stats = E(nc.sbuf_tensor("stats", [1, 3072], F32))
        outstg = E(nc.sbuf_tensor("outstg", [128, 1024], F32))
        xw2 = E(nc.sbuf_tensor("xw2", [128, 16 * NGW], BF16))
        xab = xw2[:, :]
        xsqb = None
        gat = E(nc.sbuf_tensor("gat", [128, 832], F32))
        Sn = E(nc.sbuf_tensor("Sn", [128, 512], BF16))
        B16 = E(nc.sbuf_tensor("B16", [16, 128], BF16))
        Ball = E(nc.sbuf_tensor("Ball", [128, 128], BF16))
        BallT = E(nc.sbuf_tensor("BallT", [128, 128], BF16))
        zTb = E(nc.sbuf_tensor("zTb", [32, R], BF16))
        H8tok = E(nc.sbuf_tensor("H8tok", [128, 64], F32))
        dstok = E(nc.sbuf_tensor("dstok", [128, 64], F32))
        wnb = E(nc.sbuf_tensor("wnb", [128, 64], BF16))
        wfb = E(nc.sbuf_tensor("wfb", [128, 64], BF16))
        mulT = E(nc.sbuf_tensor("mulT", [32, R], BF16))
        A_bf = E(nc.sbuf_tensor("A_bf", [128, KT * 32], BF16))
        W8_bf = E(nc.sbuf_tensor("W8_bf", [128, KT * 8], BF16))
        Bm_bf = E(nc.sbuf_tensor("Bm_bf", [32, D], BF16))
        b_sb = E(nc.sbuf_tensor("b_sb", [32, 128], F32))
        b_col = E(nc.sbuf_tensor("b_col", [128, 32], F32))
        b8B = E(nc.sbuf_tensor("b8B", [128, 8], F32))
        b8r = E(nc.sbuf_tensor("b8r", [1, 8], F32))
        LiM = E(nc.sbuf_tensor("LiMs", [8, 32], F32))
        LiM_bf = E(nc.sbuf_tensor("LiM_bf", [8, 32], BF16))
        LiMs_bf = E(nc.sbuf_tensor("LiMsc_bf", [8, 32], BF16))
        LiMT_bf = E(nc.sbuf_tensor("LiMT_bf", [32, 8], BF16))
        idn = E(nc.sbuf_tensor("idn_sb", [128, 128], BF16))
        idnf = E(nc.sbuf_tensor("idnf_sb", [128, 128], F32))
        avgp = E(nc.sbuf_tensor("avgp_sb", [128, 128], BF16))
        EXPc = E(nc.sbuf_tensor("EXP_sb", [16, 512], BF16))
        balltc = E(nc.sbuf_tensor("balltc", [16, 128], BF16))
        Q4 = E(nc.sbuf_tensor("Q4_sb", [128, 4], BF16))
        oddm = E(nc.sbuf_tensor("oddm_sb", [128, 1], F32))
        onesc = E(nc.sbuf_tensor("onesc_sb", [1, 128], F32))
        pairP = E(nc.sbuf_tensor("pairP_sb", [128, 128], BF16))
        gmax = E(nc.sbuf_tensor("gmax", [1, 16], F32))
        scB = E(nc.sbuf_tensor("scB", [128, 2], F32))

        xsqb = wmbuf[:, 0:16 * NGW]
        meanB = outstg[:, :]
        mb_bf = rstdB[:, 0:512].bitcast(BF16)
        rb_bf = rstdB[:, 512:1024].bitcast(BF16)
        dsT = outstg[0:8, :]
        H8T = rstdB[0:8, :]
        wfT = mulT[0:8, :]
        wslot = [wstage, xw2]
        tmpf = mstage[:, :].bitcast(F32)
        ekstrip = gat[:, 576:832].bitcast(BF16)

        pm = [E(nc.psum_tensor(f"pm{i}", [128, 512], F32)) for i in range(4)]
        pa = E(nc.psum_tensor("pa", [128, 512], F32))
        pb = E(nc.psum_tensor("pb", [128, 512], F32))
        pc = E(nc.psum_tensor("pc", [128, 512], F32))
        pbf = E(nc.psum_tensor("pbf", [128, 512], BF16))

        sem_names = ["S_cst", "S_xa0", "S_xa1", "S_xb0", "S_xb1", "S_res0", "S_res1",
                     "S_ccb", "S_ccm", "G_msk0", "G_msk1", "G_wst0", "G_wst1", "G_wbf0", "G_wbf1", "G_wbf2", "G_wbf3", "G_cmn", "G_cc", "G_cc2",
                     "SCQ0", "SCQ1", "CC0", "CC1", "CC2", "T", "V", "SC"]
        SEM = {nm: E(nc.semaphore(nm)) for nm in sem_names}
        DMA_SEMS = {"S_cst", "S_xa0", "S_xa1", "S_xb0", "S_xb1", "S_res0", "S_res1",
                    "S_ccb", "S_ccm", "G_msk0", "G_msk1", "G_wst0", "G_wst1", "G_wbf0", "G_wbf1", "G_wbf2", "G_wbf3", "G_cmn", "G_cc", "G_cc2",
                    "SCQ0", "SCQ1"}
        CC_SEMS = {"CC0", "CC1", "CC2"}

        block = E(nc.Block())

        PLAN = {e: [] for e in ENGS}
        CNT = {nm: 0 for nm in sem_names}
        CHK = {}

        nops = [0]

        def op(eng, emit, waits=(), inc=None, chk=None):
            nops[0] += 1
            if STOP_N is not None and nops[0] > STOP_N:
                return
            PLAN[eng].append((emit, list(waits), inc))
            if inc is not None:
                CNT[inc] += 16 if inc in DMA_SEMS else 1
                if chk is not None:
                    CHK[chk] = (inc, CNT[inc])

        def wc(name):
            if name not in CHK and STOP_N is not None:
                return ("S_cst", 0)
            return CHK[name]

        wm3 = d_wm.rearrange("p (g t n) -> p g t n", g=NG, t=KT)
        W3 = d_W.rearrange("p (g t n) -> p g t n", g=NG, t=KT)


        for dst, src in [(idn[:, :], d_idn), (idnf[:, :], d_idnf), (avgp[:, :], d_avgp),
                         (EXPc[:, :], d_EXP), (Q4[:, :], d_Q4), (oddm[:, :], d_oddm),
                         (onesc[:, :], d_onesc), (pairP[:, :], d_pairP),
                         (b_sb[:, :], d_b32), (LiM[:, :], d_LiM), (b8r[:, :], d_b8)]:
            op("sync", (lambda e, a=dst, b=src: e.dma_start(out=a, in_=b)), inc="S_cst")
        CHK["csts"] = ("S_cst", CNT["S_cst"])

        def em_mask(q):
            sl = q % 2
            def f(e):
                return e.dma_start(out=mstage[:, sl * 2048:(sl + 1) * 2048],
                                   in_=d_msk[:, q * 2048:(q + 1) * 2048],
                                   max_dma_last_dim=8192)
            return f
        if not SKIP_CC:
            op("gpsimd", lambda e: e.dma_start(out=cc_d_in.ap(), in_=d_b8[0:1, 0:1]),
               inc="G_cmn")
            op("gpsimd", lambda e: e.collective_compute(
                "AllGather", ALU.bypass, replica_groups=[list(range(NCORES))],
                ins=[cc_d_in.ap().opt()], outs=[cc_d_out.ap().opt()]),
                waits=[("G_cmn", 16)], inc="CC0")
        op("gpsimd", em_mask(0), inc="G_msk0", chk="mq0")
        op("gpsimd", em_mask(1), inc="G_msk1", chk="mq1")
        op("gpsimd", lambda e: e.dma_start(out=wstage[:, :], in_=wm3[:, 0, 0:16]),
           inc="G_wst0", chk="wst0h0")
        for sb_ in range(4):
            op("gpsimd", (lambda e, sb_=sb_: e.dma_start(
                out=Wbuf[:, sb_ * 8 * NGW:(sb_ + 1) * 8 * NGW],
                in_=W3[:, 0, 8 * sb_:8 * (sb_ + 1)])),
                inc=f"G_wbf{sb_}")
        op("gpsimd", lambda e: e.dma_start(out=A_bf[:, :], in_=d_A), inc="G_cmn")
        op("gpsimd", lambda e: e.dma_start(out=W8_bf[:, :], in_=d_W8), inc="G_cmn")
        op("gpsimd", lambda e: e.dma_start(out=Bm_bf[:, :], in_=d_Bm), inc="G_cmn", chk="cmn")

        def emit_mask_batch(q):
            sl = q % 2
            for t in range(4):
                ktg = 4 * q + t

                def em_red(e, t=t, ktg=ktg, sl=sl):
                    with nc.allow_low_precision(reason="mask block-sums <=32 exact in bf16"):
                        return e.tensor_reduce(
                            Sn[:, ktg * 16:(ktg + 1) * 16],
                            mstage[:, sl * 2048 + t * 512:sl * 2048 + (t + 1) * 512]
                                .rearrange("p (nb w) -> p nb w", w=32),
                            axis=AX.X, op=ALU.add)
                op("vector", em_red, waits=[wc(f"mq{q}")], inc="V")
            CHK[f"mrq{q}"] = ("V", CNT["V"])
            if q < 6:
                op("gpsimd", em_mask(q + 2), waits=[wc(f"mrq{q}")], inc=f"G_msk{q % 2}",
                   chk=f"mq{q+2}")

        mask_trigger = {1: 0, 3: 1, 5: 2, 7: 3, 9: 4, 11: 5, 13: 6, 15: 7}
        for kt in range(KT):
            pr = kt // 2
            P = pr % 2
            if kt % 2 == 0:
                waits = []
                if kt >= 4:
                    waits += [wc(f"sqA{pr-2}"), wc(f"xcv{pr-2}")]
                op("sync", (lambda e, kt=kt, P=P: e.dma_start(
                    out=xf[:, P * 2048:(P + 1) * 2048].rearrange("p (t n) -> p t n", t=2),
                    in_=d_xT[kt * 128:(kt + 2) * 128, :].rearrange("(t p) n -> p t n",
                                                                  p=128))),
                    waits=waits, inc=f"S_xa{P}", chk=f"xa{pr}")
                waits = [wc(f"xa{pr}")]
                op("vector", (lambda e, P=P, kt=kt: e.tensor_copy(
                    xnT[:, kt * R:(kt + 2) * R], xf[:, P * 2048:(P + 1) * 2048])),
                    waits=waits, inc="V", chk=f"xcv{pr}")
                wsq = list(waits)
                if kt >= 4:
                    wsq.append(wc(f"stT{kt-3}"))
                op("scalar", (lambda e, P=P: e.activation(
                    xsqb[:, P * 2048:(P + 1) * 2048], xf[:, P * 2048:(P + 1) * 2048],
                    AF.Square)),
                    waits=wsq, inc="SC", chk=f"sqA{pr}")
            st_ = (kt == 0)
            sp = (kt == KT - 1)
            xqoff = P * 2048 + (kt % 2) * 1024
            wst_ = [wc(f"sqA{pr}"), wc(f"xcv{pr}")]
            if kt == 0:
                wst_.append(wc("csts"))
            op("tensor", (lambda e, kt=kt, st_=st_, sp=sp: e.matmul(
                pm[0][:, :], avgp[:, :], xnT[:, kt * R:kt * R + 512], start=st_, stop=sp)),
                waits=wst_)
            op("tensor", (lambda e, kt=kt, st_=st_, sp=sp: e.matmul(
                pm[1][:, :], avgp[:, :], xnT[:, kt * R + 512:kt * R + 1024],
                start=st_, stop=sp)))
            op("tensor", (lambda e, xqoff=xqoff, st_=st_, sp=sp: e.matmul(
                pm[2][:, :], avgp[:, :], xsqb[:, xqoff:xqoff + 512], start=st_, stop=sp)))
            op("tensor", (lambda e, xqoff=xqoff, st_=st_, sp=sp: e.matmul(
                pm[3][:, :], avgp[:, :], xsqb[:, xqoff + 512:xqoff + 1024],
                start=st_, stop=sp)), inc="T", chk=f"stT{kt}")
            if kt in mask_trigger:
                emit_mask_batch(mask_trigger[kt])
            if kt == 18:
                op("tensor", lambda e: e.transpose(pa[:, 0:32], b_sb[:, :],
                                                   idnf[0:32, 0:32]),
                   waits=[wc("csts")], inc="T", chk="bcolT")
                op("tensor", lambda e: e.matmul(pc[:, 0:8], onesc[:, :], b8r[:, :],
                                                start=True, stop=True), inc="T", chk="b8T")
                op("vector", lambda e: e.tensor_copy(b_col[:, :], pa[:, 0:32]),
                   waits=[wc("bcolT")], inc="V", chk="bcolV")
                op("vector", lambda e: e.tensor_copy(b8B[:, :], pc[:, 0:8]),
                   waits=[wc("b8T")], inc="V", chk="b8V")
                for ktb in range(KT):
                    w2 = [wc("mrq7"), wc("b8V")] if ktb == 0 else []
                    op("tensor", (lambda e, ktb=ktb: e.matmul(
                        pc[0:16, 4 * ktb:4 * ktb + 4], Sn[:, ktb * 16:(ktb + 1) * 16],
                        Q4[:, :], start=True, stop=True)), waits=w2,
                        inc=("T" if ktb == KT - 1 else None),
                        chk=(f"b16T{KT-1}" if ktb == KT - 1 else None))
                op("vector", lambda e: e.tensor_scalar(B16[:, :], pc[0:16, 0:128], 0.0,
                                                       None, op0=ALU.is_gt),
                   waits=[wc(f"b16T{KT-1}")], inc="V", chk="bthr")
                op("sync", lambda e: e.dma_start(out=cc_b_in.ap(), in_=B16[:, :]),
                   waits=[wc("bthr")], inc="S_ccb", chk="ccbin")
                if SKIP_CC:
                    def em_ag(e):
                        return e.dma_start(out=cc_b_out.ap()[0:16, :], in_=cc_b_in.ap())
                    op("gpsimd", em_ag, waits=[wc("ccbin")], inc="CC1", chk="ag")
                else:
                    op("gpsimd", lambda e: e.collective_compute(
                        "AllGather", ALU.bypass, replica_groups=[list(range(NCORES))],
                        ins=[cc_b_in.ap().opt()], outs=[cc_b_out.ap().opt()]),
                        waits=[wc("ccbin")], inc="CC1", chk="ag")
                op("gpsimd", lambda e: e.dma_start(out=Ball[:, :], in_=cc_b_out.ap()),
                   waits=[("CC1", 1)], inc="G_cc", chk="ballin")

        op("vector", lambda e: e.tensor_copy(stats[0:1, 0:512], pm[0][0:1, :]),
           waits=[wc(f"stT{KT-1}")], inc="V")
        op("vector", lambda e: e.tensor_copy(stats[0:1, 512:1024], pm[1][0:1, :]), inc="V")
        op("vector", lambda e: e.tensor_copy(stats[0:1, 1024:1536], pm[2][0:1, :]), inc="V")
        op("vector", lambda e: e.tensor_copy(stats[0:1, 1536:2048], pm[3][0:1, :]), inc="V",
           chk="stev")
        op("vector", lambda e: e.tensor_tensor(stats[0:1, 2048:3072], stats[0:1, 0:1024],
                                               stats[0:1, 0:1024], op=ALU.mult), inc="V")
        op("vector", lambda e: e.tensor_tensor(stats[0:1, 1024:2048], stats[0:1, 1024:2048],
                                               stats[0:1, 2048:3072], op=ALU.subtract),
           inc="V")
        op("vector", lambda e: e.tensor_scalar(stats[0:1, 1024:2048], stats[0:1, 1024:2048],
                                               EPS_LN, None, op0=ALU.add), inc="V",
           chk="varV")
        op("scalar", lambda e: e.activation(stats[0:1, 2048:3072], stats[0:1, 1024:2048],
                                            AF.Ln),
           waits=[wc("varV")], inc="SC")
        op("scalar", lambda e: e.activation(stats[0:1, 1024:2048], stats[0:1, 2048:3072],
                                            AF.Exp, scale=-0.5), inc="SC", chk="rstdS")
        op("tensor", lambda e: e.matmul(pa[:, :], onesc[:, :], stats[0:1, 0:512],
                                        start=True, stop=True),
           waits=[wc("stev"), wc("bcolV")], inc="T", chk="bc0")
        op("tensor", lambda e: e.matmul(pb[:, :], onesc[:, :], stats[0:1, 512:1024],
                                        start=True, stop=True), inc="T", chk="bc1")
        op("vector", lambda e: e.tensor_copy(mb_bf[:, 0:512], pa[:, :]),
           waits=[wc("bc0")], inc="V", chk="bc0V")
        op("vector", lambda e: e.tensor_copy(mb_bf[:, 512:1024], pb[:, :]),
           waits=[wc("bc1")], inc="V", chk="bc1V")
        op("tensor", lambda e: e.matmul(pa[:, :], onesc[:, :], stats[0:1, 1024:1536],
                                        start=True, stop=True),
           waits=[wc("rstdS"), wc("bc0V")], inc="T", chk="bc2")
        op("tensor", lambda e: e.matmul(pb[:, :], onesc[:, :], stats[0:1, 1536:2048],
                                        start=True, stop=True),
           waits=[wc("bc1V")], inc="T", chk="bc3")
        op("vector", lambda e: e.tensor_copy(rb_bf[:, 0:512], pa[:, :]),
           waits=[wc("bc2")], inc="V", chk="bc2V")
        op("vector", lambda e: e.tensor_copy(rb_bf[:, 512:1024], pb[:, :]),
           waits=[wc("bc3")], inc="V", chk="bc3V")

        op("tensor", lambda e: e.transpose(pbf[:, 0:128], Ball[:, :], idn[:, :]),
           waits=[wc("ballin")], inc="T", chk="ballT")
        op("vector", lambda e: e.tensor_copy(BallT[:, :], pbf[:, 0:128]),
           waits=[wc("ballT")], inc="V", chk="ballTV")
        for c in range(8):
            bank = pa if c % 2 == 0 else pb
            wdma = [wc("ballTV")]
            if c >= 1:
                wdma.append(wc(f"ekmm{c-1}"))
            op("sync", (lambda e, c=c: e.dma_start(
                out=balltc[:, :], in_=BallT[16 * c:16 * (c + 1), :])),
                waits=wdma, inc="S_ccb", chk=f"btc{c}")
            waits = [wc(f"btc{c}")]
            if c >= 2:
                waits.append(wc(f"ekstrip{c-2}"))
            if c < 2:
                waits.append(wc("bc2V") if c == 0 else wc("bc3V"))
            op("tensor", (lambda e, bank=bank: e.matmul(
                bank[:, :], balltc[:, :], EXPc[:, :],
                start=True, stop=True)), waits=waits, inc="T", chk=f"ekmm{c}")
            wv = [wc(f"ekmm{c}")]
            if c >= 1:
                wv.append(wc(f"ektr{c-1}_3"))
            op("vector", (lambda e, bank=bank: e.tensor_copy(
                ekstrip[:, 0:512], bank[:, :])),
                waits=wv, inc="V", chk=f"ekstrip{c}")
            for t in range(4):
                wt = [wc(f"ekstrip{c}")]
                if c > 0 and t == 0:
                    wt.append(wc(f"ekcol{c-1}"))
                op("tensor", (lambda e, t=t: e.transpose(
                    pbf[:, t * 128:(t + 1) * 128],
                    ekstrip[:, t * 128:(t + 1) * 128], idn[:, :])),
                    waits=wt, inc="T", chk=f"ektr{c}_{t}")
            op("vector", (lambda e, c=c: e.tensor_copy(
                EKcol[:, c * 512:(c + 1) * 512], pbf[:, :])),
                waits=[wc(f"ektr{c}_3")], inc="V", chk=f"ekcol{c}")

        for kt in range(KT):
            wv = [wc("bc3V"), wc("bc2V")]
            if kt == 0:
                wv.append(wc("mrq7"))
            op("vector", (lambda e, kt=kt: e.tensor_tensor(
                mstage[:, 0:R], xnT[:, kt * R:(kt + 1) * R], mb_bf[:, :],
                op=ALU.subtract)),
                waits=wv, inc="V")
            op("vector", (lambda e, kt=kt: e.tensor_tensor(
                xnT[:, kt * R:(kt + 1) * R], mstage[:, 0:R], rb_bf[:, :], op=ALU.mult)),
                inc="V", chk=f"xn{kt}")
            Xh = kt // 16
            op("vector", (lambda e, kt=kt, Xh=Xh: e.tensor_tensor(
                wmbuf[:, kt * NGW:(kt + 1) * NGW].rearrange("p (nb w) -> p nb w", w=32),
                wslot[Xh % 2][:, (kt % 16) * NGW:((kt % 16) + 1) * NGW]
                    .rearrange("p (nb w) -> p nb w", w=32),
                EKcol[:, kt * 128:kt * 128 + 8].unsqueeze(-1).broadcast_to([128, 8, 32]),
                op=ALU.mult)),
                waits=([wc(f"stT{KT-1}")] if kt == 0 else []) +
                      [(f"G_wst{Xh % 2}", 16 * (Xh // 2 + 1)), wc(f"ekcol{kt//4}")],
                inc="V", chk=f"vwm0k{kt}")
            sp = (kt == KT - 1)
            wz = [wc(f"xn{kt}")]
            if kt == 0:
                wz += [wc("cmn"), wc("bc2V"), wc("bc3V"), wc("ekcol7"), wc("bthr")]
            op("tensor", (lambda e, kt=kt, sp=sp: e.matmul(
                pa[0:32, :], A_bf[:, kt * 32:(kt + 1) * 32], xnT[:, kt * R:kt * R + 512],
                start=(kt == 0), stop=sp, skip_group_check=True)), waits=wz)
            op("tensor", (lambda e, kt=kt, sp=sp: e.matmul(
                pb[0:32, :], A_bf[:, kt * 32:(kt + 1) * 32],
                xnT[:, kt * R + 512:kt * R + 1024],
                start=(kt == 0), stop=sp, skip_group_check=True)))
            op("tensor", (lambda e, kt=kt, sp=sp: e.matmul(
                pc[0:8, :], W8_bf[:, kt * 8:(kt + 1) * 8], xnT[:, kt * R:kt * R + 512],
                start=(kt == 0), stop=sp, skip_group_check=True)),
                inc=("T" if sp else None), chk=("zsw" if sp else None))
            for j in (0, 1):
                pmA = pm[2 * j]
                pmB = pm[2 * j + 1]
                wj = [wc(f"vwm0k{kt}")]
                if kt == 0:
                    wj.append(wc("stev"))
                if kt % 8 == 0:
                    wj.append((f"G_wbf{kt//8}", 16))
                wof = kt * NGW + j * 128
                mkchk = (j == 1 and kt % 8 == 7)
                op("tensor", (lambda e, pmA=pmA, wof=wof, kt=kt, st_=(kt == 0): e.matmul(
                    pmA[:, :], wmbuf[:, wof:wof + 128], xnT[:, kt * R:kt * R + 512],
                    start=st_, stop=False, skip_group_check=True)), waits=wj)
                op("tensor", (lambda e, pmB=pmB, wof=wof, kt=kt, st_=(kt == 0): e.matmul(
                    pmB[:, :], wmbuf[:, wof:wof + 128], xnT[:, kt * R + 512:kt * R + 1024],
                    start=st_, stop=False, skip_group_check=True)),
                    inc=("T" if mkchk else None),
                    chk=(f"tWMj1k{kt}" if mkchk else None))
                op("tensor", (lambda e, pmA=pmA, wof=wof, kt=kt: e.matmul(
                    pmA[:, :], Wbuf[:, wof:wof + 128], xnT[:, kt * R:kt * R + 512],
                    start=False, stop=False, skip_group_check=True)))
                op("tensor", (lambda e, pmB=pmB, wof=wof, kt=kt: e.matmul(
                    pmB[:, :], Wbuf[:, wof:wof + 128], xnT[:, kt * R + 512:kt * R + 1024],
                    start=False, stop=False, skip_group_check=True)),
                    inc=("T" if mkchk else None),
                    chk=(f"tWj1k{kt}" if mkchk else None))

        op("gpsimd", lambda e: e.dma_start(out=xw2[:, :], in_=wm3[:, 0, 16:32]),
           waits=[wc(f"stT{KT-1}")], inc="G_wst1", chk="wst0h1")
        op("gpsimd", lambda e: e.dma_start(out=wstage[:, :], in_=wm3[:, 1, 0:16]),
           waits=[wc("vwm0k15")], inc="G_wst0", chk="wst1h0")
        op("gpsimd", lambda e: e.dma_start(out=xw2[:, :], in_=wm3[:, 1, 16:32]),
           waits=[wc(f"vwm0k{KT-1}")], inc="G_wst1", chk="wst1h1")

        op("vector", lambda e: e.tensor_copy(zTb[:, 0:512], pa[0:32, :]),
           waits=[wc("zsw")], inc="V", chk="zloV")
        op("vector", lambda e: e.tensor_copy(zTb[:, 512:1024], pb[0:32, :]), inc="V",
           chk="zhiV")
        op("vector", lambda e: e.tensor_copy(H8T[:, 0:512], pc[0:8, :]), inc="V", chk="zevV")
        CHK["zevS"] = CHK["zevV"]
        for kt in range(KT):
            sp = (kt == KT - 1)
            waits = [wc("zevV")] if kt == 0 else []
            op("tensor", (lambda e, kt=kt, sp=sp: e.matmul(
                pc[0:8, :], W8_bf[:, kt * 8:(kt + 1) * 8],
                xnT[:, kt * R + 512:kt * R + 1024],
                start=(kt == 0), stop=sp, skip_group_check=True)),
                waits=waits, inc=("T" if sp else None), chk=("h8hi" if sp else None))
        op("vector", lambda e: e.tensor_copy(H8T[:, 512:1024], pc[0:8, :]),
           waits=[wc("h8hi")], inc="V", chk="h8hiV")

        op("scalar", lambda e: e.activation(LiM_bf[:, :], LiM[:, :], AF.Identity),
           waits=[wc("csts")], inc="SC", chk="limS")
        op("scalar", lambda e: e.activation(LiMs_bf[:, :], LiM[:, :], AF.Identity,
                                            scale=SCALING), inc="SC", chk="limsS")
        op("tensor", lambda e: e.transpose(pbf[0:32, 0:8], LiM_bf[:, :], idn[0:8, 0:8]),
           waits=[wc("limS"), wc("ekcol7")], inc="T", chk="limT")
        op("vector", lambda e: e.tensor_copy(LiMT_bf[:, :], pbf[0:32, 0:8]),
           waits=[wc("limT")], inc="V", chk="limV")

        op("tensor", lambda e: e.matmul(pa[0:8, :], LiMT_bf[:, :], zTb[:, 0:512],
                                        start=True, stop=True),
           waits=[wc("zevV"), wc("zevS"), wc("limV")], inc="T", chk="ds0")
        op("tensor", lambda e: e.matmul(pb[0:8, :], LiMT_bf[:, :], zTb[:, 512:1024],
                                        start=True, stop=True), inc="T", chk="ds1")
        op("vector", lambda e: e.tensor_copy(dsT[:, 0:512], pa[0:8, :]),
           waits=[wc("ds0")], inc="V")
        op("vector", lambda e: e.tensor_copy(dsT[:, 512:1024], pb[0:8, :]),
           waits=[wc("ds1")], inc="V", chk="dsV")

        for mt in range(8):
            waits = [wc("dsV"), wc("h8hiV")] if mt == 0 else []
            op("tensor", (lambda e, mt=mt: e.transpose(
                pa[:, mt * 8:(mt + 1) * 8], H8T[0:8, mt * 128:(mt + 1) * 128],
                idnf[0:8, 0:8])), waits=waits, inc="T")
            op("tensor", (lambda e, mt=mt: e.transpose(
                pb[:, mt * 8:(mt + 1) * 8], dsT[0:8, mt * 128:(mt + 1) * 128],
                idnf[0:8, 0:8])), inc="T", chk=("tokT" if mt == 7 else None))
        op("vector", lambda e: e.tensor_copy(H8tok[:, :], pa[:, 0:64]),
           waits=[wc("tokT")], inc="V")
        op("vector", lambda e: e.tensor_copy(dstok[:, :], pb[:, 0:64]), inc="V", chk="tokV")
        op("vector", lambda e: e.tensor_tensor(
            H8tok[:, :].rearrange("p (mt e) -> p mt e", e=8),
            H8tok[:, :].rearrange("p (mt e) -> p mt e", e=8),
            b8B[:, :].unsqueeze(1).broadcast_to([128, 8, 8]), op=ALU.add),
            waits=[wc("b8V")], inc="V", chk="h8bias")

        op("vector", lambda e: e.reduce_max(gat[:, 0:1], H8tok[:, :], axis=AX.X,
                                            apply_absolute_value=True),
           waits=[wc("h8bias")], inc="V")
        op("vector", lambda e: e.reduce_max(gat[:, 1:2], dstok[:, :], axis=AX.X,
                                            apply_absolute_value=True),
           waits=[wc("tokV")], inc="V")
        op("vector", lambda e: e.tensor_tensor(gat[:, 0:2], gat[:, 0:2],
                                               oddm[:, :].broadcast_to([128, 2]),
                                               op=ALU.mult),
           waits=[wc("csts")], inc="V", chk="rmax")
        op("tensor", lambda e: e.transpose(pc[0:2, 0:128], gat[:, 0:2], idnf[:, :]),
           waits=[wc("rmax"), wc("bthr")], inc="T", chk="mxT")
        op("vector", lambda e: e.reduce_max(gat[0:2, 8:9], pc[0:2, 0:128], axis=AX.X),
           waits=["mxT"], inc="V", chk="mxV")
        op("sync", lambda e: e.dma_start(out=cc_m_in.ap(), in_=gat[0:2, 8:9]),
           waits=[wc("mxV")], inc="S_ccm", chk="ccmin")
        if SKIP_CC:
            op("gpsimd", lambda e: e.dma_start(out=cc_m_out.ap(), in_=cc_m_in.ap()),
               waits=[wc("ccmin")], inc="CC2", chk="ar")
        else:
            op("gpsimd", lambda e: e.collective_compute(
                "AllReduce", ALU.max, replica_groups=[list(range(NCORES))],
                ins=[cc_m_in.ap().opt()], outs=[cc_m_out.ap().opt()]),
                waits=[wc("ccmin")], inc="CC2", chk="ar")
        op("gpsimd", lambda e: e.dma_start(out=gmax[0:1, 0:2], in_=cc_m_out.ap()),
           waits=[("CC2", 1)], inc="G_cc2", chk="gmaxin")

        op("vector", lambda e: e.tensor_scalar(gmax[0:1, 2:4], gmax[0:1, 0:2], EPS_G, None,
                                               op0=ALU.max),
           waits=[wc("gmaxin")], inc="V")
        op("vector", lambda e: e.reciprocal(gmax[0:1, 4:6], gmax[0:1, 2:4]), inc="V")
        op("vector", lambda e: e.tensor_scalar(gmax[0:1, 6:8], gmax[0:1, 4:6], GAMMA, None,
                                               op0=ALU.mult), inc="V", chk="scV")
        op("tensor", lambda e: e.matmul(pc[:, 0:2], onesc[:, :], gmax[0:1, 6:8],
                                        start=True, stop=True),
           waits=[wc("scV")], inc="T", chk="scT")
        op("vector", lambda e: e.tensor_copy(scB[:, :], pc[:, 0:2]),
           waits=["scT"], inc="V", chk="scBV")

        GA = gat

        def V3(ap):
            return ap.rearrange("p (mt e) -> p mt e", e=8)

        def bc8(ap):
            return ap.unsqueeze(-1).broadcast_to([128, 8, 8])

        op("vector", lambda e: e.tensor_scalar(GA[:, 64:128], H8tok[:, :], scB[:, 0:1], None,
                                               op0=ALU.mult),
           waits=[wc("scBV")], inc="V")
        op("vector", lambda e: e.scalar_tensor_tensor(GA[:, 128:192], dstok[:, :],
                                                      scB[:, 1:2], GA[:, 64:128],
                                                      op0=ALU.mult, op1=ALU.add),
           inc="V", chk="logit")
        op("scalar", lambda e: e.activation(GA[:, 192:256], GA[:, 128:192], AF.Exp),
           waits=[wc("logit")], inc="SC", chk="expS")
        op("vector", lambda e: e.tensor_reduce(GA[:, 256:264], V3(GA[:, 192:256]),
                                               axis=AX.X, op=ALU.add),
           waits=[wc("expS")], inc="V")
        op("vector", lambda e: e.reciprocal(GA[:, 336:344], GA[:, 256:264]), inc="V")
        op("vector", lambda e: e.tensor_tensor(V3(GA[:, 320:384]), V3(GA[:, 192:256]),
                                               bc8(GA[:, 336:344]), op=ALU.mult),
           inc="V", chk="probs")
        op("vector", lambda e: e.tensor_reduce(GA[:, 272:280], V3(GA[:, 320:384]),
                                               axis=AX.X, op=ALU.max), inc="V")
        op("vector", lambda e: e.tensor_tensor(V3(GA[:, 384:448]), V3(GA[:, 320:384]),
                                               bc8(GA[:, 272:280]), op=ALU.is_ge), inc="V")
        op("vector", lambda e: e.tensor_reduce(GA[:, 264:272], V3(GA[:, 384:448]),
                                               axis=AX.X, op=ALU.add), inc="V")
        op("vector", lambda e: e.scalar_tensor_tensor(GA[:, 448:512], GA[:, 384:448], -1e30,
                                                      GA[:, 320:384], op0=ALU.mult,
                                                      op1=ALU.add), inc="V")
        op("vector", lambda e: e.tensor_reduce(GA[:, 280:288], V3(GA[:, 448:512]),
                                               axis=AX.X, op=ALU.max), inc="V")
        op("vector", lambda e: e.tensor_scalar(GA[:, 288:296], GA[:, 264:272], 2.0, None,
                                               op0=ALU.is_ge), inc="V")
        op("vector", lambda e: e.tensor_tensor(GA[:, 296:304], GA[:, 272:280],
                                               GA[:, 280:288], op=ALU.subtract), inc="V")
        op("vector", lambda e: e.tensor_tensor(GA[:, 296:304], GA[:, 296:304],
                                               GA[:, 288:296], op=ALU.mult), inc="V")
        op("vector", lambda e: e.tensor_tensor(GA[:, 304:312], GA[:, 296:304],
                                               GA[:, 280:288], op=ALU.add), inc="V",
           chk="thr")
        op("vector", lambda e: e.tensor_tensor(V3(GA[:, 512:576]), bc8(GA[:, 304:312]),
                                               V3(GA[:, 320:384]), op=ALU.subtract),
           inc="V", chk="preV")
        op("scalar", lambda e: e.activation(GA[:, 576:640], GA[:, 512:576], AF.Exp,
                                            scale=2.0),
           waits=[wc("preV"), wc("ekcol7")], inc="SC", chk="qS")
        op("vector", lambda e: e.tensor_scalar(GA[:, 640:704], GA[:, 576:640], 1.0, None,
                                               op0=ALU.add),
           waits=[wc("qS")], inc="V")
        op("vector", lambda e: e.reciprocal(GA[:, 704:768], GA[:, 640:704]), inc="V")
        op("vector", lambda e: e.tensor_tensor(GA[:, 768:832], GA[:, 320:384],
                                               GA[:, 704:768], op=ALU.mult), inc="V")
        op("vector", lambda e: e.tensor_reduce(GA[:, 312:320], V3(GA[:, 768:832]),
                                               axis=AX.X, op=ALU.add), inc="V")
        op("vector", lambda e: e.tensor_scalar(GA[:, 312:320], GA[:, 312:320], 1e-9, None,
                                               op0=ALU.add), inc="V")
        op("vector", lambda e: e.reciprocal(GA[:, 328:336], GA[:, 312:320]), inc="V")
        op("vector", lambda e: e.tensor_tensor(V3(wnb[:, :]), V3(GA[:, 768:832]),
                                               bc8(GA[:, 328:336]), op=ALU.mult),
           inc="V", chk="wnV")

        ev_cnt = {}
        out_cnt = {}

        def plan_main_block(j):
            g = j // 2
            par = j % 2
            pmA = pm[2 * par]
            pmB = pm[2 * par + 1]
            gbase = (g % 2) * (KT * NGW)
            for kt in range(KT):
                waits = [wc(f"vwm{g}k{kt}")]
                if kt == 0:
                    waits.append(("V", ev_cnt[j - 2]))
                if kt % 8 == 0:
                    waits.append((f"G_wbf{kt//8}", 16 * (g + 1)))
                wof = gbase + kt * NGW + par * 128
                wofW = kt * NGW + par * 128
                mkchk = (par == 1 and kt % 8 == 7)
                op("tensor", (lambda e, pmA=pmA, wof=wof, kt=kt, st_=(kt == 0): e.matmul(
                    pmA[:, :], wmbuf[:, wof:wof + 128], xnT[:, kt * R:kt * R + 512],
                    start=st_, stop=False, skip_group_check=True)), waits=waits)
                op("tensor", (lambda e, pmB=pmB, wof=wof, kt=kt, st_=(kt == 0): e.matmul(
                    pmB[:, :], wmbuf[:, wof:wof + 128], xnT[:, kt * R + 512:kt * R + 1024],
                    start=st_, stop=False, skip_group_check=True)),
                    inc=("T" if mkchk else None),
                    chk=(f"tWMj{j}k{kt}" if mkchk else None))
                op("tensor", (lambda e, pmA=pmA, wofW=wofW, kt=kt: e.matmul(
                    pmA[:, :], Wbuf[:, wofW:wofW + 128], xnT[:, kt * R:kt * R + 512],
                    start=False, stop=False, skip_group_check=True)))
                op("tensor", (lambda e, pmB=pmB, wofW=wofW, kt=kt: e.matmul(
                    pmB[:, :], Wbuf[:, wofW:wofW + 128], xnT[:, kt * R + 512:kt * R + 1024],
                    start=False, stop=False, skip_group_check=True)),
                    inc=("T" if mkchk else None),
                    chk=(f"tWj{j}k{kt}" if mkchk else None))

        def plan_delta_and_evac(j):
            par = j % 2
            pmA = pm[2 * par]
            pmB = pm[2 * par + 1]
            op("tensor", (lambda e, pmA=pmA, j=j: e.matmul(
                pmA[:, :], Bm_bf[:, j * 128:(j + 1) * 128], mulT[:, 0:512],
                start=False, stop=True, skip_group_check=True)),
                waits=[wc("vmul")], inc="T")
            op("tensor", (lambda e, pmB=pmB, j=j: e.matmul(
                pmB[:, :], Bm_bf[:, j * 128:(j + 1) * 128], mulT[:, 512:1024],
                start=False, stop=True, skip_group_check=True)), inc="T", chk=f"tDj{j}")
            rsl = j % 2
            wres = [wc(f"vwm0k{KT-1}")] if j < 2 else [("V", ev_cnt[j - 2])]
            if j < 2:
                wres.append(wc(f"xn{KT-1}"))
            op("sync", (lambda e, j=j, rsl=rsl: e.dma_start(
                out=xf[:, rsl * R:(rsl + 1) * R], in_=d_xT[j * 128:(j + 1) * 128, :])),
                waits=wres, inc=f"S_res{rsl}", chk=f"res{j}")
            for h in range(2):
                bank = pmA if h == 0 else pmB
                osl = (2 * j + h) % 2
                waits = [wc(f"tDj{j}"), wc(f"res{j}")]
                if 2 * j + h >= 2:
                    prev = 2 * j + h - 2
                    waits.append(out_cnt[(prev // 2, prev % 2)])
                op("vector", (lambda e, bank=bank, osl=osl, rsl=rsl, j=j, h=h:
                    e.scalar_tensor_tensor(
                        outstg[:, osl * 512:(osl + 1) * 512], bank[:, :], b_col[:, j:j + 1],
                        xf[:, rsl * R + h * 512:rsl * R + (h + 1) * 512],
                        op0=ALU.add, op1=ALU.add)),
                    waits=waits, inc="V", chk=f"ev{j}_{h}")
                op("scalar", (lambda e, j=j, h=h, osl=osl: e.dma_start(
                    out=d_out[j * 128:(j + 1) * 128, h * 512:(h + 1) * 512],
                    in_=outstg[:, osl * 512:(osl + 1) * 512])),
                    waits=[wc(f"ev{j}_{h}")], inc=f"SCQ{osl}")
                out_cnt[(j, h)] = (f"SCQ{osl}", CNT[f"SCQ{osl}"])
            ev_cnt[j] = CHK[f"ev{j}_1"][1]

        def plan_vwm(g):
            base = (g % 2) * (KT * NGW)
            for kt in range(KT):
                X = 2 * g + kt // 16
                waits = [(f"G_wst{X % 2}", 16 * (X // 2 + 1))]
                if g >= 2:
                    waits.append(wc(f"tWMj{2*g-3}k{8*(kt//8)+7}"))
                op("vector", (lambda e, g=g, kt=kt, base=base, X=X: e.tensor_tensor(
                    wmbuf[:, base + kt * NGW:base + (kt + 1) * NGW]
                        .rearrange("p (nb w) -> p nb w", w=32),
                    wslot[X % 2][:, (kt % 16) * NGW:((kt % 16) + 1) * NGW]
                        .rearrange("p (nb w) -> p nb w", w=32),
                    EKcol[:, kt * 128 + 8 * g:kt * 128 + 8 * g + 8]
                        .unsqueeze(-1).broadcast_to([128, 8, 32]),
                    op=ALU.mult)),
                    waits=waits, inc="V", chk=f"vwm{g}k{kt}")

        for mt in range(8):
            waits = [wc("wnV")] if mt == 0 else []
            op("tensor", (lambda e, mt=mt: e.matmul(
                pa[:, mt * 8:(mt + 1) * 8], pairP[:, :], wnb[:, mt * 8:(mt + 1) * 8],
                start=True, stop=True)), waits=waits,
                inc=("T" if mt == 7 else None), chk=("wf8" if mt == 7 else None))
        op("vector", lambda e: e.tensor_copy(wfb[:, :], pa[:, 0:64]),
           waits=[wc("wf8")], inc="V", chk="wfbV")
        for rnd in range(2):
            for t in range(4):
                mt = rnd * 4 + t
                waits = [wc("wfbV")]
                if rnd == 1 and t == 0:
                    waits.append(wc("wfT0V"))
                op("tensor", (lambda e, mt=mt, t=t: e.transpose(
                    pbf[0:8, t * 128:(t + 1) * 128], wfb[:, mt * 8:(mt + 1) * 8],
                    idn[:, :])),
                    waits=waits, inc=("T" if t == 3 else None),
                    chk=(f"wfTr{rnd}" if t == 3 else None))
            op("vector", (lambda e, rnd=rnd: e.tensor_copy(
                wfT[0:8, rnd * 512:(rnd + 1) * 512], pbf[0:8, 0:512])),
                waits=[wc(f"wfTr{rnd}")], inc="V", chk=f"wfT{rnd}V")
        op("tensor", lambda e: e.matmul(pa[0:32, :], LiMs_bf[:, :], wfT[0:8, 0:512],
                                        start=True, stop=True),
           waits=[wc("wfT1V"), wc("limsS")], inc="T", chk="pmx0")
        op("tensor", lambda e: e.matmul(pb[0:32, :], LiMs_bf[:, :], wfT[0:8, 512:1024],
                                        start=True, stop=True), inc="T", chk="pmx1")
        op("vector", lambda e: e.tensor_tensor(mulT[:, 0:512], pa[0:32, :], zTb[:, 0:512],
                                               op=ALU.mult),
           waits=[wc("pmx0")], inc="V")
        op("vector", lambda e: e.tensor_tensor(mulT[:, 512:1024], pb[0:32, :],
                                               zTb[:, 512:1024], op=ALU.mult),
           waits=[wc("pmx1")], inc="V", chk="vmul")

        plan_delta_and_evac(0)
        plan_delta_and_evac(1)
        plan_vwm(1)
        plan_vwm(2)
        nb_eff = NB if NB_EFF is None else NB_EFF
        for j in range(2, nb_eff):
            plan_main_block(j)
            plan_delta_and_evac(j)
            if j % 2 == 1 and (j + 3) // 2 < NG:
                plan_vwm((j + 3) // 2)

        for g in range(1, (NB if NB_EFF is None else NB_EFF) // 2):
            for sb_ in range(4):
                op("gpsimd", (lambda e, g=g, sb_=sb_: e.dma_start(
                    out=Wbuf[:, sb_ * 8 * NGW:(sb_ + 1) * 8 * NGW],
                    in_=W3[:, g, 8 * sb_:8 * (sb_ + 1)])),
                    waits=[wc(f"tWj{2*g-1}k{8*sb_+7}")], inc=f"G_wbf{sb_}")
            if g + 1 < NG:
                op("gpsimd", (lambda e, g=g: e.dma_start(
                    out=wslot[0][:, :], in_=wm3[:, g + 1, 0:16])),
                   waits=[wc(f"vwm{g}k15")], inc="G_wst0", chk=f"wst{g+1}h0")
                op("gpsimd", (lambda e, g=g: e.dma_start(
                    out=wslot[1][:, :], in_=wm3[:, g + 1, 16:32])),
                   waits=[wc(f"vwm{g}k{KT-1}")], inc="G_wst1", chk=f"wst{g+1}h1")

        if debug_taps:
            tap_list = [
                ("t_xnT", xnT[:, :], [wc(f"xn{KT-1}")]),
                ("t_EKcol", EKcol[:, :], [wc("ekcol7")]),
                ("t_Ball", Ball[:, :], [wc("ballin")]),
                ("t_H8tok", H8tok[:, :], [wc("h8bias")]),
                ("t_dstok", dstok[:, :], [wc("tokV")]),
                ("t_wnb", wnb[:, :], [wc("wnV")]),
                ("t_mulT", mulT[:, :], [wc("vmul")]),
                ("t_gmax", gmax[0:1, 0:8], [wc("scV")]),
                ("t_Sn", Sn[:, :], [wc("mrq7")]),
                ("t_zT", zTb[:, :], [wc("zhiV")]),

            ]
            for nm, src, waits in tap_list:
                op("sync", (lambda e, nm=nm, src=src: e.dma_start(out=taps[nm], in_=src)),
                   waits=waits, inc="S_cst")

        PLAN["scalar"].append((None, [("SCQ0", CNT["SCQ0"]), ("SCQ1", CNT["SCQ1"])], None))
        PLAN["sync"].append((None, [("S_cst", CNT["S_cst"])], None))

        def make_runner(eng_name):
            own_sem = {"vector": "V", "scalar": "SC"}.get(eng_name)
            def run(engine):
                seen = {}
                own = 0
                for emit, waits, inc in PLAN[eng_name]:
                    if own_sem is not None and emit is not None and own > 0:
                        if seen.get(own_sem, 0) < own:
                            engine.wait_ge(SEM[own_sem], own)
                            seen[own_sem] = own
                    for wspec in waits:
                        if isinstance(wspec, str):
                            if wspec not in CHK and STOP_N is not None:
                                continue
                            semn, val = CHK[wspec]
                        else:
                            semn, val = wspec
                        if seen.get(semn, 0) < val:
                            engine.wait_ge(SEM[semn], val)
                            seen[semn] = val
                    if emit is None:
                        continue
                    inst = emit(engine)
                    if inc == own_sem:
                        own += 1
                    if inc is not None:
                        if inc in CC_SEMS:
                            if SKIP_CC:
                                inst.then_inc(SEM[inc], 16)
                            else:
                                inst.then_inc(SEM[inc])
                        elif inc in DMA_SEMS:
                            inst.then_inc(SEM[inc], 16)
                        else:
                            inst.then_inc(SEM[inc], 1)
            return run

        block.sync(make_runner("sync"))
        block.gpsimd(make_runner("gpsimd"))
        block.scalar(make_runner("scalar"))
        block.tensor(make_runner("tensor"))
        block.vector(make_runner("vector"))

    return nc


def _consts():
    idn = np.eye(128, dtype=ml_dtypes.bfloat16)
    idnf = np.eye(128, dtype=np.float32)
    avgp = np.zeros((128, 128), dtype=ml_dtypes.bfloat16)
    avgp[:, 0] = ml_dtypes.bfloat16(1.0 / 4096.0)
    EXP = np.zeros((16, 512), dtype=ml_dtypes.bfloat16)
    for kb in range(16):
        EXP[kb, kb * 32:(kb + 1) * 32] = 1
    Q4 = np.zeros((128, 4), dtype=ml_dtypes.bfloat16)
    for p in range(128):
        Q4[p, p // 32] = 1
    oddm = np.zeros((128, 1), np.float32)
    oddm[1::2, 0] = 1.0
    onesc = np.ones((1, 128), np.float32)
    pairP = np.zeros((128, 128), dtype=ml_dtypes.bfloat16)
    for m in range(128):
        pairP[m | 1, m] = 1
    return idn, idnf, avgp, EXP, Q4, oddm, onesc, pairP


def _tile_khost(M):
    K, C = M.shape
    return np.ascontiguousarray(M.reshape(32, 128, C).transpose(1, 0, 2).reshape(128, 32 * C))


def _tile_weight(Wmat):
    arr = Wmat.reshape(NG, NGW, KT, 128)
    return np.ascontiguousarray(arr.transpose(3, 0, 2, 1).reshape(128, NG * KT * NGW))


def kernel(L0_attn_data, L0_attn_weight, L0_attn_mask, L0_ffn_W, L0_ffn_b,
           L0_ffn_A, L0_ffn_Bmat, L0_ffn_LiMEs):
    X = np.ascontiguousarray(np.asarray(L0_attn_data, np.float32).reshape(2 * 4096, D))
    weight = np.asarray(L0_attn_weight, np.float32)
    mask = np.asarray(L0_attn_mask, np.int32)
    W = np.asarray(L0_ffn_W, np.float32)
    bvec = np.asarray(L0_ffn_b, np.float32)
    A = np.asarray(L0_ffn_A, np.float32)
    Bmat = np.ascontiguousarray(np.asarray(L0_ffn_Bmat, np.float32))
    LiMa = np.ascontiguousarray(np.asarray(L0_ffn_LiMEs, np.float32))

    idn, idnf, avgp, EXP, Q4, oddm, onesc, pairP = _consts()

    wmh = _tile_weight(weight)
    Wh = _tile_weight(W)
    Ah = _tile_khost(A)
    W8h = _tile_khost(np.ascontiguousarray(W[0:8, :].T))
    b32 = np.ascontiguousarray(bvec.reshape(32, 128))
    b8 = np.ascontiguousarray(bvec[0:8].reshape(1, 8))
    maskT = np.ascontiguousarray(mask.T)

    nc = build_nc(debug_taps=DEBUG_TAPS)

    in_maps = []
    for c in range(NCORES):
        xT_c = np.ascontiguousarray(X[c * R:(c + 1) * R, :].T)
        msk_c = _tile_khost(np.ascontiguousarray(maskT[:, c * 512:(c + 1) * 512]))
        in_maps.append({
            "xT": xT_c, "wmh": wmh, "Wh": Wh, "mskh": msk_c,
            "Ah": Ah, "W8h": W8h, "b32": b32, "b8": b8, "Bm": Bmat, "LiM": LiMa,
            "idn": idn, "idnf": idnf, "avgp": avgp, "EXPc": EXP, "Q4c": Q4,
            "oddm": oddm, "onesc": onesc, "pairP": pairP,
        })

    res = run_bass_kernel_spmd(nc, in_maps, core_ids=list(range(NCORES)))
    out = np.empty((2 * 4096, D), np.float32)
    for c in range(NCORES):
        out[c * R:(c + 1) * R, :] = np.asarray(res.results[c]["outT"]).T
    if DEBUG_TAPS:
        kernel.last_results = res
    return out.reshape(2, 4096, D)


# version 09
# speedup vs baseline: 1.9185x; 1.4738x over previous
import numpy as np
import ml_dtypes
import concourse.bass as bass
import concourse.mybir as mybir
from concourse.bass_utils import run_bass_kernel_spmd

F32 = mybir.dt.float32
BF16 = mybir.dt.bfloat16
I32 = mybir.dt.int32
AF = mybir.ActivationFunctionType
ALU = mybir.AluOpType
AX = mybir.AxisListType

NCORES = 8
D = 4096
R = 1024
KT = 32
NGW = 256
NG = D // NGW
NB = 32
EPS_LN = 1e-5
EPS_G = 1e-6
GAMMA = 0.5
SCALING = 16.0 / 32.0

DEBUG_TAPS = False
SKIP_CC = False
NB_EFF = None
PHASE1_ONLY = False
STOP_N = None
ACC_OP = None
ACC_OP_H = None

ENGS = ("sync", "gpsimd", "scalar", "tensor", "vector")


def build_nc(debug_taps=False):
    nc = bass.Bass("TRN2", target_bir_lowering=False, debug=False,
                   dynamic_dma_scratch_size=16384)

    d_xT = nc.dram_tensor("xT", [D, R], F32, kind="ExternalInput").ap()
    d_wm = nc.dram_tensor("wmh", [128, KT * D], F32, kind="ExternalInput").ap()
    d_W = nc.dram_tensor("Wh", [128, KT * D], F32, kind="ExternalInput").ap()
    d_msk = nc.dram_tensor("mskh", [128, KT * 512], I32, kind="ExternalInput").ap()
    d_A = nc.dram_tensor("Ah", [128, KT * 32], F32, kind="ExternalInput").ap()
    d_W8 = nc.dram_tensor("W8h", [128, KT * 8], F32, kind="ExternalInput").ap()
    d_b32 = nc.dram_tensor("b32", [32, 128], F32, kind="ExternalInput").ap()
    d_b8 = nc.dram_tensor("b8", [1, 8], F32, kind="ExternalInput").ap()
    d_Bm = nc.dram_tensor("Bm", [32, D], F32, kind="ExternalInput").ap()
    d_LiM = nc.dram_tensor("LiM", [8, 32], F32, kind="ExternalInput").ap()
    d_idn = nc.dram_tensor("idn", [128, 128], BF16, kind="ExternalInput").ap()
    d_idnf = nc.dram_tensor("idnf", [128, 128], F32, kind="ExternalInput").ap()
    d_avgp = nc.dram_tensor("avgp", [128, 128], BF16, kind="ExternalInput").ap()
    d_EXP = nc.dram_tensor("EXPc", [16, 512], BF16, kind="ExternalInput").ap()
    d_Q4 = nc.dram_tensor("Q4c", [128, 4], BF16, kind="ExternalInput").ap()
    d_oddm = nc.dram_tensor("oddm", [128, 1], F32, kind="ExternalInput").ap()
    d_onesc = nc.dram_tensor("onesc", [1, 128], F32, kind="ExternalInput").ap()
    d_pairP = nc.dram_tensor("pairP", [128, 128], BF16, kind="ExternalInput").ap()

    d_out = nc.dram_tensor("outT", [D, R], F32, kind="ExternalOutput").ap()

    cc_b_in = nc.dram_tensor("cc_b_in", [16, 128], BF16)
    cc_b_out = nc.dram_tensor("cc_b_out", [128, 128], BF16)
    cc_m_in = nc.dram_tensor("cc_m_in", [1, 2], F32)
    cc_m_out = nc.dram_tensor("cc_m_out", [1, 2], F32)
    cc_d_in = nc.dram_tensor("cc_d_in", [1, 1], F32)
    cc_d_out = nc.dram_tensor("cc_d_out", [8, 1], F32)

    taps = {}
    if debug_taps:
        for nm, shp, dt in [("t_xnT", [128, KT * R], BF16), ("t_EKcol", [128, KT * 128], BF16),
                            ("t_Ball", [128, 128], BF16), ("t_H8tok", [128, 64], F32),
                            ("t_dstok", [128, 64], F32), ("t_wnb", [128, 64], BF16),
                            ("t_mulT", [32, R], BF16), ("t_gmax", [1, 8], F32),
                            ("t_Sn", [128, 512], BF16), ("t_zT", [32, R], BF16)]:
            taps[nm] = nc.dram_tensor(nm, shp, dt, kind="ExternalOutput").ap()

    from contextlib import ExitStack
    with ExitStack() as st:
        E = st.enter_context

        xnT = E(nc.sbuf_tensor("xnT", [128, KT * R], BF16))
        wstage = E(nc.sbuf_tensor("wstage", [128, 16 * NGW], BF16))
        wmbuf = E(nc.sbuf_tensor("wmbuf", [128, 2 * KT * NGW], BF16))
        mstage = E(nc.sbuf_tensor("mstage", [128, 8 * 512], BF16))
        EKcol = E(nc.sbuf_tensor("EKcol", [128, KT * 128], BF16))
        xf = E(nc.sbuf_tensor("xf", [128, 4 * R], F32))

        rstdB = E(nc.sbuf_tensor("rstdB", [128, R], F32))
        stats = E(nc.sbuf_tensor("stats", [1, 3072], F32))
        outstg = E(nc.sbuf_tensor("outstg", [128, 1024], F32))
        xw2 = E(nc.sbuf_tensor("xw2", [128, 16 * NGW], BF16))
        xab = xw2[:, :]
        xsqb = None
        gat = E(nc.sbuf_tensor("gat", [128, 832], F32))
        Sn = E(nc.sbuf_tensor("Sn", [128, 512], BF16))
        B16 = E(nc.sbuf_tensor("B16", [16, 128], BF16))
        Ball = E(nc.sbuf_tensor("Ball", [128, 128], BF16))
        BallT = E(nc.sbuf_tensor("BallT", [128, 128], BF16))
        zTb = E(nc.sbuf_tensor("zTb", [32, R], BF16))
        H8tok = E(nc.sbuf_tensor("H8tok", [128, 64], F32))
        dstok = E(nc.sbuf_tensor("dstok", [128, 64], F32))
        wnb = E(nc.sbuf_tensor("wnb", [128, 64], BF16))
        wfb = E(nc.sbuf_tensor("wfb", [128, 64], BF16))
        mulT = E(nc.sbuf_tensor("mulT", [32, R], BF16))
        A_bf = E(nc.sbuf_tensor("A_bf", [128, KT * 32], BF16))
        W8_bf = E(nc.sbuf_tensor("W8_bf", [128, KT * 8], BF16))
        Bm_bf = E(nc.sbuf_tensor("Bm_bf", [32, D], BF16))
        b_sb = E(nc.sbuf_tensor("b_sb", [32, 128], F32))
        b_col = E(nc.sbuf_tensor("b_col", [128, 32], F32))
        b8B = E(nc.sbuf_tensor("b8B", [128, 8], F32))
        b8r = E(nc.sbuf_tensor("b8r", [1, 8], F32))
        LiM = E(nc.sbuf_tensor("LiMs", [8, 32], F32))
        LiM_bf = E(nc.sbuf_tensor("LiM_bf", [8, 32], BF16))
        LiMs_bf = E(nc.sbuf_tensor("LiMsc_bf", [8, 32], BF16))
        LiMT_bf = E(nc.sbuf_tensor("LiMT_bf", [32, 8], BF16))
        idn = E(nc.sbuf_tensor("idn_sb", [128, 128], BF16))
        idnf = E(nc.sbuf_tensor("idnf_sb", [128, 128], F32))
        avgp = E(nc.sbuf_tensor("avgp_sb", [128, 128], BF16))
        EXPc = E(nc.sbuf_tensor("EXP_sb", [16, 512], BF16))
        balltc = E(nc.sbuf_tensor("balltc", [16, 128], BF16))
        Q4 = E(nc.sbuf_tensor("Q4_sb", [128, 4], BF16))
        oddm = E(nc.sbuf_tensor("oddm_sb", [128, 1], F32))
        onesc = E(nc.sbuf_tensor("onesc_sb", [1, 128], F32))
        pairP = E(nc.sbuf_tensor("pairP_sb", [128, 128], BF16))
        gmax = E(nc.sbuf_tensor("gmax", [1, 16], F32))
        scB = E(nc.sbuf_tensor("scB", [128, 2], F32))

        xsqb = wmbuf[:, 0:16 * NGW]
        meanB = outstg[:, :]
        mb_bf = rstdB[:, 0:512].bitcast(BF16)
        rb_bf = rstdB[:, 512:1024].bitcast(BF16)
        dsT = outstg[0:8, :]
        H8T = rstdB[0:8, :]
        wfT = mulT[0:8, :]
        wslot = [wstage, xw2]
        tmpf = mstage[:, :].bitcast(F32)
        ekstrip = gat[:, 576:832].bitcast(BF16)

        pm = [E(nc.psum_tensor(f"pm{i}", [128, 512], F32)) for i in range(4)]
        pa = E(nc.psum_tensor("pa", [128, 512], F32))
        pb = E(nc.psum_tensor("pb", [128, 512], F32))
        pc = E(nc.psum_tensor("pc", [128, 512], F32))
        pbf = E(nc.psum_tensor("pbf", [128, 512], BF16))

        sem_names = ["S_cst", "S_xa0", "S_xa1", "S_xb0", "S_xb1", "S_res0", "S_res1",
                     "S_ccb", "S_ccm", "G_msk0", "G_msk1", "G_wst0", "G_wst1", "G_wbf0", "G_wbf1", "G_wbf2", "G_wbf3", "G_cmn", "G_cc", "G_cc2",
                     "SCQ0", "SCQ1", "CC0", "CC1", "CC2", "T", "V", "SC"]
        SEM = {nm: E(nc.semaphore(nm)) for nm in sem_names}
        DMA_SEMS = {"S_cst", "S_xa0", "S_xa1", "S_xb0", "S_xb1", "S_res0", "S_res1",
                    "S_ccb", "S_ccm", "G_msk0", "G_msk1", "G_wst0", "G_wst1", "G_wbf0", "G_wbf1", "G_wbf2", "G_wbf3", "G_cmn", "G_cc", "G_cc2",
                    "SCQ0", "SCQ1"}
        CC_SEMS = {"CC0", "CC1", "CC2"}

        block = E(nc.Block())

        PLAN = {e: [] for e in ENGS}
        CNT = {nm: 0 for nm in sem_names}
        CHK = {}

        nops = [0]

        def op(eng, emit, waits=(), inc=None, chk=None):
            nops[0] += 1
            if STOP_N is not None and nops[0] > STOP_N:
                return
            PLAN[eng].append((emit, list(waits), inc))
            if inc is not None:
                CNT[inc] += 16 if inc in DMA_SEMS else 1
                if chk is not None:
                    CHK[chk] = (inc, CNT[inc])

        def wc(name):
            if name not in CHK and STOP_N is not None:
                return ("S_cst", 0)
            return CHK[name]

        wm3 = d_wm.rearrange("p (g t n) -> p g t n", g=NG, t=KT)
        W3 = d_W.rearrange("p (g t n) -> p g t n", g=NG, t=KT)


        for dst, src in [(idn[:, :], d_idn), (idnf[:, :], d_idnf), (avgp[:, :], d_avgp),
                         (EXPc[:, :], d_EXP), (Q4[:, :], d_Q4), (oddm[:, :], d_oddm),
                         (onesc[:, :], d_onesc), (pairP[:, :], d_pairP),
                         (b_sb[:, :], d_b32), (LiM[:, :], d_LiM), (b8r[:, :], d_b8)]:
            op("sync", (lambda e, a=dst, b=src: e.dma_start(out=a, in_=b)), inc="S_cst")
        CHK["csts"] = ("S_cst", CNT["S_cst"])

        def em_mask(q):
            sl = q % 2
            def f(e):
                return e.dma_start(out=mstage[:, sl * 2048:(sl + 1) * 2048],
                                   in_=d_msk[:, q * 2048:(q + 1) * 2048],
                                   max_dma_last_dim=8192)
            return f
        if not SKIP_CC:
            op("gpsimd", lambda e: e.dma_start(out=cc_d_in.ap(), in_=d_b8[0:1, 0:1]),
               inc="G_cmn")
            op("gpsimd", lambda e: e.collective_compute(
                "AllGather", ALU.bypass, replica_groups=[list(range(NCORES))],
                ins=[cc_d_in.ap().opt()], outs=[cc_d_out.ap().opt()]),
                waits=[("G_cmn", 16)], inc="CC0")
        op("gpsimd", em_mask(0), inc="G_msk0", chk="mq0")
        op("gpsimd", em_mask(1), inc="G_msk1", chk="mq1")
        op("gpsimd", lambda e: e.dma_start(out=wstage[:, :], in_=wm3[:, 0, 0:16]),
           inc="G_wst0", chk="wst0h0")
        op("gpsimd", lambda e: e.dma_start(out=A_bf[:, :], in_=d_A), inc="G_cmn")
        op("gpsimd", lambda e: e.dma_start(out=W8_bf[:, :], in_=d_W8), inc="G_cmn")
        op("gpsimd", lambda e: e.dma_start(out=Bm_bf[:, :], in_=d_Bm), inc="G_cmn", chk="cmn")

        def emit_mask_batch(q):
            sl = q % 2
            for t in range(4):
                ktg = 4 * q + t

                def em_red(e, t=t, ktg=ktg, sl=sl):
                    with nc.allow_low_precision(reason="mask block-sums <=32 exact in bf16"):
                        return e.tensor_reduce(
                            Sn[:, ktg * 16:(ktg + 1) * 16],
                            mstage[:, sl * 2048 + t * 512:sl * 2048 + (t + 1) * 512]
                                .rearrange("p (nb w) -> p nb w", w=32),
                            axis=AX.X, op=ALU.add)
                op("vector", em_red, waits=[wc(f"mq{q}")], inc="V")
            CHK[f"mrq{q}"] = ("V", CNT["V"])
            if q < 6:
                op("gpsimd", em_mask(q + 2), waits=[wc(f"mrq{q}")], inc=f"G_msk{q % 2}",
                   chk=f"mq{q+2}")

        mask_trigger = {1: 0, 3: 1, 5: 2, 7: 3, 9: 4, 11: 5, 13: 6, 15: 7}
        for kt in range(KT):
            pr = kt // 2
            P = pr % 2
            if kt % 2 == 0:
                waits = []
                if kt >= 4:
                    waits += [wc(f"sqA{pr-2}"), wc(f"xcv{pr-2}")]
                op("sync", (lambda e, kt=kt, P=P: e.dma_start(
                    out=xf[:, P * 2048:(P + 1) * 2048].rearrange("p (t n) -> p t n", t=2),
                    in_=d_xT[kt * 128:(kt + 2) * 128, :].rearrange("(t p) n -> p t n",
                                                                  p=128))),
                    waits=waits, inc=f"S_xa{P}", chk=f"xa{pr}")
                waits = [wc(f"xa{pr}")]
                op("vector", (lambda e, P=P, kt=kt: e.tensor_copy(
                    xnT[:, kt * R:(kt + 2) * R], xf[:, P * 2048:(P + 1) * 2048])),
                    waits=waits, inc="V", chk=f"xcv{pr}")
                wsq = list(waits)
                if kt >= 4:
                    wsq.append(wc(f"stT{kt-3}"))
                op("scalar", (lambda e, P=P: e.activation(
                    xsqb[:, P * 2048:(P + 1) * 2048], xf[:, P * 2048:(P + 1) * 2048],
                    AF.Square)),
                    waits=wsq, inc="SC", chk=f"sqA{pr}")
            st_ = (kt == 0)
            sp = (kt == KT - 1)
            xqoff = P * 2048 + (kt % 2) * 1024
            wst_ = [wc(f"sqA{pr}"), wc(f"xcv{pr}")]
            if kt == 0:
                wst_.append(wc("csts"))
            op("tensor", (lambda e, kt=kt, st_=st_, sp=sp: e.matmul(
                pm[0][:, :], avgp[:, :], xnT[:, kt * R:kt * R + 512], start=st_, stop=sp)),
                waits=wst_)
            op("tensor", (lambda e, kt=kt, st_=st_, sp=sp: e.matmul(
                pm[1][:, :], avgp[:, :], xnT[:, kt * R + 512:kt * R + 1024],
                start=st_, stop=sp)))
            op("tensor", (lambda e, xqoff=xqoff, st_=st_, sp=sp: e.matmul(
                pm[2][:, :], avgp[:, :], xsqb[:, xqoff:xqoff + 512], start=st_, stop=sp)))
            op("tensor", (lambda e, xqoff=xqoff, st_=st_, sp=sp: e.matmul(
                pm[3][:, :], avgp[:, :], xsqb[:, xqoff + 512:xqoff + 1024],
                start=st_, stop=sp)), inc="T", chk=f"stT{kt}")
            if kt in mask_trigger:
                emit_mask_batch(mask_trigger[kt])
            if kt == 18:
                op("tensor", lambda e: e.transpose(pa[:, 0:32], b_sb[:, :],
                                                   idnf[0:32, 0:32]),
                   waits=[wc("csts")], inc="T", chk="bcolT")
                op("tensor", lambda e: e.matmul(pc[:, 0:8], onesc[:, :], b8r[:, :],
                                                start=True, stop=True), inc="T", chk="b8T")
                op("vector", lambda e: e.tensor_copy(b_col[:, :], pa[:, 0:32]),
                   waits=[wc("bcolT")], inc="V", chk="bcolV")
                op("vector", lambda e: e.tensor_copy(b8B[:, :], pc[:, 0:8]),
                   waits=[wc("b8T")], inc="V", chk="b8V")
                for ktb in range(KT):
                    w2 = [wc("mrq7"), wc("b8V")] if ktb == 0 else []
                    op("tensor", (lambda e, ktb=ktb: e.matmul(
                        pc[0:16, 4 * ktb:4 * ktb + 4], Sn[:, ktb * 16:(ktb + 1) * 16],
                        Q4[:, :], start=True, stop=True)), waits=w2,
                        inc=("T" if ktb == KT - 1 else None),
                        chk=(f"b16T{KT-1}" if ktb == KT - 1 else None))
                op("vector", lambda e: e.tensor_scalar(B16[:, :], pc[0:16, 0:128], 0.0,
                                                       None, op0=ALU.is_gt),
                   waits=[wc(f"b16T{KT-1}")], inc="V", chk="bthr")
                op("sync", lambda e: e.dma_start(out=cc_b_in.ap(), in_=B16[:, :]),
                   waits=[wc("bthr")], inc="S_ccb", chk="ccbin")
                if SKIP_CC:
                    def em_ag(e):
                        return e.dma_start(out=cc_b_out.ap()[0:16, :], in_=cc_b_in.ap())
                    op("gpsimd", em_ag, waits=[wc("ccbin")], inc="CC1", chk="ag")
                else:
                    op("gpsimd", lambda e: e.collective_compute(
                        "AllGather", ALU.bypass, replica_groups=[list(range(NCORES))],
                        ins=[cc_b_in.ap().opt()], outs=[cc_b_out.ap().opt()]),
                        waits=[wc("ccbin")], inc="CC1", chk="ag")
                op("gpsimd", lambda e: e.dma_start(out=Ball[:, :], in_=cc_b_out.ap()),
                   waits=[("CC1", 1)], inc="G_cc", chk="ballin")

        op("vector", lambda e: e.tensor_copy(stats[0:1, 0:512], pm[0][0:1, :]),
           waits=[wc(f"stT{KT-1}")], inc="V")
        op("vector", lambda e: e.tensor_copy(stats[0:1, 512:1024], pm[1][0:1, :]), inc="V")
        op("vector", lambda e: e.tensor_copy(stats[0:1, 1024:1536], pm[2][0:1, :]), inc="V")
        op("vector", lambda e: e.tensor_copy(stats[0:1, 1536:2048], pm[3][0:1, :]), inc="V",
           chk="stev")
        op("vector", lambda e: e.tensor_tensor(stats[0:1, 2048:3072], stats[0:1, 0:1024],
                                               stats[0:1, 0:1024], op=ALU.mult), inc="V")
        op("vector", lambda e: e.tensor_tensor(stats[0:1, 1024:2048], stats[0:1, 1024:2048],
                                               stats[0:1, 2048:3072], op=ALU.subtract),
           inc="V")
        op("vector", lambda e: e.tensor_scalar(stats[0:1, 1024:2048], stats[0:1, 1024:2048],
                                               EPS_LN, None, op0=ALU.add), inc="V",
           chk="varV")
        op("scalar", lambda e: e.activation(stats[0:1, 2048:3072], stats[0:1, 1024:2048],
                                            AF.Ln),
           waits=[wc("varV")], inc="SC")
        op("scalar", lambda e: e.activation(stats[0:1, 1024:2048], stats[0:1, 2048:3072],
                                            AF.Exp, scale=-0.5), inc="SC", chk="rstdS")
        op("tensor", lambda e: e.matmul(pa[:, :], onesc[:, :], stats[0:1, 0:512],
                                        start=True, stop=True),
           waits=[wc("stev"), wc("bcolV")], inc="T", chk="bc0")
        op("tensor", lambda e: e.matmul(pb[:, :], onesc[:, :], stats[0:1, 512:1024],
                                        start=True, stop=True), inc="T", chk="bc1")
        op("vector", lambda e: e.tensor_copy(mb_bf[:, 0:512], pa[:, :]),
           waits=[wc("bc0")], inc="V", chk="bc0V")
        op("vector", lambda e: e.tensor_copy(mb_bf[:, 512:1024], pb[:, :]),
           waits=[wc("bc1")], inc="V", chk="bc1V")
        op("tensor", lambda e: e.matmul(pa[:, :], onesc[:, :], stats[0:1, 1024:1536],
                                        start=True, stop=True),
           waits=[wc("rstdS"), wc("bc0V")], inc="T", chk="bc2")
        op("tensor", lambda e: e.matmul(pb[:, :], onesc[:, :], stats[0:1, 1536:2048],
                                        start=True, stop=True),
           waits=[wc("bc1V")], inc="T", chk="bc3")
        op("vector", lambda e: e.tensor_copy(rb_bf[:, 0:512], pa[:, :]),
           waits=[wc("bc2")], inc="V", chk="bc2V")
        op("vector", lambda e: e.tensor_copy(rb_bf[:, 512:1024], pb[:, :]),
           waits=[wc("bc3")], inc="V", chk="bc3V")

        op("tensor", lambda e: e.transpose(pbf[:, 0:128], Ball[:, :], idn[:, :]),
           waits=[wc("ballin")], inc="T", chk="ballT")
        op("vector", lambda e: e.tensor_copy(BallT[:, :], pbf[:, 0:128]),
           waits=[wc("ballT")], inc="V", chk="ballTV")
        for c in range(8):
            bank = pa if c % 2 == 0 else pb
            wdma = [wc("ballTV")]
            if c >= 1:
                wdma.append(wc(f"ekmm{c-1}"))
            op("sync", (lambda e, c=c: e.dma_start(
                out=balltc[:, :], in_=BallT[16 * c:16 * (c + 1), :])),
                waits=wdma, inc="S_ccb", chk=f"btc{c}")
            waits = [wc(f"btc{c}")]
            if c >= 2:
                waits.append(wc(f"ekstrip{c-2}"))
            if c < 2:
                waits.append(wc("bc2V") if c == 0 else wc("bc3V"))
            op("tensor", (lambda e, bank=bank: e.matmul(
                bank[:, :], balltc[:, :], EXPc[:, :],
                start=True, stop=True)), waits=waits, inc="T", chk=f"ekmm{c}")
            wv = [wc(f"ekmm{c}")]
            if c >= 1:
                wv.append(wc(f"ektr{c-1}_3"))
            op("vector", (lambda e, bank=bank: e.tensor_copy(
                ekstrip[:, 0:512], bank[:, :])),
                waits=wv, inc="V", chk=f"ekstrip{c}")
            for t in range(4):
                wt = [wc(f"ekstrip{c}")]
                if c > 0 and t == 0:
                    wt.append(wc(f"ekcol{c-1}"))
                op("tensor", (lambda e, t=t: e.transpose(
                    pbf[:, t * 128:(t + 1) * 128],
                    ekstrip[:, t * 128:(t + 1) * 128], idn[:, :])),
                    waits=wt, inc="T", chk=f"ektr{c}_{t}")
            op("vector", (lambda e, c=c: e.tensor_copy(
                EKcol[:, c * 512:(c + 1) * 512], pbf[:, :])),
                waits=[wc(f"ektr{c}_3")], inc="V", chk=f"ekcol{c}")

        for kt in range(KT):
            wv = [wc("bc3V"), wc("bc2V")]
            if kt == 0:
                wv.append(wc("mrq7"))
            op("vector", (lambda e, kt=kt: e.tensor_tensor(
                mstage[:, 0:R], xnT[:, kt * R:(kt + 1) * R], mb_bf[:, :],
                op=ALU.subtract)),
                waits=wv, inc="V")
            op("vector", (lambda e, kt=kt: e.tensor_tensor(
                xnT[:, kt * R:(kt + 1) * R], mstage[:, 0:R], rb_bf[:, :], op=ALU.mult)),
                inc="V", chk=f"xn{kt}")
            Xh = kt // 16
            op("vector", (lambda e, kt=kt, Xh=Xh: e.tensor_tensor(
                wmbuf[:, kt * NGW:(kt + 1) * NGW].rearrange("p (nb w) -> p nb w", w=32),
                wslot[Xh % 2][:, (kt % 16) * NGW:((kt % 16) + 1) * NGW]
                    .rearrange("p (nb w) -> p nb w", w=32),
                EKcol[:, kt * 128:kt * 128 + 8].unsqueeze(-1).broadcast_to([128, 8, 32]),
                op=ALU.mult)),
                waits=([wc(f"stT{KT-1}")] if kt == 0 else []) +
                      [(f"G_wst{Xh % 2}", 16 * (Xh // 2 + 1)), wc(f"ekcol{kt//4}")],
                inc="V", chk=f"vwm0k{kt}")
            sp = (kt == KT - 1)
            wz = [wc(f"xn{kt}")]
            if kt == 0:
                wz += [wc("cmn"), wc("bc2V"), wc("bc3V"), wc("ekcol7"), wc("bthr")]
            op("tensor", (lambda e, kt=kt, sp=sp: e.matmul(
                pa[0:32, :], A_bf[:, kt * 32:(kt + 1) * 32], xnT[:, kt * R:kt * R + 512],
                start=(kt == 0), stop=sp, skip_group_check=True)), waits=wz)
            op("tensor", (lambda e, kt=kt, sp=sp: e.matmul(
                pb[0:32, :], A_bf[:, kt * 32:(kt + 1) * 32],
                xnT[:, kt * R + 512:kt * R + 1024],
                start=(kt == 0), stop=sp, skip_group_check=True)))
            op("tensor", (lambda e, kt=kt, sp=sp: e.matmul(
                pc[0:8, :], W8_bf[:, kt * 8:(kt + 1) * 8], xnT[:, kt * R:kt * R + 512],
                start=(kt == 0), stop=sp, skip_group_check=True)),
                inc=("T" if sp else None), chk=("zsw" if sp else None))
            for j in (0, 1):
                pmA = pm[2 * j]
                pmB = pm[2 * j + 1]
                wj = [(f"G_wbf{kt // 8}", 16)]
                if kt == 0:
                    wj.append(wc("stev"))
                wof = kt * NGW + j * 128
                mkchk = (j == 1 and kt % 8 == 7)
                op("tensor", (lambda e, pmA=pmA, wof=wof, kt=kt, st_=(kt == 0): e.matmul(
                    pmA[:, :], wmbuf[:, wof:wof + 128], xnT[:, kt * R:kt * R + 512],
                    start=st_, stop=False, skip_group_check=True)), waits=wj)
                op("tensor", (lambda e, pmB=pmB, wof=wof, kt=kt, st_=(kt == 0): e.matmul(
                    pmB[:, :], wmbuf[:, wof:wof + 128], xnT[:, kt * R + 512:kt * R + 1024],
                    start=st_, stop=False, skip_group_check=True)),
                    inc=("T" if mkchk else None),
                    chk=(f"tWMj1k{kt}" if mkchk else None))

        op("gpsimd", lambda e: e.dma_start(out=xw2[:, :], in_=wm3[:, 0, 16:32]),
           waits=[wc(f"stT{KT-1}")], inc="G_wst1", chk="wst0h1")
        for q in range(4):
            op("gpsimd", (lambda e, q=q: e.dma_start(
                out=wmbuf[:, q * 8 * NGW:(q + 1) * 8 * NGW],
                in_=W3[:, 0, 8 * q:8 * (q + 1)], accum_op=(ACC_OP or ALU.add))),
               waits=[wc(f"vwm0k{8*q+7}")], inc=f"G_wbf{q}", chk=f"wa0q{q}")
            if q == 1:
                op("gpsimd", lambda e: e.dma_start(out=wstage[:, :], in_=wm3[:, 1, 0:16]),
                   waits=[wc("vwm0k15")], inc="G_wst0", chk="wst1h0")
            if q == 3:
                op("gpsimd", lambda e: e.dma_start(out=xw2[:, :], in_=wm3[:, 1, 16:32]),
                   waits=[wc(f"vwm0k{KT-1}")], inc="G_wst1", chk="wst1h1")

        op("vector", lambda e: e.tensor_copy(zTb[:, 0:512], pa[0:32, :]),
           waits=[wc("zsw")], inc="V", chk="zloV")
        op("vector", lambda e: e.tensor_copy(zTb[:, 512:1024], pb[0:32, :]), inc="V",
           chk="zhiV")
        op("vector", lambda e: e.tensor_copy(H8T[:, 0:512], pc[0:8, :]), inc="V", chk="zevV")
        CHK["zevS"] = CHK["zevV"]
        for kt in range(KT):
            sp = (kt == KT - 1)
            waits = [wc("zevV")] if kt == 0 else []
            op("tensor", (lambda e, kt=kt, sp=sp: e.matmul(
                pc[0:8, :], W8_bf[:, kt * 8:(kt + 1) * 8],
                xnT[:, kt * R + 512:kt * R + 1024],
                start=(kt == 0), stop=sp, skip_group_check=True)),
                waits=waits, inc=("T" if sp else None), chk=("h8hi" if sp else None))
        op("vector", lambda e: e.tensor_copy(H8T[:, 512:1024], pc[0:8, :]),
           waits=[wc("h8hi")], inc="V", chk="h8hiV")

        op("scalar", lambda e: e.activation(LiM_bf[:, :], LiM[:, :], AF.Identity),
           waits=[wc("csts")], inc="SC", chk="limS")
        op("scalar", lambda e: e.activation(LiMs_bf[:, :], LiM[:, :], AF.Identity,
                                            scale=SCALING), inc="SC", chk="limsS")
        op("tensor", lambda e: e.transpose(pbf[0:32, 0:8], LiM_bf[:, :], idn[0:8, 0:8]),
           waits=[wc("limS"), wc("ekcol7")], inc="T", chk="limT")
        op("vector", lambda e: e.tensor_copy(LiMT_bf[:, :], pbf[0:32, 0:8]),
           waits=[wc("limT")], inc="V", chk="limV")

        op("tensor", lambda e: e.matmul(pa[0:8, :], LiMT_bf[:, :], zTb[:, 0:512],
                                        start=True, stop=True),
           waits=[wc("zevV"), wc("zevS"), wc("limV")], inc="T", chk="ds0")
        op("tensor", lambda e: e.matmul(pb[0:8, :], LiMT_bf[:, :], zTb[:, 512:1024],
                                        start=True, stop=True), inc="T", chk="ds1")
        op("vector", lambda e: e.tensor_copy(dsT[:, 0:512], pa[0:8, :]),
           waits=[wc("ds0")], inc="V")
        op("vector", lambda e: e.tensor_copy(dsT[:, 512:1024], pb[0:8, :]),
           waits=[wc("ds1")], inc="V", chk="dsV")

        for mt in range(8):
            waits = [wc("dsV"), wc("h8hiV")] if mt == 0 else []
            op("tensor", (lambda e, mt=mt: e.transpose(
                pa[:, mt * 8:(mt + 1) * 8], H8T[0:8, mt * 128:(mt + 1) * 128],
                idnf[0:8, 0:8])), waits=waits, inc="T")
            op("tensor", (lambda e, mt=mt: e.transpose(
                pb[:, mt * 8:(mt + 1) * 8], dsT[0:8, mt * 128:(mt + 1) * 128],
                idnf[0:8, 0:8])), inc="T", chk=("tokT" if mt == 7 else None))
        op("vector", lambda e: e.tensor_copy(H8tok[:, :], pa[:, 0:64]),
           waits=[wc("tokT")], inc="V")
        op("vector", lambda e: e.tensor_copy(dstok[:, :], pb[:, 0:64]), inc="V", chk="tokV")
        op("vector", lambda e: e.tensor_tensor(
            H8tok[:, :].rearrange("p (mt e) -> p mt e", e=8),
            H8tok[:, :].rearrange("p (mt e) -> p mt e", e=8),
            b8B[:, :].unsqueeze(1).broadcast_to([128, 8, 8]), op=ALU.add),
            waits=[wc("b8V")], inc="V", chk="h8bias")

        op("vector", lambda e: e.reduce_max(gat[:, 0:1], H8tok[:, :], axis=AX.X,
                                            apply_absolute_value=True),
           waits=[wc("h8bias")], inc="V")
        op("vector", lambda e: e.reduce_max(gat[:, 1:2], dstok[:, :], axis=AX.X,
                                            apply_absolute_value=True),
           waits=[wc("tokV")], inc="V")
        op("vector", lambda e: e.tensor_tensor(gat[:, 0:2], gat[:, 0:2],
                                               oddm[:, :].broadcast_to([128, 2]),
                                               op=ALU.mult),
           waits=[wc("csts")], inc="V", chk="rmax")
        op("tensor", lambda e: e.transpose(pc[0:2, 0:128], gat[:, 0:2], idnf[:, :]),
           waits=[wc("rmax"), wc("bthr")], inc="T", chk="mxT")
        op("vector", lambda e: e.reduce_max(gat[0:2, 8:9], pc[0:2, 0:128], axis=AX.X),
           waits=["mxT"], inc="V", chk="mxV")
        op("sync", lambda e: e.dma_start(out=cc_m_in.ap(), in_=gat[0:2, 8:9]),
           waits=[wc("mxV")], inc="S_ccm", chk="ccmin")
        if SKIP_CC:
            op("gpsimd", lambda e: e.dma_start(out=cc_m_out.ap(), in_=cc_m_in.ap()),
               waits=[wc("ccmin")], inc="CC2", chk="ar")
        else:
            op("gpsimd", lambda e: e.collective_compute(
                "AllReduce", ALU.max, replica_groups=[list(range(NCORES))],
                ins=[cc_m_in.ap().opt()], outs=[cc_m_out.ap().opt()]),
                waits=[wc("ccmin")], inc="CC2", chk="ar")
        op("gpsimd", lambda e: e.dma_start(out=gmax[0:1, 0:2], in_=cc_m_out.ap()),
           waits=[("CC2", 1)], inc="G_cc2", chk="gmaxin")

        op("vector", lambda e: e.tensor_scalar(gmax[0:1, 2:4], gmax[0:1, 0:2], EPS_G, None,
                                               op0=ALU.max),
           waits=[wc("gmaxin")], inc="V")
        op("vector", lambda e: e.reciprocal(gmax[0:1, 4:6], gmax[0:1, 2:4]), inc="V")
        op("vector", lambda e: e.tensor_scalar(gmax[0:1, 6:8], gmax[0:1, 4:6], GAMMA, None,
                                               op0=ALU.mult), inc="V", chk="scV")
        op("tensor", lambda e: e.matmul(pc[:, 0:2], onesc[:, :], gmax[0:1, 6:8],
                                        start=True, stop=True),
           waits=[wc("scV")], inc="T", chk="scT")
        op("vector", lambda e: e.tensor_copy(scB[:, :], pc[:, 0:2]),
           waits=["scT"], inc="V", chk="scBV")

        GA = gat

        def V3(ap):
            return ap.rearrange("p (mt e) -> p mt e", e=8)

        def bc8(ap):
            return ap.unsqueeze(-1).broadcast_to([128, 8, 8])

        op("vector", lambda e: e.tensor_scalar(GA[:, 64:128], H8tok[:, :], scB[:, 0:1], None,
                                               op0=ALU.mult),
           waits=[wc("scBV")], inc="V")
        op("vector", lambda e: e.scalar_tensor_tensor(GA[:, 128:192], dstok[:, :],
                                                      scB[:, 1:2], GA[:, 64:128],
                                                      op0=ALU.mult, op1=ALU.add),
           inc="V", chk="logit")
        op("scalar", lambda e: e.activation(GA[:, 192:256], GA[:, 128:192], AF.Exp),
           waits=[wc("logit")], inc="SC", chk="expS")
        op("vector", lambda e: e.tensor_reduce(GA[:, 256:264], V3(GA[:, 192:256]),
                                               axis=AX.X, op=ALU.add),
           waits=[wc("expS")], inc="V")
        op("vector", lambda e: e.reciprocal(GA[:, 336:344], GA[:, 256:264]), inc="V")
        op("vector", lambda e: e.tensor_tensor(V3(GA[:, 320:384]), V3(GA[:, 192:256]),
                                               bc8(GA[:, 336:344]), op=ALU.mult),
           inc="V", chk="probs")
        op("vector", lambda e: e.tensor_reduce(GA[:, 272:280], V3(GA[:, 320:384]),
                                               axis=AX.X, op=ALU.max), inc="V")
        op("vector", lambda e: e.tensor_tensor(V3(GA[:, 384:448]), V3(GA[:, 320:384]),
                                               bc8(GA[:, 272:280]), op=ALU.is_ge), inc="V")
        op("vector", lambda e: e.tensor_reduce(GA[:, 264:272], V3(GA[:, 384:448]),
                                               axis=AX.X, op=ALU.add), inc="V")
        op("vector", lambda e: e.scalar_tensor_tensor(GA[:, 448:512], GA[:, 384:448], -1e30,
                                                      GA[:, 320:384], op0=ALU.mult,
                                                      op1=ALU.add), inc="V")
        op("vector", lambda e: e.tensor_reduce(GA[:, 280:288], V3(GA[:, 448:512]),
                                               axis=AX.X, op=ALU.max), inc="V")
        op("vector", lambda e: e.tensor_scalar(GA[:, 288:296], GA[:, 264:272], 2.0, None,
                                               op0=ALU.is_ge), inc="V")
        op("vector", lambda e: e.tensor_tensor(GA[:, 296:304], GA[:, 272:280],
                                               GA[:, 280:288], op=ALU.subtract), inc="V")
        op("vector", lambda e: e.tensor_tensor(GA[:, 296:304], GA[:, 296:304],
                                               GA[:, 288:296], op=ALU.mult), inc="V")
        op("vector", lambda e: e.tensor_tensor(GA[:, 304:312], GA[:, 296:304],
                                               GA[:, 280:288], op=ALU.add), inc="V",
           chk="thr")
        op("vector", lambda e: e.tensor_tensor(V3(GA[:, 512:576]), bc8(GA[:, 304:312]),
                                               V3(GA[:, 320:384]), op=ALU.subtract),
           inc="V", chk="preV")
        op("scalar", lambda e: e.activation(GA[:, 576:640], GA[:, 512:576], AF.Exp,
                                            scale=2.0),
           waits=[wc("preV"), wc("ekcol7")], inc="SC", chk="qS")
        op("vector", lambda e: e.tensor_scalar(GA[:, 640:704], GA[:, 576:640], 1.0, None,
                                               op0=ALU.add),
           waits=[wc("qS")], inc="V")
        op("vector", lambda e: e.reciprocal(GA[:, 704:768], GA[:, 640:704]), inc="V")
        op("vector", lambda e: e.tensor_tensor(GA[:, 768:832], GA[:, 320:384],
                                               GA[:, 704:768], op=ALU.mult), inc="V")
        op("vector", lambda e: e.tensor_reduce(GA[:, 312:320], V3(GA[:, 768:832]),
                                               axis=AX.X, op=ALU.add), inc="V")
        op("vector", lambda e: e.tensor_scalar(GA[:, 312:320], GA[:, 312:320], 1e-9, None,
                                               op0=ALU.add), inc="V")
        op("vector", lambda e: e.reciprocal(GA[:, 328:336], GA[:, 312:320]), inc="V")
        op("vector", lambda e: e.tensor_tensor(V3(wnb[:, :]), V3(GA[:, 768:832]),
                                               bc8(GA[:, 328:336]), op=ALU.mult),
           inc="V", chk="wnV")

        ev_cnt = {}
        out_cnt = {}

        def plan_main_block(j):
            g = j // 2
            par = j % 2
            pmA = pm[2 * par]
            pmB = pm[2 * par + 1]
            gbase = (g % 2) * (KT * NGW)
            for kt in range(KT):
                waits = [(f"G_wbf{kt // 8}", 16 * (g + 1))]
                if kt == 0:
                    waits.append(("V", ev_cnt[j - 2]))
                wof = gbase + kt * NGW + par * 128
                mkchk = (par == 1 and kt % 8 == 7)
                op("tensor", (lambda e, pmA=pmA, wof=wof, kt=kt, st_=(kt == 0): e.matmul(
                    pmA[:, :], wmbuf[:, wof:wof + 128], xnT[:, kt * R:kt * R + 512],
                    start=st_, stop=False, skip_group_check=True)), waits=waits)
                op("tensor", (lambda e, pmB=pmB, wof=wof, kt=kt, st_=(kt == 0): e.matmul(
                    pmB[:, :], wmbuf[:, wof:wof + 128], xnT[:, kt * R + 512:kt * R + 1024],
                    start=st_, stop=False, skip_group_check=True)),
                    inc=("T" if mkchk else None),
                    chk=(f"tWMj{j}k{kt}" if mkchk else None))

        def plan_delta_and_evac(j):
            par = j % 2
            pmA = pm[2 * par]
            pmB = pm[2 * par + 1]
            op("tensor", (lambda e, pmA=pmA, j=j: e.matmul(
                pmA[:, :], Bm_bf[:, j * 128:(j + 1) * 128], mulT[:, 0:512],
                start=False, stop=True, skip_group_check=True)),
                waits=[wc("vmul")], inc="T")
            op("tensor", (lambda e, pmB=pmB, j=j: e.matmul(
                pmB[:, :], Bm_bf[:, j * 128:(j + 1) * 128], mulT[:, 512:1024],
                start=False, stop=True, skip_group_check=True)), inc="T", chk=f"tDj{j}")
            rsl = j % 2
            wres = [wc(f"vwm0k{KT-1}")] if j < 2 else [("V", ev_cnt[j - 2])]
            if j < 2:
                wres.append(wc(f"xn{KT-1}"))
            op("sync", (lambda e, j=j, rsl=rsl: e.dma_start(
                out=xf[:, rsl * R:(rsl + 1) * R], in_=d_xT[j * 128:(j + 1) * 128, :])),
                waits=wres, inc=f"S_res{rsl}", chk=f"res{j}")
            for h in range(2):
                bank = pmA if h == 0 else pmB
                osl = (2 * j + h) % 2
                waits = [wc(f"tDj{j}"), wc(f"res{j}")]
                if 2 * j + h >= 2:
                    prev = 2 * j + h - 2
                    waits.append(out_cnt[(prev // 2, prev % 2)])
                op("vector", (lambda e, bank=bank, osl=osl, rsl=rsl, j=j, h=h:
                    e.scalar_tensor_tensor(
                        outstg[:, osl * 512:(osl + 1) * 512], bank[:, :], b_col[:, j:j + 1],
                        xf[:, rsl * R + h * 512:rsl * R + (h + 1) * 512],
                        op0=ALU.add, op1=ALU.add)),
                    waits=waits, inc="V", chk=f"ev{j}_{h}")
                op("scalar", (lambda e, j=j, h=h, osl=osl: e.dma_start(
                    out=d_out[j * 128:(j + 1) * 128, h * 512:(h + 1) * 512],
                    in_=outstg[:, osl * 512:(osl + 1) * 512])),
                    waits=[wc(f"ev{j}_{h}")], inc=f"SCQ{osl}")
                out_cnt[(j, h)] = (f"SCQ{osl}", CNT[f"SCQ{osl}"])
            ev_cnt[j] = CHK[f"ev{j}_1"][1]

        def plan_vwm(g):
            base = (g % 2) * (KT * NGW)
            for kt in range(KT):
                X = 2 * g + kt // 16
                waits = [(f"G_wst{X % 2}", 16 * (X // 2 + 1))]
                if g >= 2:
                    waits.append(wc(f"tWMj{2*g-3}k{8*(kt//8)+7}"))
                op("vector", (lambda e, g=g, kt=kt, base=base, X=X: e.tensor_tensor(
                    wmbuf[:, base + kt * NGW:base + (kt + 1) * NGW]
                        .rearrange("p (nb w) -> p nb w", w=32),
                    wslot[X % 2][:, (kt % 16) * NGW:((kt % 16) + 1) * NGW]
                        .rearrange("p (nb w) -> p nb w", w=32),
                    EKcol[:, kt * 128 + 8 * g:kt * 128 + 8 * g + 8]
                        .unsqueeze(-1).broadcast_to([128, 8, 32]),
                    op=ALU.mult)),
                    waits=waits, inc="V", chk=f"vwm{g}k{kt}")

        for mt in range(8):
            waits = [wc("wnV")] if mt == 0 else []
            op("tensor", (lambda e, mt=mt: e.matmul(
                pa[:, mt * 8:(mt + 1) * 8], pairP[:, :], wnb[:, mt * 8:(mt + 1) * 8],
                start=True, stop=True)), waits=waits,
                inc=("T" if mt == 7 else None), chk=("wf8" if mt == 7 else None))
        op("vector", lambda e: e.tensor_copy(wfb[:, :], pa[:, 0:64]),
           waits=[wc("wf8")], inc="V", chk="wfbV")
        for rnd in range(2):
            for t in range(4):
                mt = rnd * 4 + t
                waits = [wc("wfbV")]
                if rnd == 1 and t == 0:
                    waits.append(wc("wfT0V"))
                op("tensor", (lambda e, mt=mt, t=t: e.transpose(
                    pbf[0:8, t * 128:(t + 1) * 128], wfb[:, mt * 8:(mt + 1) * 8],
                    idn[:, :])),
                    waits=waits, inc=("T" if t == 3 else None),
                    chk=(f"wfTr{rnd}" if t == 3 else None))
            op("vector", (lambda e, rnd=rnd: e.tensor_copy(
                wfT[0:8, rnd * 512:(rnd + 1) * 512], pbf[0:8, 0:512])),
                waits=[wc(f"wfTr{rnd}")], inc="V", chk=f"wfT{rnd}V")
        op("tensor", lambda e: e.matmul(pa[0:32, :], LiMs_bf[:, :], wfT[0:8, 0:512],
                                        start=True, stop=True),
           waits=[wc("wfT1V"), wc("limsS")], inc="T", chk="pmx0")
        op("tensor", lambda e: e.matmul(pb[0:32, :], LiMs_bf[:, :], wfT[0:8, 512:1024],
                                        start=True, stop=True), inc="T", chk="pmx1")
        op("vector", lambda e: e.tensor_tensor(mulT[:, 0:512], pa[0:32, :], zTb[:, 0:512],
                                               op=ALU.mult),
           waits=[wc("pmx0")], inc="V")
        op("vector", lambda e: e.tensor_tensor(mulT[:, 512:1024], pb[0:32, :],
                                               zTb[:, 512:1024], op=ALU.mult),
           waits=[wc("pmx1")], inc="V", chk="vmul")

        plan_delta_and_evac(0)
        plan_delta_and_evac(1)
        plan_vwm(1)
        plan_vwm(2)
        nb_eff = NB if NB_EFF is None else NB_EFF
        for j in range(2, nb_eff):
            plan_main_block(j)
            plan_delta_and_evac(j)
            if j % 2 == 1 and (j + 3) // 2 < NG:
                plan_vwm((j + 3) // 2)

        for g in range(1, (NB if NB_EFF is None else NB_EFF) // 2):
            gb = (g % 2) * (KT * NGW)
            for q in range(4):
                op("gpsimd", (lambda e, g=g, gb=gb, q=q: e.dma_start(
                    out=wmbuf[:, gb + q * 8 * NGW:gb + (q + 1) * 8 * NGW],
                    in_=W3[:, g, 8 * q:8 * (q + 1)],
                    accum_op=(ACC_OP_H or ACC_OP or ALU.add))),
                   waits=[wc(f"vwm{g}k{8*q+7}"), wc(f"tWMj{2*g-1}k{8*q+7}")],
                   inc=f"G_wbf{q}", chk=f"wa{g}q{q}")
            if g + 1 < NG:
                op("gpsimd", (lambda e, g=g: e.dma_start(
                    out=wslot[0][:, :], in_=wm3[:, g + 1, 0:16])),
                   waits=[wc(f"vwm{g}k15")], inc="G_wst0", chk=f"wst{g+1}h0")
                op("gpsimd", (lambda e, g=g: e.dma_start(
                    out=wslot[1][:, :], in_=wm3[:, g + 1, 16:32])),
                   waits=[wc(f"vwm{g}k{KT-1}")], inc="G_wst1", chk=f"wst{g+1}h1")

        if debug_taps:
            tap_list = [
                ("t_xnT", xnT[:, :], [wc(f"xn{KT-1}")]),
                ("t_EKcol", EKcol[:, :], [wc("ekcol7")]),
                ("t_Ball", Ball[:, :], [wc("ballin")]),
                ("t_H8tok", H8tok[:, :], [wc("h8bias")]),
                ("t_dstok", dstok[:, :], [wc("tokV")]),
                ("t_wnb", wnb[:, :], [wc("wnV")]),
                ("t_mulT", mulT[:, :], [wc("vmul")]),
                ("t_gmax", gmax[0:1, 0:8], [wc("scV")]),
                ("t_Sn", Sn[:, :], [wc("mrq7")]),
                ("t_zT", zTb[:, :], [wc("zhiV")]),

            ]
            for nm, src, waits in tap_list:
                op("sync", (lambda e, nm=nm, src=src: e.dma_start(out=taps[nm], in_=src)),
                   waits=waits, inc="S_cst")

        PLAN["scalar"].append((None, [("SCQ0", CNT["SCQ0"]), ("SCQ1", CNT["SCQ1"])], None))
        PLAN["sync"].append((None, [("S_cst", CNT["S_cst"])], None))

        def make_runner(eng_name):
            own_sem = {"vector": "V", "scalar": "SC"}.get(eng_name)
            def run(engine):
                seen = {}
                own = 0
                for emit, waits, inc in PLAN[eng_name]:
                    if own_sem is not None and emit is not None and own > 0:
                        if seen.get(own_sem, 0) < own:
                            engine.wait_ge(SEM[own_sem], own)
                            seen[own_sem] = own
                    for wspec in waits:
                        if isinstance(wspec, str):
                            if wspec not in CHK and STOP_N is not None:
                                continue
                            semn, val = CHK[wspec]
                        else:
                            semn, val = wspec
                        if seen.get(semn, 0) < val:
                            engine.wait_ge(SEM[semn], val)
                            seen[semn] = val
                    if emit is None:
                        continue
                    inst = emit(engine)
                    if inc == own_sem:
                        own += 1
                    if inc is not None:
                        if inc in CC_SEMS:
                            if SKIP_CC:
                                inst.then_inc(SEM[inc], 16)
                            else:
                                inst.then_inc(SEM[inc])
                        elif inc in DMA_SEMS:
                            inst.then_inc(SEM[inc], 16)
                        else:
                            inst.then_inc(SEM[inc], 1)
            return run

        block.sync(make_runner("sync"))
        block.gpsimd(make_runner("gpsimd"))
        block.scalar(make_runner("scalar"))
        block.tensor(make_runner("tensor"))
        block.vector(make_runner("vector"))

    return nc


def _consts():
    idn = np.eye(128, dtype=ml_dtypes.bfloat16)
    idnf = np.eye(128, dtype=np.float32)
    avgp = np.zeros((128, 128), dtype=ml_dtypes.bfloat16)
    avgp[:, 0] = ml_dtypes.bfloat16(1.0 / 4096.0)
    EXP = np.zeros((16, 512), dtype=ml_dtypes.bfloat16)
    for kb in range(16):
        EXP[kb, kb * 32:(kb + 1) * 32] = 1
    Q4 = np.zeros((128, 4), dtype=ml_dtypes.bfloat16)
    for p in range(128):
        Q4[p, p // 32] = 1
    oddm = np.zeros((128, 1), np.float32)
    oddm[1::2, 0] = 1.0
    onesc = np.ones((1, 128), np.float32)
    pairP = np.zeros((128, 128), dtype=ml_dtypes.bfloat16)
    for m in range(128):
        pairP[m | 1, m] = 1
    return idn, idnf, avgp, EXP, Q4, oddm, onesc, pairP


def _tile_khost(M):
    K, C = M.shape
    return np.ascontiguousarray(M.reshape(32, 128, C).transpose(1, 0, 2).reshape(128, 32 * C))


def _tile_weight(Wmat):
    arr = Wmat.reshape(NG, NGW, KT, 128)
    return np.ascontiguousarray(arr.transpose(3, 0, 2, 1).reshape(128, NG * KT * NGW))


def kernel(L0_attn_data, L0_attn_weight, L0_attn_mask, L0_ffn_W, L0_ffn_b,
           L0_ffn_A, L0_ffn_Bmat, L0_ffn_LiMEs):
    X = np.ascontiguousarray(np.asarray(L0_attn_data, np.float32).reshape(2 * 4096, D))
    weight = np.asarray(L0_attn_weight, np.float32)
    mask = np.asarray(L0_attn_mask, np.int32)
    W = np.asarray(L0_ffn_W, np.float32)
    bvec = np.asarray(L0_ffn_b, np.float32)
    A = np.asarray(L0_ffn_A, np.float32)
    Bmat = np.ascontiguousarray(np.asarray(L0_ffn_Bmat, np.float32))
    LiMa = np.ascontiguousarray(np.asarray(L0_ffn_LiMEs, np.float32))

    idn, idnf, avgp, EXP, Q4, oddm, onesc, pairP = _consts()

    wmh = _tile_weight(weight)
    Wh = _tile_weight(W)
    Ah = _tile_khost(A)
    W8h = _tile_khost(np.ascontiguousarray(W[0:8, :].T))
    b32 = np.ascontiguousarray(bvec.reshape(32, 128))
    b8 = np.ascontiguousarray(bvec[0:8].reshape(1, 8))
    maskT = np.ascontiguousarray(mask.T)

    nc = build_nc(debug_taps=DEBUG_TAPS)

    in_maps = []
    for c in range(NCORES):
        xT_c = np.ascontiguousarray(X[c * R:(c + 1) * R, :].T)
        msk_c = _tile_khost(np.ascontiguousarray(maskT[:, c * 512:(c + 1) * 512]))
        in_maps.append({
            "xT": xT_c, "wmh": wmh, "Wh": Wh, "mskh": msk_c,
            "Ah": Ah, "W8h": W8h, "b32": b32, "b8": b8, "Bm": Bmat, "LiM": LiMa,
            "idn": idn, "idnf": idnf, "avgp": avgp, "EXPc": EXP, "Q4c": Q4,
            "oddm": oddm, "onesc": onesc, "pairP": pairP,
        })

    res = run_bass_kernel_spmd(nc, in_maps, core_ids=list(range(NCORES)))
    out = np.empty((2 * 4096, D), np.float32)
    for c in range(NCORES):
        out[c * R:(c + 1) * R, :] = np.asarray(res.results[c]["outT"]).T
    if DEBUG_TAPS:
        kernel.last_results = res
    return out.reshape(2, 4096, D)


# version 11
# speedup vs baseline: 1.9308x; 1.0064x over previous
import numpy as np
import ml_dtypes
import concourse.bass as bass
import concourse.mybir as mybir
from concourse.bass_utils import run_bass_kernel_spmd

F32 = mybir.dt.float32
BF16 = mybir.dt.bfloat16
I32 = mybir.dt.int32
AF = mybir.ActivationFunctionType
ALU = mybir.AluOpType
AX = mybir.AxisListType

NCORES = 8
D = 4096
R = 1024
KT = 32
NGW = 256
NG = D // NGW
NB = 32
EPS_LN = 1e-5
EPS_G = 1e-6
GAMMA = 0.5
SCALING = 16.0 / 32.0

DEBUG_TAPS = False
SKIP_CC = False
NB_EFF = None
PHASE1_ONLY = False
STOP_N = None
ACC_OP = None
ACC_OP_H = None

ENGS = ("sync", "gpsimd", "scalar", "tensor", "vector")


def build_nc(debug_taps=False):
    nc = bass.Bass("TRN2", target_bir_lowering=False, debug=False,
                   dynamic_dma_scratch_size=16384)

    d_xT = nc.dram_tensor("xT", [D, R], F32, kind="ExternalInput").ap()
    d_wm = nc.dram_tensor("wmh", [128, KT * D], F32, kind="ExternalInput").ap()
    d_W = nc.dram_tensor("Wh", [128, KT * D], F32, kind="ExternalInput").ap()
    d_msk = nc.dram_tensor("mskh", [128, KT * 512], I32, kind="ExternalInput").ap()
    d_A = nc.dram_tensor("Ah", [128, KT * 40], F32, kind="ExternalInput").ap()
    d_LiMT = nc.dram_tensor("LiMT", [32, 8], F32, kind="ExternalInput").ap()
    d_onebf = nc.dram_tensor("onebf", [128, 1], BF16, kind="ExternalInput").ap()
    d_b32 = nc.dram_tensor("b32", [32, 128], F32, kind="ExternalInput").ap()
    d_b8 = nc.dram_tensor("b8", [1, 8], F32, kind="ExternalInput").ap()
    d_Bm = nc.dram_tensor("Bm", [32, D], F32, kind="ExternalInput").ap()
    d_LiM = nc.dram_tensor("LiM", [8, 32], F32, kind="ExternalInput").ap()
    d_idn = nc.dram_tensor("idn", [128, 128], BF16, kind="ExternalInput").ap()
    d_idnf = nc.dram_tensor("idnf", [128, 128], F32, kind="ExternalInput").ap()
    d_avgp = nc.dram_tensor("avgp", [128, 128], BF16, kind="ExternalInput").ap()
    d_EXP = nc.dram_tensor("EXPc", [16, 512], BF16, kind="ExternalInput").ap()
    d_Q4 = nc.dram_tensor("Q4c", [128, 4], BF16, kind="ExternalInput").ap()
    d_oddm = nc.dram_tensor("oddm", [128, 1], F32, kind="ExternalInput").ap()
    d_onesc = nc.dram_tensor("onesc", [1, 128], F32, kind="ExternalInput").ap()
    d_pairP = nc.dram_tensor("pairP", [128, 128], BF16, kind="ExternalInput").ap()

    d_out = nc.dram_tensor("outT", [D, R], F32, kind="ExternalOutput").ap()

    cc_b_in = nc.dram_tensor("cc_b_in", [16, 128], BF16)
    cc_b_out = nc.dram_tensor("cc_b_out", [128, 128], BF16)
    cc_m_in = nc.dram_tensor("cc_m_in", [1, 2], F32)
    cc_m_out = nc.dram_tensor("cc_m_out", [1, 2], F32)
    cc_d_in = nc.dram_tensor("cc_d_in", [1, 1], F32)
    cc_d_out = nc.dram_tensor("cc_d_out", [8, 1], F32)

    taps = {}
    if debug_taps:
        for nm, shp, dt in [("t_xnT", [128, KT * R], BF16), ("t_EKcol", [128, KT * 128], BF16),
                            ("t_Ball", [128, 128], BF16), ("t_H8tok", [128, 64], F32),
                            ("t_dstok", [128, 64], F32), ("t_wnb", [128, 64], BF16),
                            ("t_mulT", [32, R], BF16), ("t_gmax", [1, 8], F32),
                            ("t_Sn", [128, 512], BF16), ("t_zT", [32, R], BF16)]:
            taps[nm] = nc.dram_tensor(nm, shp, dt, kind="ExternalOutput").ap()

    from contextlib import ExitStack
    with ExitStack() as st:
        E = st.enter_context

        xnT = E(nc.sbuf_tensor("xnT", [128, KT * R], BF16))
        wstage = E(nc.sbuf_tensor("wstage", [128, 16 * NGW], BF16))
        wmbuf = E(nc.sbuf_tensor("wmbuf", [128, 2 * KT * NGW], BF16))
        mstage = E(nc.sbuf_tensor("mstage", [128, 8 * 512], BF16))
        EKcol = E(nc.sbuf_tensor("EKcol", [128, KT * 128], BF16))
        xf = E(nc.sbuf_tensor("xf", [128, 4 * R], F32))

        rstdB = E(nc.sbuf_tensor("rstdB", [128, R], F32))
        stats = E(nc.sbuf_tensor("stats", [1, 3072], F32))
        outstg = E(nc.sbuf_tensor("outstg", [128, 1024], F32))
        xw2 = E(nc.sbuf_tensor("xw2", [128, 16 * NGW], BF16))
        xab = xw2[:, :]
        xsqb = None
        gat = E(nc.sbuf_tensor("gat", [128, 832], F32))
        Sn = E(nc.sbuf_tensor("Sn", [128, 512], BF16))
        B16 = E(nc.sbuf_tensor("B16", [16, 128], BF16))
        Ball = E(nc.sbuf_tensor("Ball", [128, 128], BF16))
        BallT = E(nc.sbuf_tensor("BallT", [128, 128], BF16))
        zTb = E(nc.sbuf_tensor("zTb", [32, R], BF16))
        H8tok = E(nc.sbuf_tensor("H8tok", [128, 64], F32))
        dstok = E(nc.sbuf_tensor("dstok", [128, 64], F32))
        wnb = E(nc.sbuf_tensor("wnb", [128, 64], BF16))
        wfb = E(nc.sbuf_tensor("wfb", [128, 64], BF16))
        mulT = E(nc.sbuf_tensor("mulT", [32, R], BF16))
        A_bf = E(nc.sbuf_tensor("A_bf", [128, KT * 40], BF16))
        zfx = E(nc.sbuf_tensor("zfx", [40, R], F32))
        H8Tn = E(nc.sbuf_tensor("H8Tn", [8, R], F32))
        mrow_bf = E(nc.sbuf_tensor("mrow_bf", [1, R], BF16))
        negasum = E(nc.sbuf_tensor("negasum", [1, 40], BF16))
        onebf = E(nc.sbuf_tensor("onebf_sb", [128, 1], BF16))
        LiMTf = E(nc.sbuf_tensor("LiMTf", [32, 8], F32))
        Bm_bf = E(nc.sbuf_tensor("Bm_bf", [32, D], BF16))
        b_sb = E(nc.sbuf_tensor("b_sb", [32, 128], F32))
        b_col = E(nc.sbuf_tensor("b_col", [128, 32], F32))
        b8B = E(nc.sbuf_tensor("b8B", [128, 8], F32))
        b8r = E(nc.sbuf_tensor("b8r", [1, 8], F32))
        LiM = E(nc.sbuf_tensor("LiMs", [8, 32], F32))
        LiMs_bf = E(nc.sbuf_tensor("LiMsc_bf", [8, 32], BF16))
        LiMT_bf = E(nc.sbuf_tensor("LiMT_bf", [32, 8], BF16))
        idn = E(nc.sbuf_tensor("idn_sb", [128, 128], BF16))
        idnf = E(nc.sbuf_tensor("idnf_sb", [128, 128], F32))
        avgp = E(nc.sbuf_tensor("avgp_sb", [128, 128], BF16))
        EXPc = E(nc.sbuf_tensor("EXP_sb", [16, 512], BF16))
        balltc = E(nc.sbuf_tensor("balltc", [16, 128], BF16))
        Q4 = E(nc.sbuf_tensor("Q4_sb", [128, 4], BF16))
        oddm = E(nc.sbuf_tensor("oddm_sb", [128, 1], F32))
        onesc = E(nc.sbuf_tensor("onesc_sb", [1, 128], F32))
        pairP = E(nc.sbuf_tensor("pairP_sb", [128, 128], BF16))
        gmax = E(nc.sbuf_tensor("gmax", [1, 16], F32))
        scB = E(nc.sbuf_tensor("scB", [128, 2], F32))

        xsqb = wmbuf[:, 0:16 * NGW]
        meanB = outstg[:, :]
        mb_bf = rstdB[:, 0:512].bitcast(BF16)
        rb_bf = rstdB[:, 512:1024].bitcast(BF16)
        dsT = outstg[0:8, :]
        wfT = mulT[0:8, :]
        wslot = [wstage, xw2]
        tmpf = mstage[:, :].bitcast(F32)
        ekstrip = gat[:, 576:832].bitcast(BF16)

        pm = [E(nc.psum_tensor(f"pm{i}", [128, 512], F32)) for i in range(4)]
        pa = E(nc.psum_tensor("pa", [128, 512], F32))
        pb = E(nc.psum_tensor("pb", [128, 512], F32))
        pc = E(nc.psum_tensor("pc", [128, 512], F32))
        pbf = E(nc.psum_tensor("pbf", [128, 512], BF16))

        sem_names = ["S_cst", "S_xa0", "S_xa1", "S_xb0", "S_xb1", "S_res0", "S_res1",
                     "S_ccb", "S_ccm", "G_msk0", "G_msk1", "G_wst0", "G_wst1", "G_wbf0", "G_wbf1", "G_wbf2", "G_wbf3", "G_cmn", "G_cc", "G_cc2",
                     "SCQ0", "SCQ1", "CC0", "CC1", "CC2", "T", "V", "SC"]
        SEM = {nm: E(nc.semaphore(nm)) for nm in sem_names}
        DMA_SEMS = {"S_cst", "S_xa0", "S_xa1", "S_xb0", "S_xb1", "S_res0", "S_res1",
                    "S_ccb", "S_ccm", "G_msk0", "G_msk1", "G_wst0", "G_wst1", "G_wbf0", "G_wbf1", "G_wbf2", "G_wbf3", "G_cmn", "G_cc", "G_cc2",
                    "SCQ0", "SCQ1"}
        CC_SEMS = {"CC0", "CC1", "CC2"}

        block = E(nc.Block())

        PLAN = {e: [] for e in ENGS}
        CNT = {nm: 0 for nm in sem_names}
        CHK = {}

        nops = [0]

        def op(eng, emit, waits=(), inc=None, chk=None):
            nops[0] += 1
            if STOP_N is not None and nops[0] > STOP_N:
                return
            PLAN[eng].append((emit, list(waits), inc))
            if inc is not None:
                CNT[inc] += 16 if inc in DMA_SEMS else 1
                if chk is not None:
                    CHK[chk] = (inc, CNT[inc])

        def wc(name):
            if name not in CHK and STOP_N is not None:
                return ("S_cst", 0)
            return CHK[name]

        wm3 = d_wm.rearrange("p (g t n) -> p g t n", g=NG, t=KT)
        W3 = d_W.rearrange("p (g t n) -> p g t n", g=NG, t=KT)


        for dst, src in [(idn[:, :], d_idn), (idnf[:, :], d_idnf), (avgp[:, :], d_avgp),
                         (EXPc[:, :], d_EXP), (Q4[:, :], d_Q4), (oddm[:, :], d_oddm),
                         (onesc[:, :], d_onesc), (pairP[:, :], d_pairP),
                         (b_sb[:, :], d_b32), (LiM[:, :], d_LiM), (b8r[:, :], d_b8),
                         (LiMTf[:, :], d_LiMT), (onebf[:, :], d_onebf)]:
            op("sync", (lambda e, a=dst, b=src: e.dma_start(out=a, in_=b)), inc="S_cst")
        CHK["csts"] = ("S_cst", CNT["S_cst"])

        def em_mask(q):
            sl = q % 2
            def f(e):
                return e.dma_start(out=mstage[:, sl * 2048:(sl + 1) * 2048],
                                   in_=d_msk[:, q * 2048:(q + 1) * 2048],
                                   max_dma_last_dim=8192)
            return f
        if not SKIP_CC:
            op("gpsimd", lambda e: e.dma_start(out=cc_d_in.ap(), in_=d_b8[0:1, 0:1]),
               inc="G_cmn")
            op("gpsimd", lambda e: e.collective_compute(
                "AllGather", ALU.bypass, replica_groups=[list(range(NCORES))],
                ins=[cc_d_in.ap().opt()], outs=[cc_d_out.ap().opt()]),
                waits=[("G_cmn", 16)], inc="CC0")
        op("gpsimd", em_mask(0), inc="G_msk0", chk="mq0")
        op("gpsimd", em_mask(1), inc="G_msk1", chk="mq1")
        op("gpsimd", lambda e: e.dma_start(out=wstage[:, :], in_=wm3[:, 0, 0:16]),
           inc="G_wst0", chk="wst0h0")
        op("gpsimd", lambda e: e.dma_start(out=A_bf[:, :], in_=d_A), inc="G_cmn")
        op("gpsimd", lambda e: e.dma_start(out=Bm_bf[:, :], in_=d_Bm), inc="G_cmn", chk="cmn")

        def emit_mask_batch(q):
            sl = q % 2
            for t in range(4):
                ktg = 4 * q + t

                def em_red(e, t=t, ktg=ktg, sl=sl):
                    with nc.allow_low_precision(reason="mask block-sums <=32 exact in bf16"):
                        return e.tensor_reduce(
                            Sn[:, ktg * 16:(ktg + 1) * 16],
                            mstage[:, sl * 2048 + t * 512:sl * 2048 + (t + 1) * 512]
                                .rearrange("p (nb w) -> p nb w", w=32),
                            axis=AX.X, op=ALU.add)
                op("vector", em_red, waits=[wc(f"mq{q}")], inc="V")
            CHK[f"mrq{q}"] = ("V", CNT["V"])
            if q < 6:
                op("gpsimd", em_mask(q + 2), waits=[wc(f"mrq{q}")], inc=f"G_msk{q % 2}",
                   chk=f"mq{q+2}")

        for kt in range(KT):
            w_ = [wc("cmn"), wc("csts")] if kt == 0 else []
            op("tensor", (lambda e, kt=kt: e.matmul(
                pc[0:1, 0:40], onebf[:, 0:1], A_bf[:, kt * 40:(kt + 1) * 40],
                start=(kt == 0), stop=(kt == KT - 1), skip_group_check=True)),
                waits=w_, inc=("T" if kt == KT - 1 else None),
                chk=("asumT" if kt == KT - 1 else None))
        op("vector", lambda e: e.tensor_scalar(negasum[:, :], pc[0:1, 0:40], -1.0, None,
                                               op0=ALU.mult),
           waits=[wc("asumT")], inc="V", chk="asumV")

        def emit_ballT():
            op("tensor", lambda e: e.transpose(pbf[:, 0:128], Ball[:, :], idn[:, :]),
               waits=[wc("ballin")], inc="T", chk="ballT")
            op("vector", lambda e: e.tensor_copy(BallT[:, :], pbf[:, 0:128]),
               waits=[wc("ballT")], inc="V", chk="ballTV")

        def emit_exp_chunk(c):
            wdma = [wc("ballTV")]
            if c >= 1:
                wdma.append(wc(f"ekmm{c-1}"))
            op("sync", (lambda e, c=c: e.dma_start(
                out=balltc[:, :], in_=BallT[16 * c:16 * (c + 1), :])),
                waits=wdma, inc="S_ccb", chk=f"btc{c}")
            waits = [wc(f"btc{c}")]
            if c >= 1:
                waits.append(wc(f"ekstrip{c-1}"))
            if c == 0:
                waits.append(wc("bthr"))
            op("tensor", (lambda e: e.matmul(
                pc[:, :], balltc[:, :], EXPc[:, :],
                start=True, stop=True)), waits=waits, inc="T", chk=f"ekmm{c}")
            wv = [wc(f"ekmm{c}")]
            if c >= 1:
                wv.append(wc(f"ektr{c-1}_3"))
            op("vector", (lambda e: e.tensor_copy(
                ekstrip[:, 0:512], pc[:, :])),
                waits=wv, inc="V", chk=f"ekstrip{c}")
            for t in range(4):
                wt = [wc(f"ekstrip{c}")]
                if c > 0 and t == 0:
                    wt.append(wc(f"ekcol{c-1}"))
                op("tensor", (lambda e, t=t: e.transpose(
                    pbf[:, t * 128:(t + 1) * 128],
                    ekstrip[:, t * 128:(t + 1) * 128], idn[:, :])),
                    waits=wt, inc="T", chk=f"ektr{c}_{t}")
            op("vector", (lambda e, c=c: e.tensor_copy(
                EKcol[:, c * 512:(c + 1) * 512], pbf[:, :])),
                waits=[wc(f"ektr{c}_3")], inc="V", chk=f"ekcol{c}")

        mask_trigger = {1: 0, 3: 1, 5: 2, 7: 3, 9: 4, 11: 5, 13: 6, 15: 7}
        for kt in range(KT):
            pr = kt // 2
            P = pr % 2
            if kt % 2 == 0:
                waits = []
                if kt >= 4:
                    waits += [wc(f"sqA{pr-2}"), wc(f"xcv{pr-2}")]
                op("sync", (lambda e, kt=kt, P=P: e.dma_start(
                    out=xf[:, P * 2048:(P + 1) * 2048].rearrange("p (t n) -> p t n", t=2),
                    in_=d_xT[kt * 128:(kt + 2) * 128, :].rearrange("(t p) n -> p t n",
                                                                  p=128))),
                    waits=waits, inc=f"S_xa{P}", chk=f"xa{pr}")
                waits = [wc(f"xa{pr}")]
                op("vector", (lambda e, P=P, kt=kt: e.tensor_copy(
                    xnT[:, kt * R:(kt + 2) * R], xf[:, P * 2048:(P + 1) * 2048])),
                    waits=waits, inc="V", chk=f"xcv{pr}")
                wsq = list(waits)
                if kt >= 4:
                    wsq.append(wc(f"stT{kt-3}"))
                op("scalar", (lambda e, P=P: e.activation(
                    xsqb[:, P * 2048:(P + 1) * 2048], xf[:, P * 2048:(P + 1) * 2048],
                    AF.Square)),
                    waits=wsq, inc="SC", chk=f"sqA{pr}")
            st_ = (kt == 0)
            sp = (kt == KT - 1)
            xqoff = P * 2048 + (kt % 2) * 1024
            wst_ = [wc(f"sqA{pr}"), wc(f"xcv{pr}")]
            if kt == 0:
                wst_.append(wc("csts"))
            op("tensor", (lambda e, kt=kt, st_=st_, sp=sp: e.matmul(
                pm[0][:, :], avgp[:, :], xnT[:, kt * R:kt * R + 512], start=st_, stop=sp)),
                waits=wst_)
            op("tensor", (lambda e, kt=kt, st_=st_, sp=sp: e.matmul(
                pm[1][:, :], avgp[:, :], xnT[:, kt * R + 512:kt * R + 1024],
                start=st_, stop=sp)))
            op("tensor", (lambda e, xqoff=xqoff, st_=st_, sp=sp: e.matmul(
                pm[2][:, :], avgp[:, :], xsqb[:, xqoff:xqoff + 512], start=st_, stop=sp)))
            op("tensor", (lambda e, xqoff=xqoff, st_=st_, sp=sp: e.matmul(
                pm[3][:, :], avgp[:, :], xsqb[:, xqoff + 512:xqoff + 1024],
                start=st_, stop=sp)), inc="T", chk=f"stT{kt}")
            op("tensor", (lambda e, kt=kt, st_=st_: e.matmul(
                pa[0:40, :], A_bf[:, kt * 40:(kt + 1) * 40], xnT[:, kt * R:kt * R + 512],
                start=st_, stop=False, skip_group_check=True)))
            op("tensor", (lambda e, kt=kt, st_=st_: e.matmul(
                pb[0:40, :], A_bf[:, kt * 40:(kt + 1) * 40],
                xnT[:, kt * R + 512:kt * R + 1024],
                start=st_, stop=False, skip_group_check=True)),
                inc=("T" if kt == KT - 1 else None), chk=("zrA" if kt == KT - 1 else None))
            if kt in mask_trigger:
                emit_mask_batch(mask_trigger[kt])
            if kt == 20:
                emit_ballT()
            if 21 <= kt <= 28:
                emit_exp_chunk(kt - 21)
            if kt == 18:
                op("tensor", lambda e: e.transpose(pc[:, 8:40], b_sb[:, :],
                                                   idnf[0:32, 0:32]),
                   waits=[wc("csts"), wc("asumV")], inc="T", chk="bcolT")
                op("tensor", lambda e: e.matmul(pc[:, 0:8], onesc[:, :], b8r[:, :],
                                                start=True, stop=True), inc="T", chk="b8T")
                op("vector", lambda e: e.tensor_copy(b_col[:, :], pc[:, 8:40]),
                   waits=[wc("bcolT")], inc="V", chk="bcolV")
                op("vector", lambda e: e.tensor_copy(b8B[:, :], pc[:, 0:8]),
                   waits=[wc("b8T")], inc="V", chk="b8V")
                for ktb in range(KT):
                    w2 = [wc("mrq7"), wc("b8V"), wc("bcolV")] if ktb == 0 else []
                    op("tensor", (lambda e, ktb=ktb: e.matmul(
                        pc[0:16, 4 * ktb:4 * ktb + 4], Sn[:, ktb * 16:(ktb + 1) * 16],
                        Q4[:, :], start=True, stop=True)), waits=w2,
                        inc=("T" if ktb == KT - 1 else None),
                        chk=(f"b16T{KT-1}" if ktb == KT - 1 else None))
                op("vector", lambda e: e.tensor_scalar(B16[:, :], pc[0:16, 0:128], 0.0,
                                                       None, op0=ALU.is_gt),
                   waits=[wc(f"b16T{KT-1}")], inc="V", chk="bthr")
                op("sync", lambda e: e.dma_start(out=cc_b_in.ap(), in_=B16[:, :]),
                   waits=[wc("bthr")], inc="S_ccb", chk="ccbin")
                if SKIP_CC:
                    def em_ag(e):
                        return e.dma_start(out=cc_b_out.ap()[0:16, :], in_=cc_b_in.ap())
                    op("gpsimd", em_ag, waits=[wc("ccbin")], inc="CC1", chk="ag")
                else:
                    op("gpsimd", lambda e: e.collective_compute(
                        "AllGather", ALU.bypass, replica_groups=[list(range(NCORES))],
                        ins=[cc_b_in.ap().opt()], outs=[cc_b_out.ap().opt()]),
                        waits=[wc("ccbin")], inc="CC1", chk="ag")
                op("gpsimd", lambda e: e.dma_start(out=Ball[:, :], in_=cc_b_out.ap()),
                   waits=[("CC1", 1)], inc="G_cc", chk="ballin")

        op("vector", lambda e: e.tensor_copy(stats[0:1, 0:512], pm[0][0:1, :]),
           waits=[wc(f"stT{KT-1}")], inc="V")
        op("vector", lambda e: e.tensor_copy(stats[0:1, 512:1024], pm[1][0:1, :]), inc="V")
        op("vector", lambda e: e.tensor_copy(stats[0:1, 1024:1536], pm[2][0:1, :]), inc="V")
        op("vector", lambda e: e.tensor_copy(stats[0:1, 1536:2048], pm[3][0:1, :]), inc="V",
           chk="stev")
        op("vector", lambda e: e.tensor_tensor(stats[0:1, 2048:3072], stats[0:1, 0:1024],
                                               stats[0:1, 0:1024], op=ALU.mult), inc="V")
        op("vector", lambda e: e.tensor_tensor(stats[0:1, 1024:2048], stats[0:1, 1024:2048],
                                               stats[0:1, 2048:3072], op=ALU.subtract),
           inc="V")
        op("vector", lambda e: e.tensor_scalar(stats[0:1, 1024:2048], stats[0:1, 1024:2048],
                                               EPS_LN, None, op0=ALU.add), inc="V",
           chk="varV")
        op("scalar", lambda e: e.activation(stats[0:1, 2048:3072], stats[0:1, 1024:2048],
                                            AF.Ln),
           waits=[wc("varV")], inc="SC")
        op("scalar", lambda e: e.activation(stats[0:1, 1024:2048], stats[0:1, 2048:3072],
                                            AF.Exp, scale=-0.5), inc="SC", chk="rstdS")
        op("vector", lambda e: e.tensor_copy(mrow_bf[:, :], stats[0:1, 0:1024]),
           waits=[wc("stev")], inc="V", chk="mrowV")
        op("tensor", lambda e: e.matmul(pa[0:40, :], negasum[0:1, 0:40],
                                        mrow_bf[0:1, 0:512],
                                        start=False, stop=True, skip_group_check=True),
           waits=[wc("mrowV"), wc("asumV")])
        op("tensor", lambda e: e.matmul(pb[0:40, :], negasum[0:1, 0:40],
                                        mrow_bf[0:1, 512:1024],
                                        start=False, stop=True, skip_group_check=True),
           inc="T", chk="zr1")
        op("vector", lambda e: e.tensor_copy(zfx[:, 0:512], pa[0:40, :]),
           waits=[wc("zr1")], inc="V")
        op("vector", lambda e: e.tensor_copy(zfx[:, 512:1024], pb[0:40, :]), inc="V",
           chk="zrevV")
        op("tensor", lambda e: e.matmul(pa[:, :], onesc[:, :], stats[0:1, 0:512],
                                        start=True, stop=True),
           waits=[wc("stev"), wc("zrevV")], inc="T", chk="bc0")
        op("tensor", lambda e: e.matmul(pb[:, :], onesc[:, :], stats[0:1, 512:1024],
                                        start=True, stop=True), inc="T", chk="bc1")
        op("vector", lambda e: e.tensor_copy(mb_bf[:, 0:512], pa[:, :]),
           waits=[wc("bc0")], inc="V", chk="bc0V")
        op("vector", lambda e: e.tensor_copy(mb_bf[:, 512:1024], pb[:, :]),
           waits=[wc("bc1")], inc="V", chk="bc1V")
        op("tensor", lambda e: e.matmul(pa[:, :], onesc[:, :], stats[0:1, 1024:1536],
                                        start=True, stop=True),
           waits=[wc("rstdS"), wc("bc0V")], inc="T", chk="bc2")
        op("tensor", lambda e: e.matmul(pb[:, :], onesc[:, :], stats[0:1, 1536:2048],
                                        start=True, stop=True),
           waits=[wc("bc1V")], inc="T", chk="bc3")
        op("vector", lambda e: e.tensor_copy(rb_bf[:, 0:512], pa[:, :]),
           waits=[wc("bc2")], inc="V", chk="bc2V")
        op("vector", lambda e: e.tensor_copy(rb_bf[:, 512:1024], pb[:, :]),
           waits=[wc("bc3")], inc="V", chk="bc3V")
        op("vector", lambda e: e.tensor_tensor(zfx[:, :], zfx[:, :], rb_bf[0:40, :],
                                               op=ALU.mult),
           waits=[wc("bc3V"), wc("zrevV")], inc="V", chk="zfixV")
        op("vector", lambda e: e.tensor_copy(zTb[:, :], zfx[0:32, :]),
           waits=[wc("zfixV")], inc="V", chk="ztbV")
        op("sync", lambda e: e.dma_start(out=H8Tn[:, :], in_=zfx[32:40, :]),
           waits=[wc("ztbV")], inc="S_xb0", chk="h8mv")

        op("scalar", lambda e: e.activation(LiMs_bf[:, :], LiM[:, :], AF.Identity,
                                            scale=SCALING),
           waits=[wc("csts")], inc="SC", chk="limsS")
        op("scalar", lambda e: e.activation(LiMT_bf[:, :], LiMTf[:, :], AF.Identity),
           inc="SC", chk="limV")

        op("tensor", lambda e: e.matmul(pa[0:8, :], LiMT_bf[:, :], zTb[:, 0:512],
                                        start=True, stop=True),
           waits=[wc("ztbV"), wc("limV")], inc="T", chk="ds0")
        op("tensor", lambda e: e.matmul(pb[0:8, :], LiMT_bf[:, :], zTb[:, 512:1024],
                                        start=True, stop=True), inc="T", chk="ds1")
        for mt in range(8):
            waits = ["dsV", ("S_xb0", 16)] if mt == 0 else []
            op("tensor", (lambda e, mt=mt: e.transpose(
                pa[:, mt * 8:(mt + 1) * 8], H8Tn[0:8, mt * 128:(mt + 1) * 128],
                idnf[0:8, 0:8])), waits=waits, inc="T")
            op("tensor", (lambda e, mt=mt: e.transpose(
                pb[:, mt * 8:(mt + 1) * 8], dsT[0:8, mt * 128:(mt + 1) * 128],
                idnf[0:8, 0:8])), inc="T", chk=("tokT" if mt == 7 else None))
        op("tensor", lambda e: e.transpose(pc[0:2, 0:128], gat[:, 0:2], idnf[:, :]),
           waits=["rmax", wc("ekstrip7")], inc="T", chk="mxT")

        def emit_maxes_v():
            op("vector", lambda e: e.tensor_copy(dsT[:, 0:512], pa[0:8, :]),
               waits=[wc("ds0")], inc="V")
            op("vector", lambda e: e.tensor_copy(dsT[:, 512:1024], pb[0:8, :]),
               waits=[wc("ds1")], inc="V", chk="dsV")
            op("vector", lambda e: e.tensor_copy(H8tok[:, :], pa[:, 0:64]),
               waits=[wc("tokT")], inc="V")
            op("vector", lambda e: e.tensor_copy(dstok[:, :], pb[:, 0:64]), inc="V",
               chk="tokV")
            op("vector", lambda e: e.tensor_tensor(
                H8tok[:, :].rearrange("p (mt e) -> p mt e", e=8),
                H8tok[:, :].rearrange("p (mt e) -> p mt e", e=8),
                b8B[:, :].unsqueeze(1).broadcast_to([128, 8, 8]), op=ALU.add),
                waits=[wc("b8V")], inc="V", chk="h8bias")
            op("vector", lambda e: e.reduce_max(gat[:, 0:1], H8tok[:, :], axis=AX.X,
                                                apply_absolute_value=True),
               waits=[wc("h8bias")], inc="V")
            op("vector", lambda e: e.reduce_max(gat[:, 1:2], dstok[:, :], axis=AX.X,
                                                apply_absolute_value=True),
               waits=[wc("tokV")], inc="V")
            op("vector", lambda e: e.tensor_tensor(gat[:, 0:2], gat[:, 0:2],
                                                   oddm[:, :].broadcast_to([128, 2]),
                                                   op=ALU.mult),
               waits=[wc("csts")], inc="V", chk="rmax")
            op("vector", lambda e: e.reduce_max(gat[0:2, 8:9], pc[0:2, 0:128], axis=AX.X),
               waits=["mxT"], inc="V", chk="mxV")
            op("sync", lambda e: e.dma_start(out=cc_m_in.ap(), in_=gat[0:2, 8:9]),
               waits=[wc("mxV")], inc="S_ccm", chk="ccmin")

        for kt in range(KT):
            if kt == 4:
                emit_maxes_v()
            wv = [wc("bc3V"), wc("bc2V")]
            if kt == 0:
                wv.append(wc("mrq7"))
            op("vector", (lambda e, kt=kt: e.tensor_tensor(
                mstage[:, 0:R], xnT[:, kt * R:(kt + 1) * R], mb_bf[:, :],
                op=ALU.subtract)),
                waits=wv, inc="V")
            op("vector", (lambda e, kt=kt: e.tensor_tensor(
                xnT[:, kt * R:(kt + 1) * R], mstage[:, 0:R], rb_bf[:, :], op=ALU.mult)),
                inc="V", chk=f"xn{kt}")
            Xh = kt // 16
            op("vector", (lambda e, kt=kt, Xh=Xh: e.tensor_tensor(
                wmbuf[:, kt * NGW:(kt + 1) * NGW].rearrange("p (nb w) -> p nb w", w=32),
                wslot[Xh % 2][:, (kt % 16) * NGW:((kt % 16) + 1) * NGW]
                    .rearrange("p (nb w) -> p nb w", w=32),
                EKcol[:, kt * 128:kt * 128 + 8].unsqueeze(-1).broadcast_to([128, 8, 32]),
                op=ALU.mult)),
                waits=([wc(f"stT{KT-1}")] if kt == 0 else []) +
                      [(f"G_wst{Xh % 2}", 16 * (Xh // 2 + 1)), wc(f"ekcol{kt//4}")],
                inc="V", chk=f"vwm0k{kt}")
            for j in (0, 1):
                pmA = pm[2 * j]
                pmB = pm[2 * j + 1]
                wj = [(f"G_wbf{kt // 8}", 16)]
                if j == 0:
                    wj.append(wc(f"xn{kt}"))
                if kt == 0:
                    wj.append(wc("stev"))
                wof = kt * NGW + j * 128
                mkchk = (j == 1 and kt % 8 == 7)
                op("tensor", (lambda e, pmA=pmA, wof=wof, kt=kt, st_=(kt == 0): e.matmul(
                    pmA[:, :], wmbuf[:, wof:wof + 128], xnT[:, kt * R:kt * R + 512],
                    start=st_, stop=False, skip_group_check=True)), waits=wj)
                op("tensor", (lambda e, pmB=pmB, wof=wof, kt=kt, st_=(kt == 0): e.matmul(
                    pmB[:, :], wmbuf[:, wof:wof + 128], xnT[:, kt * R + 512:kt * R + 1024],
                    start=st_, stop=False, skip_group_check=True)),
                    inc=("T" if mkchk else None),
                    chk=(f"tWMj1k{kt}" if mkchk else None))

        op("gpsimd", lambda e: e.dma_start(out=xw2[:, :], in_=wm3[:, 0, 16:32]),
           waits=[wc(f"stT{KT-1}")], inc="G_wst1", chk="wst0h1")
        for q in range(4):
            op("gpsimd", (lambda e, q=q: e.dma_start(
                out=wmbuf[:, q * 8 * NGW:(q + 1) * 8 * NGW],
                in_=W3[:, 0, 8 * q:8 * (q + 1)], accum_op=(ACC_OP or ALU.add))),
               waits=[wc(f"vwm0k{8*q+7}")], inc=f"G_wbf{q}", chk=f"wa0q{q}")
            if q == 0:
                if SKIP_CC:
                    op("gpsimd", lambda e: e.dma_start(out=cc_m_out.ap(),
                                                       in_=cc_m_in.ap()),
                       waits=["ccmin"], inc="CC2", chk="ar")
                else:
                    op("gpsimd", lambda e: e.collective_compute(
                        "AllReduce", ALU.max, replica_groups=[list(range(NCORES))],
                        ins=[cc_m_in.ap().opt()], outs=[cc_m_out.ap().opt()]),
                        waits=["ccmin"], inc="CC2", chk="ar")
            if q == 1:
                op("gpsimd", lambda e: e.dma_start(out=wstage[:, :], in_=wm3[:, 1, 0:16]),
                   waits=[wc("vwm0k15")], inc="G_wst0", chk="wst1h0")
            if q == 3:
                op("gpsimd", lambda e: e.dma_start(out=xw2[:, :], in_=wm3[:, 1, 16:32]),
                   waits=[wc(f"vwm0k{KT-1}")], inc="G_wst1", chk="wst1h1")

        op("sync", lambda e: e.dma_start(out=gmax[0:1, 0:2], in_=cc_m_out.ap()),
           waits=[("CC2", 1)], inc="G_cc2", chk="gmaxin")

        op("vector", lambda e: e.tensor_scalar(gmax[0:1, 2:4], gmax[0:1, 0:2], EPS_G, None,
                                               op0=ALU.max),
           waits=[wc("gmaxin")], inc="V")
        op("vector", lambda e: e.reciprocal(gmax[0:1, 4:6], gmax[0:1, 2:4]), inc="V")
        op("vector", lambda e: e.tensor_scalar(gmax[0:1, 6:8], gmax[0:1, 4:6], GAMMA, None,
                                               op0=ALU.mult), inc="V", chk="scV")
        op("tensor", lambda e: e.matmul(pc[:, 0:2], onesc[:, :], gmax[0:1, 6:8],
                                        start=True, stop=True),
           waits=[wc("scV")], inc="T", chk="scT")
        op("vector", lambda e: e.tensor_copy(scB[:, :], pc[:, 0:2]),
           waits=["scT"], inc="V", chk="scBV")

        GA = gat

        def V3(ap):
            return ap.rearrange("p (mt e) -> p mt e", e=8)

        def bc8(ap):
            return ap.unsqueeze(-1).broadcast_to([128, 8, 8])

        op("vector", lambda e: e.tensor_scalar(GA[:, 64:128], H8tok[:, :], scB[:, 0:1], None,
                                               op0=ALU.mult),
           waits=[wc("scBV")], inc="V")
        op("vector", lambda e: e.scalar_tensor_tensor(GA[:, 128:192], dstok[:, :],
                                                      scB[:, 1:2], GA[:, 64:128],
                                                      op0=ALU.mult, op1=ALU.add),
           inc="V", chk="logit")
        op("scalar", lambda e: e.activation(GA[:, 192:256], GA[:, 128:192], AF.Exp),
           waits=[wc("logit")], inc="SC", chk="expS")
        op("vector", lambda e: e.tensor_reduce(GA[:, 256:264], V3(GA[:, 192:256]),
                                               axis=AX.X, op=ALU.add),
           waits=[wc("expS")], inc="V")
        op("vector", lambda e: e.reciprocal(GA[:, 336:344], GA[:, 256:264]), inc="V")
        op("vector", lambda e: e.tensor_tensor(V3(GA[:, 320:384]), V3(GA[:, 192:256]),
                                               bc8(GA[:, 336:344]), op=ALU.mult),
           inc="V", chk="probs")
        op("vector", lambda e: e.tensor_reduce(GA[:, 272:280], V3(GA[:, 320:384]),
                                               axis=AX.X, op=ALU.max), inc="V")
        op("vector", lambda e: e.tensor_tensor(V3(GA[:, 384:448]), V3(GA[:, 320:384]),
                                               bc8(GA[:, 272:280]), op=ALU.is_ge), inc="V")
        op("vector", lambda e: e.tensor_reduce(GA[:, 264:272], V3(GA[:, 384:448]),
                                               axis=AX.X, op=ALU.add), inc="V")
        op("vector", lambda e: e.scalar_tensor_tensor(GA[:, 448:512], GA[:, 384:448], -1e30,
                                                      GA[:, 320:384], op0=ALU.mult,
                                                      op1=ALU.add), inc="V")
        op("vector", lambda e: e.tensor_reduce(GA[:, 280:288], V3(GA[:, 448:512]),
                                               axis=AX.X, op=ALU.max), inc="V")
        op("vector", lambda e: e.tensor_scalar(GA[:, 288:296], GA[:, 264:272], 2.0, None,
                                               op0=ALU.is_ge), inc="V")
        op("vector", lambda e: e.tensor_tensor(GA[:, 296:304], GA[:, 272:280],
                                               GA[:, 280:288], op=ALU.subtract), inc="V")
        op("vector", lambda e: e.tensor_tensor(GA[:, 296:304], GA[:, 296:304],
                                               GA[:, 288:296], op=ALU.mult), inc="V")
        op("vector", lambda e: e.tensor_tensor(GA[:, 304:312], GA[:, 296:304],
                                               GA[:, 280:288], op=ALU.add), inc="V",
           chk="thr")
        op("vector", lambda e: e.tensor_tensor(V3(GA[:, 512:576]), bc8(GA[:, 304:312]),
                                               V3(GA[:, 320:384]), op=ALU.subtract),
           inc="V", chk="preV")
        op("scalar", lambda e: e.activation(GA[:, 576:640], GA[:, 512:576], AF.Exp,
                                            scale=2.0),
           waits=[wc("preV"), wc("ekcol7")], inc="SC", chk="qS")
        op("vector", lambda e: e.tensor_scalar(GA[:, 640:704], GA[:, 576:640], 1.0, None,
                                               op0=ALU.add),
           waits=[wc("qS")], inc="V")
        op("vector", lambda e: e.reciprocal(GA[:, 704:768], GA[:, 640:704]), inc="V")
        op("vector", lambda e: e.tensor_tensor(GA[:, 768:832], GA[:, 320:384],
                                               GA[:, 704:768], op=ALU.mult), inc="V")
        op("vector", lambda e: e.tensor_reduce(GA[:, 312:320], V3(GA[:, 768:832]),
                                               axis=AX.X, op=ALU.add), inc="V")
        op("vector", lambda e: e.tensor_scalar(GA[:, 312:320], GA[:, 312:320], 1e-9, None,
                                               op0=ALU.add), inc="V")
        op("vector", lambda e: e.reciprocal(GA[:, 328:336], GA[:, 312:320]), inc="V")
        op("vector", lambda e: e.tensor_tensor(V3(wnb[:, :]), V3(GA[:, 768:832]),
                                               bc8(GA[:, 328:336]), op=ALU.mult),
           inc="V", chk="wnV")

        ev_cnt = {}
        out_cnt = {}

        def plan_main_block(j):
            g = j // 2
            par = j % 2
            pmA = pm[2 * par]
            pmB = pm[2 * par + 1]
            gbase = (g % 2) * (KT * NGW)
            for kt in range(KT):
                waits = [(f"G_wbf{kt // 8}", 16 * (g + 1))]
                if kt == 0:
                    waits.append(("V", ev_cnt[j - 2]))
                wof = gbase + kt * NGW + par * 128
                mkchk = (par == 1 and kt % 8 == 7)
                op("tensor", (lambda e, pmA=pmA, wof=wof, kt=kt, st_=(kt == 0): e.matmul(
                    pmA[:, :], wmbuf[:, wof:wof + 128], xnT[:, kt * R:kt * R + 512],
                    start=st_, stop=False, skip_group_check=True)), waits=waits)
                op("tensor", (lambda e, pmB=pmB, wof=wof, kt=kt, st_=(kt == 0): e.matmul(
                    pmB[:, :], wmbuf[:, wof:wof + 128], xnT[:, kt * R + 512:kt * R + 1024],
                    start=st_, stop=False, skip_group_check=True)),
                    inc=("T" if mkchk else None),
                    chk=(f"tWMj{j}k{kt}" if mkchk else None))

        def plan_delta_and_evac(j):
            par = j % 2
            pmA = pm[2 * par]
            pmB = pm[2 * par + 1]
            op("tensor", (lambda e, pmA=pmA, j=j: e.matmul(
                pmA[:, :], Bm_bf[:, j * 128:(j + 1) * 128], mulT[:, 0:512],
                start=False, stop=True, skip_group_check=True)),
                waits=[wc("vmul")], inc="T")
            op("tensor", (lambda e, pmB=pmB, j=j: e.matmul(
                pmB[:, :], Bm_bf[:, j * 128:(j + 1) * 128], mulT[:, 512:1024],
                start=False, stop=True, skip_group_check=True)), inc="T", chk=f"tDj{j}")
            rsl = j % 2
            wres = [wc(f"vwm0k{KT-1}")] if j < 2 else [("V", ev_cnt[j - 2])]
            if j < 2:
                wres.append(wc(f"xn{KT-1}"))
            op("sync", (lambda e, j=j, rsl=rsl: e.dma_start(
                out=xf[:, rsl * R:(rsl + 1) * R], in_=d_xT[j * 128:(j + 1) * 128, :])),
                waits=wres, inc=f"S_res{rsl}", chk=f"res{j}")
            for h in range(2):
                bank = pmA if h == 0 else pmB
                osl = (2 * j + h) % 2
                waits = [wc(f"tDj{j}"), wc(f"res{j}")]
                if 2 * j + h >= 2:
                    prev = 2 * j + h - 2
                    waits.append(out_cnt[(prev // 2, prev % 2)])
                op("vector", (lambda e, bank=bank, osl=osl, rsl=rsl, j=j, h=h:
                    e.scalar_tensor_tensor(
                        outstg[:, osl * 512:(osl + 1) * 512], bank[:, :], b_col[:, j:j + 1],
                        xf[:, rsl * R + h * 512:rsl * R + (h + 1) * 512],
                        op0=ALU.add, op1=ALU.add)),
                    waits=waits, inc="V", chk=f"ev{j}_{h}")
                op("scalar", (lambda e, j=j, h=h, osl=osl: e.dma_start(
                    out=d_out[j * 128:(j + 1) * 128, h * 512:(h + 1) * 512],
                    in_=outstg[:, osl * 512:(osl + 1) * 512])),
                    waits=[wc(f"ev{j}_{h}")], inc=f"SCQ{osl}")
                out_cnt[(j, h)] = (f"SCQ{osl}", CNT[f"SCQ{osl}"])
            ev_cnt[j] = CHK[f"ev{j}_1"][1]

        def plan_vwm(g):
            base = (g % 2) * (KT * NGW)
            for kt in range(KT):
                X = 2 * g + kt // 16
                waits = [(f"G_wst{X % 2}", 16 * (X // 2 + 1))]
                if g >= 2:
                    waits.append(wc(f"tWMj{2*g-3}k{8*(kt//8)+7}"))
                op("vector", (lambda e, g=g, kt=kt, base=base, X=X: e.tensor_tensor(
                    wmbuf[:, base + kt * NGW:base + (kt + 1) * NGW]
                        .rearrange("p (nb w) -> p nb w", w=32),
                    wslot[X % 2][:, (kt % 16) * NGW:((kt % 16) + 1) * NGW]
                        .rearrange("p (nb w) -> p nb w", w=32),
                    EKcol[:, kt * 128 + 8 * g:kt * 128 + 8 * g + 8]
                        .unsqueeze(-1).broadcast_to([128, 8, 32]),
                    op=ALU.mult)),
                    waits=waits, inc="V", chk=f"vwm{g}k{kt}")

        for mt in range(8):
            waits = [wc("wnV")] if mt == 0 else []
            op("tensor", (lambda e, mt=mt: e.matmul(
                pa[:, mt * 8:(mt + 1) * 8], pairP[:, :], wnb[:, mt * 8:(mt + 1) * 8],
                start=True, stop=True)), waits=waits,
                inc=("T" if mt == 7 else None), chk=("wf8" if mt == 7 else None))
        op("vector", lambda e: e.tensor_copy(wfb[:, :], pa[:, 0:64]),
           waits=[wc("wf8")], inc="V", chk="wfbV")
        for rnd in range(2):
            for t in range(4):
                mt = rnd * 4 + t
                waits = [wc("wfbV")]
                if rnd == 1 and t == 0:
                    waits.append(wc("wfT0V"))
                op("tensor", (lambda e, mt=mt, t=t: e.transpose(
                    pbf[0:8, t * 128:(t + 1) * 128], wfb[:, mt * 8:(mt + 1) * 8],
                    idn[:, :])),
                    waits=waits, inc=("T" if t == 3 else None),
                    chk=(f"wfTr{rnd}" if t == 3 else None))
            op("vector", (lambda e, rnd=rnd: e.tensor_copy(
                wfT[0:8, rnd * 512:(rnd + 1) * 512], pbf[0:8, 0:512])),
                waits=[wc(f"wfTr{rnd}")], inc="V", chk=f"wfT{rnd}V")
        op("tensor", lambda e: e.matmul(pa[0:32, :], LiMs_bf[:, :], wfT[0:8, 0:512],
                                        start=True, stop=True),
           waits=[wc("wfT1V"), wc("limsS")], inc="T", chk="pmx0")
        op("tensor", lambda e: e.matmul(pb[0:32, :], LiMs_bf[:, :], wfT[0:8, 512:1024],
                                        start=True, stop=True), inc="T", chk="pmx1")
        op("vector", lambda e: e.tensor_tensor(mulT[:, 0:512], pa[0:32, :], zTb[:, 0:512],
                                               op=ALU.mult),
           waits=[wc("pmx0")], inc="V")
        op("vector", lambda e: e.tensor_tensor(mulT[:, 512:1024], pb[0:32, :],
                                               zTb[:, 512:1024], op=ALU.mult),
           waits=[wc("pmx1")], inc="V", chk="vmul")

        plan_delta_and_evac(0)
        plan_delta_and_evac(1)
        plan_vwm(1)
        plan_vwm(2)
        nb_eff = NB if NB_EFF is None else NB_EFF
        for j in range(2, nb_eff):
            plan_main_block(j)
            plan_delta_and_evac(j)
            if j % 2 == 1 and (j + 3) // 2 < NG:
                plan_vwm((j + 3) // 2)

        for g in range(1, (NB if NB_EFF is None else NB_EFF) // 2):
            gb = (g % 2) * (KT * NGW)
            for q in range(4):
                op("gpsimd", (lambda e, g=g, gb=gb, q=q: e.dma_start(
                    out=wmbuf[:, gb + q * 8 * NGW:gb + (q + 1) * 8 * NGW],
                    in_=W3[:, g, 8 * q:8 * (q + 1)],
                    accum_op=(ACC_OP_H or ACC_OP or ALU.add))),
                   waits=[wc(f"vwm{g}k{8*q+7}"), wc(f"tWMj{2*g-1}k{8*q+7}")],
                   inc=f"G_wbf{q}", chk=f"wa{g}q{q}")
            if g + 1 < NG:
                op("gpsimd", (lambda e, g=g: e.dma_start(
                    out=wslot[0][:, :], in_=wm3[:, g + 1, 0:16])),
                   waits=[wc(f"vwm{g}k15")], inc="G_wst0", chk=f"wst{g+1}h0")
                op("gpsimd", (lambda e, g=g: e.dma_start(
                    out=wslot[1][:, :], in_=wm3[:, g + 1, 16:32])),
                   waits=[wc(f"vwm{g}k{KT-1}")], inc="G_wst1", chk=f"wst{g+1}h1")

        if debug_taps:
            tap_list = [
                ("t_xnT", xnT[:, :], [wc(f"xn{KT-1}")]),
                ("t_EKcol", EKcol[:, :], [wc("ekcol7")]),
                ("t_Ball", Ball[:, :], [wc("ballin")]),
                ("t_H8tok", H8tok[:, :], [wc("h8bias")]),
                ("t_dstok", dstok[:, :], [wc("tokV")]),
                ("t_wnb", wnb[:, :], [wc("wnV")]),
                ("t_mulT", mulT[:, :], [wc("vmul")]),
                ("t_gmax", gmax[0:1, 0:8], [wc("scV")]),
                ("t_Sn", Sn[:, :], [wc("mrq7")]),
                ("t_zT", zTb[:, :], [wc("ztbV")]),

            ]
            for nm, src, waits in tap_list:
                op("sync", (lambda e, nm=nm, src=src: e.dma_start(out=taps[nm], in_=src)),
                   waits=waits, inc="S_cst")

        PLAN["scalar"].append((None, [("SCQ0", CNT["SCQ0"]), ("SCQ1", CNT["SCQ1"])], None))
        PLAN["sync"].append((None, [("S_cst", CNT["S_cst"])], None))

        def make_runner(eng_name):
            own_sem = {"vector": "V", "scalar": "SC"}.get(eng_name)
            def run(engine):
                seen = {}
                own = 0
                for emit, waits, inc in PLAN[eng_name]:
                    if own_sem is not None and emit is not None and own > 0:
                        if seen.get(own_sem, 0) < own:
                            engine.wait_ge(SEM[own_sem], own)
                            seen[own_sem] = own
                    for wspec in waits:
                        if isinstance(wspec, str):
                            if wspec not in CHK and STOP_N is not None:
                                continue
                            semn, val = CHK[wspec]
                        else:
                            semn, val = wspec
                        if seen.get(semn, 0) < val:
                            engine.wait_ge(SEM[semn], val)
                            seen[semn] = val
                    if emit is None:
                        continue
                    inst = emit(engine)
                    if inc == own_sem:
                        own += 1
                    if inc is not None:
                        if inc in CC_SEMS:
                            if SKIP_CC:
                                inst.then_inc(SEM[inc], 16)
                            else:
                                inst.then_inc(SEM[inc])
                        elif inc in DMA_SEMS:
                            inst.then_inc(SEM[inc], 16)
                        else:
                            inst.then_inc(SEM[inc], 1)
            return run

        block.sync(make_runner("sync"))
        block.gpsimd(make_runner("gpsimd"))
        block.scalar(make_runner("scalar"))
        block.tensor(make_runner("tensor"))
        block.vector(make_runner("vector"))

    return nc


def _consts():
    idn = np.eye(128, dtype=ml_dtypes.bfloat16)
    idnf = np.eye(128, dtype=np.float32)
    avgp = np.zeros((128, 128), dtype=ml_dtypes.bfloat16)
    avgp[:, 0] = ml_dtypes.bfloat16(1.0 / 4096.0)
    EXP = np.zeros((16, 512), dtype=ml_dtypes.bfloat16)
    for kb in range(16):
        EXP[kb, kb * 32:(kb + 1) * 32] = 1
    Q4 = np.zeros((128, 4), dtype=ml_dtypes.bfloat16)
    for p in range(128):
        Q4[p, p // 32] = 1
    oddm = np.zeros((128, 1), np.float32)
    oddm[1::2, 0] = 1.0
    onesc = np.ones((1, 128), np.float32)
    pairP = np.zeros((128, 128), dtype=ml_dtypes.bfloat16)
    for m in range(128):
        pairP[m | 1, m] = 1
    onebf = np.ones((128, 1), dtype=ml_dtypes.bfloat16)
    return idn, idnf, avgp, EXP, Q4, oddm, onesc, pairP, onebf


def _tile_khost(M):
    K, C = M.shape
    return np.ascontiguousarray(M.reshape(32, 128, C).transpose(1, 0, 2).reshape(128, 32 * C))


def _tile_weight(Wmat):
    arr = Wmat.reshape(NG, NGW, KT, 128)
    return np.ascontiguousarray(arr.transpose(3, 0, 2, 1).reshape(128, NG * KT * NGW))


def kernel(L0_attn_data, L0_attn_weight, L0_attn_mask, L0_ffn_W, L0_ffn_b,
           L0_ffn_A, L0_ffn_Bmat, L0_ffn_LiMEs):
    X = np.ascontiguousarray(np.asarray(L0_attn_data, np.float32).reshape(2 * 4096, D))
    weight = np.asarray(L0_attn_weight, np.float32)
    mask = np.asarray(L0_attn_mask, np.int32)
    W = np.asarray(L0_ffn_W, np.float32)
    bvec = np.asarray(L0_ffn_b, np.float32)
    A = np.asarray(L0_ffn_A, np.float32)
    Bmat = np.ascontiguousarray(np.asarray(L0_ffn_Bmat, np.float32))
    LiMa = np.ascontiguousarray(np.asarray(L0_ffn_LiMEs, np.float32))

    idn, idnf, avgp, EXP, Q4, oddm, onesc, pairP, onebf = _consts()

    wmh = _tile_weight(weight)
    Wh = _tile_weight(W)
    Ah = _tile_khost(np.ascontiguousarray(np.hstack([A, W[0:8, :].T])))
    LiMT = np.ascontiguousarray(LiMa.T)
    b32 = np.ascontiguousarray(bvec.reshape(32, 128))
    b8 = np.ascontiguousarray(bvec[0:8].reshape(1, 8))
    maskT = np.ascontiguousarray(mask.T)

    nc = build_nc(debug_taps=DEBUG_TAPS)

    in_maps = []
    for c in range(NCORES):
        xT_c = np.ascontiguousarray(X[c * R:(c + 1) * R, :].T)
        msk_c = _tile_khost(np.ascontiguousarray(maskT[:, c * 512:(c + 1) * 512]))
        in_maps.append({
            "xT": xT_c, "wmh": wmh, "Wh": Wh, "mskh": msk_c,
            "Ah": Ah, "LiMT": LiMT, "onebf": onebf,
            "b32": b32, "b8": b8, "Bm": Bmat, "LiM": LiMa,
            "idn": idn, "idnf": idnf, "avgp": avgp, "EXPc": EXP, "Q4c": Q4,
            "oddm": oddm, "onesc": onesc, "pairP": pairP,
        })

    res = run_bass_kernel_spmd(nc, in_maps, core_ids=list(range(NCORES)))
    out = np.empty((2 * 4096, D), np.float32)
    for c in range(NCORES):
        out[c * R:(c + 1) * R, :] = np.asarray(res.results[c]["outT"]).T
    if DEBUG_TAPS:
        kernel.last_results = res
    return out.reshape(2, 4096, D)
